# version 05
# baseline (speedup 1.0000x reference)
from contextlib import ExitStack

import numpy as np
import ml_dtypes
import concourse.bass as bass
import concourse.mybir as mybir
from concourse.bass_utils import run_bass_kernel_spmd

F32 = mybir.dt.float32
BF16 = mybir.dt.bfloat16
I32 = mybir.dt.int32
AF = mybir.ActivationFunctionType
ALU = mybir.AluOpType

D = 1024
HID = 3072
NPAIR = 8
EH = 64
P = 128
KC = D // P
HC = HID // P
LN_EPS = 1e-5
FEAT_EPS = 1e-4
DEN_EPS = 8.0 * FEAT_EPS
WSCALE = 1.0 / 127.0

N_CORES = 8
SEQ_LEN = 8192
T_MAIN = 16
TOK = P

BF = ml_dtypes.bfloat16




class _MockInstr:
    def then_inc(self, *a, **k):
        return self

    def __getattr__(self, name):
        return lambda *a, **k: self


class _MockEng:
    def __getattr__(self, name):
        return lambda *a, **k: _MockInstr()


_EDGE_OPS = frozenset((
    "activation", "add", "mul", "copy", "square", "sqrt", "sign", "memzero",
    "memset", "tensor_copy", "tensor_tensor", "tensor_add", "tensor_sub",
    "tensor_mul", "tensor_max", "tensor_scalar", "tensor_single_scalar",
    "tensor_scalar_add", "tensor_scalar_sub", "tensor_scalar_mul",
    "tensor_scalar_max", "tensor_scalar_min", "tensor_relu", "tensor_reduce",
    "reduce_max", "reduce_sum", "scalar_tensor_tensor", "tensor_tensor_scan",
    "bn_stats", "bn_aggr", "reciprocal", "select", "copy_predicated",
    "iota", "pool", "pool_avg", "pool_max", "stream_shuffle", "engine_nop",
))


class _Edged:

    def __init__(self, env, eng, sem_name):
        self._env = env
        self._eng = eng
        self._sem = sem_name

    def __getattr__(self, name):
        env = self._env
        if name in _EDGE_OPS:
            def f(*a, **k):
                c = env.counts.get(self._sem, 0)
                if env.emit:
                    if c > 0:
                        self._eng.wait_ge(env.sems[self._sem], c)
                    instr = getattr(self._eng, name)(*a, **k)
                    instr.then_inc(env.sems[self._sem], 1)
                else:
                    instr = _MockInstr()
                env.counts[self._sem] = c + 1
                return instr
            return f
        if not env.emit:
            return lambda *a, **k: _MockInstr()
        return getattr(self._eng, name)


class Env:

    AUTO = frozenset(("v", "s", "p"))

    def __init__(self, plan, emit, eng, sems, auto_sem=None):
        self.plan = plan
        self.counts = {}
        self.emit = emit
        self.sems = sems
        self.raw_eng = eng
        if auto_sem is not None:
            self.eng = _Edged(self, eng, auto_sem)
        else:
            self.eng = eng if emit else _MockEng()

    def _bump(self, sem, key, n):
        c = self.counts.get(sem, 0) + n
        self.counts[sem] = c
        if key is not None:
            self._mark(sem, key, c)
        return c

    def _mark(self, sem, key, c):
        if not self.emit:
            assert (sem, key) not in self.plan, ("dup mark", sem, key)
            self.plan[(sem, key)] = c
        else:
            assert self.plan.get((sem, key)) == c, (
                "plan/emit drift", sem, key, self.plan.get((sem, key)), c)

    def inc(self, instr, sem, key=None, n=1):
        if sem in self.AUTO:
            if key is not None:
                self._mark(sem, key, self.counts.get(sem, 0))
            return instr
        self._bump(sem, key, n)
        if self.emit:
            instr.then_inc(self.sems[sem], n)
        return instr

    def wait(self, sem, key):
        if self.emit:
            eng = self.raw_eng if self.raw_eng is not None else None
            eng.wait_ge(self.sems[sem], self.plan[(sem, key)])

    def selfwait(self, sem):
        if self.emit:
            self.raw_eng.wait_ge(self.sems[sem], self.counts[sem])

    def wait_val(self, sem, val):
        if self.emit:
            self.raw_eng.wait_ge(self.sems[sem], val)


def run_bodies(bodies, sems):
    plan = {}
    for item in bodies:
        attach, body, auto = item
        body(Env(plan, False, None, sems, auto))
    for item in bodies:
        attach, body, auto = item
        attach(lambda eng, body=body, auto=auto: body(
            Env(plan, True, eng, sems, auto)))
    return plan




def emit_ln_stats(E, v, stats, mv, src_lo, src_hi, key):
    v.bn_stats(out=stats[:, 0:6], in_=src_lo)
    E.inc(v.bn_stats(out=stats[:, 6:12], in_=src_hi), "v")
    E.selfwait("v")
    E.inc(v.bn_aggr(out=mv[:], in_=stats[:, 0:12]), "v", key)
    E.selfwait("v")


def emit_rstd(E, s, mv, rstd, eps_ap):
    s.activation(out=rstd[:], in_=mv[:, 1:2], func=AF.Ln, bias=eps_ap)
    return s.activation(out=rstd[:], in_=rstd[:], func=AF.Exp, scale=-0.5)


def sadd_views(dst, src, r, blk_src):
    raise NotImplementedError




def build_state(T):
    nc = bass.Bass("TRN2", target_bir_lowering=False)
    NT = T * TOK
    x_in = nc.dram_tensor("x_own", [NT, D], F32, kind="ExternalInput").ap()
    wqkvT = nc.dram_tensor("wqkvT", [D, 3 * D], F32, kind="ExternalInput").ap()
    bqkv = nc.dram_tensor("bqkv", [1, 3 * D], F32, kind="ExternalInput").ap()
    ident_in = nc.dram_tensor("ident", [P, P], BF16, kind="ExternalInput").ap()
    onescol_in = nc.dram_tensor("onescol", [P, 1], BF16, kind="ExternalInput").ap()
    onesrow_in = nc.dram_tensor("onesrow", [1, P], BF16, kind="ExternalInput").ap()
    stot_out = nc.dram_tensor("stot", [P, 528], F32, kind="ExternalOutput").ap()

    with ExitStack() as ctx:
        ident = ctx.enter_context(nc.sbuf_tensor("identS", [P, P], BF16))
        onescol = ctx.enter_context(nc.sbuf_tensor("onescolS", [P, 1], BF16))
        onesrow = ctx.enter_context(nc.sbuf_tensor("onesrowS", [1, P], BF16))
        brow = ctx.enter_context(nc.sbuf_tensor("brow", [1, 3 * D], BF16))
        wkv_bf = ctx.enter_context(nc.sbuf_tensor("wkv_bf", [P, KC * 2 * D], BF16))
        wld_t = ctx.enter_context(nc.sbuf_tensor("wld", [P, 2 * 3 * D], F32))
        xt_t = ctx.enter_context(nc.sbuf_tensor("xt", [P, 2 * D], F32))
        h1_tm = ctx.enter_context(nc.sbuf_tensor("h1_tm", [P, D], BF16))
        h1T = ctx.enter_context(nc.sbuf_tensor("h1T", [P, D], BF16))
        phik_tm = ctx.enter_context(nc.sbuf_tensor("phik_tm", [P, D], BF16))
        v_tm = ctx.enter_context(nc.sbuf_tensor("v_tm", [P, D], BF16))
        t1 = ctx.enter_context(nc.sbuf_tensor("t1", [P, D], F32))
        t2 = ctx.enter_context(nc.sbuf_tensor("t2", [P, D], F32))
        t3 = ctx.enter_context(nc.sbuf_tensor("t3", [P, D], F32))
        stats = ctx.enter_context(nc.sbuf_tensor("stats", [P, 12], F32))
        mv = ctx.enter_context(nc.sbuf_tensor("mv", [P, 2], F32))
        rstd = ctx.enter_context(nc.sbuf_tensor("rstd", [P, 1], F32))
        c_eps = ctx.enter_context(nc.sbuf_tensor("c_eps", [P, 1], F32))
        sacc = ctx.enter_context(nc.sbuf_tensor("sacc", [P, 528], F32))
        pT = ctx.enter_context(nc.psum_tensor("pT", [P, 1024], BF16))
        pK = ctx.enter_context(nc.psum_tensor("pK", [P, 1024], F32))
        pV = ctx.enter_context(nc.psum_tensor("pV", [P, 1024], F32))
        pS = ctx.enter_context(nc.psum_tensor("pS", [P, 512], F32))
        pC = ctx.enter_context(nc.psum_tensor("pC", [P, 512], F32))
        h_x0 = ctx.enter_context(nc.semaphore("m_x0"))
        h_x1 = ctx.enter_context(nc.semaphore("m_x1"))
        h_c = ctx.enter_context(nc.semaphore("m_c"))
        h_ld = ctx.enter_context(nc.semaphore("m_ld"))
        h_pe = ctx.enter_context(nc.semaphore("m_pe"))
        h_v = ctx.enter_context(nc.semaphore("m_v"))
        h_s = ctx.enter_context(nc.semaphore("m_s"))
        h_out = ctx.enter_context(nc.semaphore("m_out"))
        block = ctx.enter_context(nc.Block())

        wld = [wld_t.ap()[:, 0:3 * D], wld_t.ap()[:, 3 * D:6 * D]]
        xt = [xt_t.ap()[:, 0:D], xt_t.ap()[:, D:2 * D]]
        sems = {"x0": h_x0, "x1": h_x1, "c": h_c, "ld": h_ld, "pe": h_pe,
                "v": h_v, "s": h_s, "out": h_out}

        def body_sync(E):
            sy = E.eng
            for name, dst, srcap in (
                ("ident", ident[:], ident_in),
                ("onescol", onescol[:], onescol_in),
                ("onesrow", onesrow[:], onesrow_in),
            ):
                E.inc(sy.dma_start(out=dst, in_=srcap), "c", name, 16)
            E.inc(sy.dma_start(out=wld[0][0:1, :], in_=bqkv), "c", "call", 16)
            for c in range(KC):
                E.wait("v", "bq" if c == 0 else ("wcv", c - 1))
                E.inc(sy.dma_start(out=wld[0],
                                   in_=wqkvT[c * P:(c + 1) * P, :]),
                      "ld", ("w", c), 16)
            for j in range(T):
                if j >= 2:
                    E.wait("pe", ("h1T", j - 2))
                E.inc(sy.dma_start(out=xt[j % 2],
                                   in_=x_in[j * TOK:(j + 1) * TOK, :]),
                      "x%d" % (j % 2), ("x", j), 16)
            E.wait_val("out", 16)

        def body_vector(E):
            v = E.eng
            E.wait("c", "call")
            E.inc(v.tensor_copy(out=brow[:], in_=wld[0][0:1, :]), "v", "bq")
            for c in range(KC):
                E.wait("ld", ("w", c))
                E.inc(v.tensor_copy(out=wkv_bf[:, c * 2 * D:(c + 1) * 2 * D],
                                    in_=wld[0][:, D:3 * D]),
                      "v", ("wcv", c))
            E.inc(v.memset(sacc[:], 0.0), "v", "sz")
            E.inc(v.memset(c_eps[:], LN_EPS), "v", "ceps")
            for j in range(T):
                E.wait("x%d" % (j % 2), ("x", j))
                emit_ln_stats(E, v, stats, mv, xt[j % 2][:, 0:512],
                              xt[j % 2][:, 512:1024], ("xa", j))
                E.wait("s", ("rstd", j))
                E.inc(v.tensor_scalar(out=h1_tm[:], in0=xt[j % 2],
                                      scalar1=mv[:, 0:1], scalar2=rstd[:],
                                      op0=ALU.subtract, op1=ALU.mult),
                      "v", ("h1", j))
                E.wait("pe", ("h1T", j))
                E.inc(v.tensor_copy(out=h1T[:], in_=pT[:]), "v", ("h1T", j))
                E.wait("s", ("kel", j))
                E.inc(v.scalar_tensor_tensor(out=phik_tm[:], in0=t2[:],
                                             scalar=FEAT_EPS, in1=t3[:],
                                             op0=ALU.add, op1=ALU.add),
                      "v", ("phik", j))
                for r in range(2):
                    E.wait("pe", ("st", j, r))
                    ps = (pS if r == 0 else pC)[:]
                    sview = sacc[:, 0:512]
                    v.tensor_tensor(
                        out=sview[0:64].rearrange("p (g f) -> p g f", f=64
                                                  )[:, 4 * r:4 * r + 4, :],
                        in0=sview[0:64].rearrange("p (g f) -> p g f", f=64
                                                  )[:, 4 * r:4 * r + 4, :],
                        in1=ps[0:64].rearrange("p (g f) -> p g f", f=P
                                               )[:, :, 0:64],
                        op=ALU.add)
                    E.inc(v.tensor_tensor(
                        out=sview[64:128].rearrange("p (g f) -> p g f", f=64
                                                    )[:, 4 * r:4 * r + 4, :],
                        in0=sview[64:128].rearrange("p (g f) -> p g f", f=64
                                                    )[:, 4 * r:4 * r + 4, :],
                        in1=ps[64:128].rearrange("p (g f) -> p g f", f=P
                                                 )[:, :, 64:128],
                        op=ALU.add), "v", ("sadd", j, r))
                E.wait("pe", ("scol", j))
                sc = sacc[:, 512:528]
                v.tensor_tensor(
                    out=sc[0:64].rearrange("p (g o) -> p g o", o=1)[:, 0:16:2, :],
                    in0=sc[0:64].rearrange("p (g o) -> p g o", o=1)[:, 0:16:2, :],
                    in1=pK[0:64, 0:8].rearrange("p (g o) -> p g o", o=1),
                    op=ALU.add)
                E.inc(v.tensor_tensor(
                    out=sc[64:128].rearrange("p (g o) -> p g o", o=1)[:, 1:16:2, :],
                    in0=sc[64:128].rearrange("p (g o) -> p g o", o=1)[:, 1:16:2, :],
                    in1=pK[64:128, 0:8].rearrange("p (g o) -> p g o", o=1),
                    op=ALU.add), "v", ("scadd", j))

        def body_scalar(E):
            s = E.eng
            E.wait("v", "ceps")
            for j in range(T):
                E.wait("v", ("xa", j))
                E.inc(emit_rstd(E, s, mv, rstd, c_eps[:]), "s", ("rstd", j))
                E.wait("pe", ("k", j))
                s.activation(out=t1[:], in_=pK[:], func=AF.Relu, scale=-1.0)
                s.activation(out=t2[:], in_=t1[:], func=AF.Exp, scale=-1.0)
                E.inc(s.activation(out=t3[:], in_=pK[:], func=AF.Relu),
                      "s", ("kel", j))
                E.wait("pe", ("v", j))
                E.inc(s.activation(out=v_tm[:], in_=pV[:], func=AF.Copy),
                      "s", ("v", j))
            E.wait("v", ("scadd", T - 1))
            E.inc(s.dma_start(out=stot_out, in_=sacc[:]), "out", "res", 16)

        def body_tensor(E):
            t = E.eng
            for j in range(T):
                E.wait("v", ("h1", j))
                if j == 0:
                    E.wait("c", "call")
                if j > 0:
                    E.wait("v", ("h1T", j - 1))
                for c in range(KC):
                    ins = t.transpose(pT[:, c * P:(c + 1) * P],
                                      h1_tm[:, c * P:(c + 1) * P], ident[:])
                E.inc(ins, "pe", ("h1T", j))
                E.wait("v", ("h1T", j))
                if j == 0:
                    E.wait("v", ("wcv", KC - 1))
                for half, pdst, name in ((0, pK, "k"), (1, pV, "v")):
                    if j > 0:
                        if half == 0:
                            E.wait("s", ("kel", j - 1))
                            E.wait("v", ("scadd", j - 1))
                        else:
                            E.wait("s", ("v", j - 1))
                            E.wait("pe", ("st", j - 1, 1))
                    for n4 in range(2):
                        for c in range(KC):
                            t.matmul(pdst[:, n4 * 512:(n4 + 1) * 512],
                                     h1T[:, c * P:(c + 1) * P],
                                     wkv_bf[:, c * 2 * D + half * D + n4 * 512:
                                            c * 2 * D + half * D + (n4 + 1) * 512],
                                     start=(c == 0), stop=False)
                        ins = t.matmul(pdst[:, n4 * 512:(n4 + 1) * 512],
                                       onesrow[:],
                                       brow[0:1, D + half * D + n4 * 512:
                                            D + half * D + (n4 + 1) * 512],
                                       start=False, stop=True)
                    E.inc(ins, "pe", (name, j))
                E.wait("v", ("phik", j))
                E.wait("s", ("v", j))
                for r in range(2):
                    ps = pS if r == 0 else pC
                    if j > 0:
                        E.wait("v", ("sadd", j - 1, r))
                    for pp in range(4):
                        pr = r * 4 + pp
                        ins = t.matmul(ps[:, pp * P:(pp + 1) * P],
                                       phik_tm[:, pr * P:(pr + 1) * P],
                                       v_tm[:, pr * P:(pr + 1) * P],
                                       start=True, stop=True)
                    E.inc(ins, "pe", ("st", j, r))
                E.wait("s", ("kel", j))
                if j > 0:
                    E.wait("v", ("scadd", j - 1))
                for pr in range(NPAIR):
                    ins = t.matmul(pK[:, pr:pr + 1],
                                   phik_tm[:, pr * P:(pr + 1) * P],
                                   onescol[:], start=True, stop=True)
                E.inc(ins, "pe", ("scol", j))

        run_bodies([
            (block.sync, body_sync, None),
            (block.vector, body_vector, "v"),
            (block.scalar, body_scalar, "s"),
            (block.tensor, body_tensor, None),
        ], sems)
    return nc




def build_main(T, upto='full'):
    STAGES = ['setup', 'h1', 'qkv', 'attn', 'wout', 'main', 'full']
    LVL = STAGES.index(upto)
    nc = bass.Bass("TRN2", target_bir_lowering=False)
    NT = T * TOK
    HT = (T + 1) // 2
    HLEN = HT * TOK
    SL = 256
    NSL = HID // SL

    x_in = nc.dram_tensor("x_main", [NT, D], F32, kind="ExternalInput").ap()
    wqkvT_in = nc.dram_tensor("wqkvT", [D, 3 * D], F32, kind="ExternalInput").ap()
    woutT_in = nc.dram_tensor("woutT", [D, D], F32, kind="ExternalInput").ap()
    bqkv_in = nc.dram_tensor("bqkv", [1, 3 * D], F32, kind="ExternalInput").ap()
    bout_in = nc.dram_tensor("bout", [1, D], F32, kind="ExternalInput").ap()
    gamma_in = nc.dram_tensor("gamma", [1, D], F32, kind="ExternalInput").ap()
    beta_in = nc.dram_tensor("beta", [1, D], F32, kind="ExternalInput").ap()
    wgT_in = nc.dram_tensor("wgT", [D, HID], I32, kind="ExternalInput").ap()
    wuT_in = nc.dram_tensor("wuT", [D, HID], I32, kind="ExternalInput").ap()
    wdT_in = nc.dram_tensor("wdT", [HID, D], I32, kind="ExternalInput").ap()
    maskT4_in = nc.dram_tensor("maskT4", [P, 512], F32, kind="ExternalInput").ap()
    ident_in = nc.dram_tensor("ident", [P, P], BF16, kind="ExternalInput").ap()
    onescol_in = nc.dram_tensor("onescol", [P, 1], BF16, kind="ExternalInput").ap()
    onesrow_in = nc.dram_tensor("onesrow", [1, P], BF16, kind="ExternalInput").ap()
    onesf_in = nc.dram_tensor("onesf", [1, P], F32, kind="ExternalInput").ap()
    stot_in = nc.dram_tensor("stot_all", [P, 8 * 528], F32,
                             kind="ExternalInput").ap()
    maskbc_in = nc.dram_tensor("mask_bc", [P, 8], F32, kind="ExternalInput").ap()

    out = nc.dram_tensor("out", [NT, D], F32, kind="ExternalOutput").ap()
    x2_hbm = nc.dram_tensor("x2_scratch", [NT, D], F32, kind="Internal").ap()

    with ExitStack() as ctx:
        maskT4 = ctx.enter_context(nc.sbuf_tensor("maskT4S", [P, 512], F32))
        ident = ctx.enter_context(nc.sbuf_tensor("identS", [P, P], BF16))
        onescol = ctx.enter_context(nc.sbuf_tensor("onescolS", [P, 1], BF16))
        onesrow = ctx.enter_context(nc.sbuf_tensor("onesrowS", [1, P], BF16))
        onesf = ctx.enter_context(nc.sbuf_tensor("onesfS", [1, P], F32))
        brow = ctx.enter_context(nc.sbuf_tensor("brow", [1, 3 * D], BF16))
        borow = ctx.enter_context(nc.sbuf_tensor("borow", [1, D], BF16))
        stats = ctx.enter_context(nc.sbuf_tensor("stats", [P, 12], F32))
        mv = ctx.enter_context(nc.sbuf_tensor("mv", [P, 2], F32))
        rstd = ctx.enter_context(nc.sbuf_tensor("rstd", [P, 1], F32))
        c_eps = ctx.enter_context(nc.sbuf_tensor("c_eps", [P, 1], F32))
        rden = ctx.enter_context(nc.sbuf_tensor("rden", [P, 16], F32))
        rdpre = ctx.enter_context(nc.sbuf_tensor("rdpre", [P, 16], F32))
        gbc = ctx.enter_context(nc.sbuf_tensor("gbc", [P, D], F32))
        bbc = ctx.enter_context(nc.sbuf_tensor("bbc", [P, D], F32))
        wq_bf = ctx.enter_context(nc.sbuf_tensor("wq_bf", [P, KC * 3 * D], BF16))
        wo_bf = ctx.enter_context(nc.sbuf_tensor("wo_bf", [P, KC * D], BF16))
        stall = ctx.enter_context(nc.sbuf_tensor("stall", [P, 8 * 528], F32))
        mbc = ctx.enter_context(nc.sbuf_tensor("mbc", [P, 8], F32))
        sini = ctx.enter_context(nc.sbuf_tensor("sini", [P, 528], F32))
        srun = ctx.enter_context(nc.sbuf_tensor("srun", [P, 8 * P], F32))
        ssum = ctx.enter_context(nc.sbuf_tensor("ssum", [P, 16], F32))
        srun_bf = ctx.enter_context(nc.sbuf_tensor("srun_bf", [P, 8 * P], BF16))
        ssum_bf = ctx.enter_context(nc.sbuf_tensor("ssum_bf", [P, 16], BF16))
        xt_t = ctx.enter_context(nc.sbuf_tensor("xt", [P, 2 * D], F32))
        h1_tm = ctx.enter_context(nc.sbuf_tensor("h1_tm", [P, D], BF16))
        h1T = ctx.enter_context(nc.sbuf_tensor("h1T", [P, D], BF16))
        phiq = ctx.enter_context(nc.sbuf_tensor("phiq", [P, D], BF16))
        phiq_e = ctx.enter_context(nc.sbuf_tensor("phiq_e", [P, D], BF16))
        phiq_o = ctx.enter_context(nc.sbuf_tensor("phiq_o", [P, D], BF16))
        phik_tm = ctx.enter_context(nc.sbuf_tensor("phik_tm", [P, D], BF16))
        phik_fm = ctx.enter_context(nc.sbuf_tensor("phik_fm", [P, D], BF16))
        v_tm = ctx.enter_context(nc.sbuf_tensor("v_tm", [P, D], BF16))
        am = ctx.enter_context(nc.sbuf_tensor("am", [P, 512], BF16))
        attnT = ctx.enter_context(nc.sbuf_tensor("attnT", [P, D], BF16))
        attn_bf = ctx.enter_context(nc.sbuf_tensor("attn_bf", [P, D], BF16))
        t1 = ctx.enter_context(nc.sbuf_tensor("t1", [P, D], F32))
        t2 = ctx.enter_context(nc.sbuf_tensor("t2", [P, D], F32))
        t3 = ctx.enter_context(nc.sbuf_tensor("t3", [P, D], F32))
        o_n = ctx.enter_context(nc.sbuf_tensor("o_n", [P, D], F32))
        a_g = ctx.enter_context(nc.sbuf_tensor("a_g", [P, D], F32))
        x2sb_t = ctx.enter_context(nc.sbuf_tensor("x2sb", [P, 2 * D], F32))
        h2T = ctx.enter_context(nc.sbuf_tensor("h2T", [P, KC * 1024], BF16))
        wg_sl_t = ctx.enter_context(nc.sbuf_tensor("wg_sl", [P, 2 * KC * SL], BF16))
        wu_sl_t = ctx.enter_context(nc.sbuf_tensor("wu_sl", [P, 2 * KC * SL], BF16))
        silu_t = ctx.enter_context(nc.sbuf_tensor("silu_t", [P, 1024], F32))
        f_n = ctx.enter_context(nc.sbuf_tensor("f_n", [P, D], F32))
        wdx = ctx.enter_context(nc.sbuf_tensor("wdx", [P, 1024], BF16))
        pT = ctx.enter_context(nc.psum_tensor("pT", [P, 1024], BF16))
        pQ = ctx.enter_context(nc.psum_tensor("pQ", [P, 1024], F32))
        pK = ctx.enter_context(nc.psum_tensor("pK", [P, 1024], F32))
        pV = ctx.enter_context(nc.psum_tensor("pV", [P, 1024], F32))
        pD = ctx.enter_context(nc.psum_tensor("pD", [P, 512], F32))
        h_x0 = ctx.enter_context(nc.semaphore("s_x0"))
        h_x1 = ctx.enter_context(nc.semaphore("s_x1"))
        h_c = ctx.enter_context(nc.semaphore("s_c"))
        h_ld = ctx.enter_context(nc.semaphore("s_ld"))
        h_pe = ctx.enter_context(nc.semaphore("s_pe"))
        h_v = ctx.enter_context(nc.semaphore("s_v"))
        h_s = ctx.enter_context(nc.semaphore("s_s"))
        h_p = ctx.enter_context(nc.semaphore("s_p"))
        h_w0 = ctx.enter_context(nc.semaphore("s_w0"))
        h_w1 = ctx.enter_context(nc.semaphore("s_w1"))
        h_o0 = ctx.enter_context(nc.semaphore("s_o0"))
        h_o1 = ctx.enter_context(nc.semaphore("s_o1"))
        block = ctx.enter_context(nc.Block())

        xt = [xt_t.ap()[:, 0:D], xt_t.ap()[:, D:2 * D]]
        x2sb = [x2sb_t.ap()[:, 0:D], x2sb_t.ap()[:, D:2 * D]]
        wg_sl = [wg_sl_t.ap()[:, 0:KC * SL], wg_sl_t.ap()[:, KC * SL:]]
        wu_sl = [wu_sl_t.ap()[:, 0:KC * SL], wu_sl_t.ap()[:, KC * SL:]]
        sems = {"x0": h_x0, "x1": h_x1, "c": h_c, "ld": h_ld, "pe": h_pe,
                "v": h_v, "s": h_s, "p": h_p, "w0": h_w0, "w1": h_w1,
                "o0": h_o0, "o1": h_o1}
        if upto == 'full':
            assert HLEN == 1024 and T == 16, "V1 layout sized for T=16"

        wld = h2T.ap().bitcast(F32)[:, 0:3 * D]
        hst = wq_bf.ap()
        wgu_int = stall.ap().bitcast(I32)[:, 0:KC * SL]
        wuu_int = stall.ap().bitcast(I32)[:, KC * SL:2 * KC * SL]
        wd_int = srun.ap().bitcast(I32)[:, 0:1024]
        wd_bf = []
        for tens in (t1, t2, t3, o_n, a_g, gbc, bbc):
            b = tens.ap().bitcast(BF16)
            wd_bf.append(b[:, 0:1024])
            wd_bf.append(b[:, 1024:2048])
        wd_bf.append(sini.ap().bitcast(BF16)[:, 0:1024])
        wd_bf.append(srun_bf.ap()[:, 0:1024])
        for tens in (phiq, phik_tm, phik_fm, v_tm, wdx, h1T, attnT,
                     attn_bf):
            wd_bf.append(tens.ap())
        assert len(wd_bf) == HC
        x2r = [xt[0], xt[1]]
        outsb = [x2sb[0], x2sb[1]]
        pT32 = pT.ap().bitcast(F32)

        NU = 2
        UNIT_LIST = []
        for ha in range(2):
            for sl in range(NSL):
                for u in range(NU):
                    UNIT_LIST.append((ha * NSL + sl, u))
        KSEQ_OF = {ku: i for i, ku in enumerate(UNIT_LIST)}
        GUP_BANKS = {}
        for i, ku in enumerate(UNIT_LIST):
            if i % 2 == 0:
                GUP_BANKS[ku] = (pQ[:], [pK[:]])
            else:
                GUP_BANKS[ku] = (pV[:], [pT32, pD[:]])

        def halves():
            for ha in range(2):
                yield (ha, 0, HT) if ha == 0 else (ha, HT, T - HT)

        LASTK = [NSL - 1, 2 * NSL - 1]
        H0_UNITS = [ku for ku in UNIT_LIST if ku[0] < NSL]
        H0_LAST_ODD = [ku for ku in H0_UNITS if KSEQ_OF[ku] % 2 == 1][-1]

        def body_sync(E):
            sy = E.eng
            for name, dst, srcap in (
                ("maskT4", maskT4[:], maskT4_in),
                ("ident", ident[:], ident_in),
                ("onescol", onescol[:], onescol_in),
                ("onesrow", onesrow[:], onesrow_in),
                ("onesf", onesf[:], onesf_in),
                ("stall", stall[:], stot_in),
                ("mbc", mbc[:], maskbc_in),
            ):
                E.inc(sy.dma_start(out=dst, in_=srcap), "c", name, 16)
            E.inc(sy.dma_start(out=wld[0:1, :], in_=bqkv_in), "c", "bq", 16)
            E.inc(sy.dma_start(out=t1[0:1, 0:D], in_=bout_in), "c", "bo", 16)
            E.inc(sy.dma_start(out=t2[0:1, 0:D], in_=gamma_in), "c", "ga", 16)
            E.inc(sy.dma_start(out=t3[0:1, 0:D], in_=beta_in), "c", "call", 16)
            for c in range(KC):
                if c == 0:
                    E.wait("v", "rows")
                    E.wait("pe", "gbcmm")
                else:
                    E.wait("v", ("wq", c - 1))
                E.inc(sy.dma_start(out=wld[:, :],
                                   in_=wqkvT_in[c * P:(c + 1) * P, :]),
                      "ld", ("wq", c), 16)
            for c in range(KC):
                E.wait("v", ("wq", KC - 1) if c == 0 else ("wo", c - 1))
                E.inc(sy.dma_start(out=wld[:, 0:D],
                                   in_=woutT_in[c * P:(c + 1) * P, :]),
                      "ld", ("wo", c), 16)
            if upto == 'setup':
                return
            for j in range(T):
                if j >= 2:
                    if LVL >= 5:
                        E.wait("p", ("x2", j - 2))
                    else:
                        E.wait("v", ("h1", j - 2))
                E.inc(sy.dma_start(out=xt[j % 2],
                                   in_=x_in[j * TOK:(j + 1) * TOK, :]),
                      "x%d" % (j % 2), ("x", j), 16)
            if upto != 'full':
                if upto == 'main':
                    for jp in range(max(0, T - 2), T):
                        E.wait("w%d" % (jp % 2), ("x2w", jp))
                return
            buffree = [("p", ("x2", T - 1)), ("p", ("x2", T - 1))]
            for ha, t0, tn in halves():
                for tt in range(tn):
                    tg = t0 + tt
                    E.wait("w%d" % (tg % 2), ("x2w", tg))
                    E.wait(*buffree[tt % 2])
                    E.inc(sy.dma_start(out=x2r[tt % 2],
                                       in_=x2_hbm[tg * TOK:(tg + 1) * TOK, :]),
                          "x%d" % (tt % 2), ("x2a", ha, tt), 16)
                    buffree[tt % 2] = ("v", ("h2", ha, tt))
                for sl in range(NSL):
                    kseq = ha * NSL + sl
                    if kseq == 0:
                        E.wait("v", "sexp")
                        E.wait("v", ("wo", KC - 1))
                    elif kseq == NSL:
                        E.wait("p", ("wdc", HC - 1))
                        E.wait("p", ("wuc", kseq - 1))
                    else:
                        E.wait("p", ("wuc", kseq - 1))
                    if kseq >= 2:
                        E.wait("pe", ("gup", kseq - 2, NU - 1))
                    E.inc(sy.dma_start(
                        out=wgu_int.rearrange("p (c h) -> p c h", c=KC),
                        in_=wgT_in[:, sl * SL:(sl + 1) * SL].rearrange(
                            "(c p) h -> p c h", p=P)), "ld", ("wg", kseq), 16)
                    E.wait("p", ("wgc", kseq))
                    if kseq >= 2:
                        E.wait("pe", ("gup", kseq - 2, NU - 1))
                    E.inc(sy.dma_start(
                        out=wuu_int.rearrange("p (c h) -> p c h", c=KC),
                        in_=wuT_in[:, sl * SL:(sl + 1) * SL].rearrange(
                            "(c p) h -> p c h", p=P)), "ld", ("wu", kseq), 16)
                if ha == 0:
                    for c in range(HC):
                        if c == 0:
                            E.wait("p", ("x2", T - 1))
                            E.wait("v", ("st", T - 1))
                            E.wait("w0", ("x2w", T - 2))
                            E.wait("w1", ("x2w", T - 1))
                            E.wait("p", ("wuc", NSL - 1))
                        else:
                            E.wait("p", ("wdc", c - 1))
                        E.inc(sy.dma_start(out=wd_int,
                                           in_=wdT_in[c * P:(c + 1) * P, :]),
                              "ld", ("wd", c), 16)
                for tt in range(tn):
                    tg = t0 + tt
                    E.wait(*buffree[tt % 2])
                    E.inc(sy.dma_start(out=x2r[tt % 2],
                                       in_=x2_hbm[tg * TOK:(tg + 1) * TOK, :]),
                          "x%d" % (tt % 2), ("x2b", ha, tt), 16)
                    buffree[tt % 2] = ("p", ("res", ha, tt))
            E.wait("o0", ("out", 1, T - HT - 2))
            E.wait("o1", ("out", 1, T - HT - 1))

        def body_vector(E):
            v = E.eng
            E.wait("c", "call")
            v.tensor_copy(out=brow[:], in_=wld[0:1, :])
            E.inc(v.tensor_copy(out=borow[:], in_=t1[0:1, 0:D]), "v", "rows")
            E.wait("pe", "gbcmm")
            v.tensor_copy(out=gbc[:], in_=pQ[:])
            E.inc(v.tensor_copy(out=bbc[:], in_=pK[:]), "v", "gbcc")
            for c in range(KC):
                E.wait("ld", ("wq", c))
                E.inc(v.tensor_copy(out=wq_bf[:, c * 3 * D:(c + 1) * 3 * D],
                                    in_=wld[:, :]), "v", ("wq", c))
            for c in range(KC):
                E.wait("ld", ("wo", c))
                E.inc(v.tensor_copy(out=wo_bf[:, c * D:(c + 1) * D],
                                    in_=wld[:, 0:D]), "v", ("wo", c))
            E.inc(v.memset(c_eps[:], LN_EPS), "v", "ceps")
            v.memset(phiq_e[:], 0.0)
            E.inc(v.memset(phiq_o[:], 0.0), "v", "phiqz0")
            v.tensor_scalar(out=sini[:], in0=stall[:, 0:528],
                            scalar1=mbc[:, 0:1], scalar2=None, op0=ALU.mult)
            for r in range(1, 8):
                v.tensor_scalar(out=t1[:, 0:528],
                                in0=stall[:, r * 528:(r + 1) * 528],
                                scalar1=mbc[:, r:r + 1], scalar2=None,
                                op0=ALU.mult)
                v.tensor_tensor(out=sini[:], in0=sini[:], in1=t1[:, 0:528],
                                op=ALU.add)
            E.inc(v.memset(srun[:], 0.0), "v")
            E.selfwait("v")
            v.tensor_copy(
                out=srun[0:64, :].rearrange("p (g f) -> p g f", f=P)[:, :, 0:64],
                in_=sini[0:64, 0:512].rearrange("p (g f) -> p g f", f=64))
            v.tensor_copy(
                out=srun[64:128, :].rearrange("p (g f) -> p g f", f=P
                                              )[:, :, 64:128],
                in_=sini[64:128, 0:512].rearrange("p (g f) -> p g f", f=64))
            E.inc(v.tensor_copy(out=ssum[:], in_=sini[:, 512:528]),
                  "v", "sexp")
            E.selfwait("v")
            if upto == 'setup':
                return
            for j in range(T):
                E.wait("x%d" % (j % 2), ("x", j))
                if j > 0 and LVL >= 5:
                    E.wait("p", ("anorm", j - 1))
                emit_ln_stats(E, v, stats, mv, xt[j % 2][:, 0:512],
                              xt[j % 2][:, 512:1024], ("xa", j))
                E.wait("s", ("xrstd", j))
                E.inc(v.tensor_scalar(out=h1_tm[:], in0=xt[j % 2],
                                      scalar1=mv[:, 0:1], scalar2=rstd[:],
                                      op0=ALU.subtract, op1=ALU.mult),
                      "v", ("h1", j))
                E.wait("pe", ("h1T", j))
                E.inc(v.tensor_copy(out=h1T[:], in_=pT[:]), "v", ("h1T", j))
                if LVL < 2:
                    continue
                if LVL >= 3:
                    if j > 0:
                        E.wait("pe", ("scol", j - 1))
                    v.tensor_copy(out=srun_bf[:], in_=srun[:])
                    E.inc(v.tensor_copy(out=ssum_bf[:], in_=ssum[:]),
                          "v", ("scast", j))
                E.wait("s", ("qel", j))
                E.inc(v.scalar_tensor_tensor(out=phiq[:], in0=t2[:],
                                             scalar=FEAT_EPS, in1=t3[:],
                                             op0=ALU.add, op1=ALU.add),
                      "v", ("phiq", j))
                if LVL >= 3:
                    if j > 0:
                        E.wait("pe", ("at", j - 1, 3))
                    v.tensor_copy(out=phiq_e[0:64, :], in_=phiq[0:64, :])
                    E.inc(v.tensor_copy(out=phiq_o[64:128, :],
                                        in_=phiq[64:128, :]),
                          "v", ("phiqz", j))
                E.wait("s", ("kel", j))
                E.inc(v.scalar_tensor_tensor(out=phik_tm[:], in0=t1[:],
                                             scalar=FEAT_EPS, in1=t3[:],
                                             op0=ALU.add, op1=ALU.add),
                      "v", ("phik", j))
                E.wait("pe", ("kfm", j))
                E.inc(v.tensor_copy(out=phik_fm[:], in_=pT[:]),
                      "v", ("kfm", j))
                if LVL < 3:
                    continue
                if j == 0:
                    E.wait("c", "call")
                for g in range(4):
                    E.wait("pe", ("at", j, g))
                    if g > 0:
                        E.wait("pe", ("avg", j, g - 1))
                    E.inc(v.tensor_tensor(out=am[:], in0=pV[:, 0:512],
                                          in1=maskT4[:], op=ALU.mult),
                          "v", ("mask", j, g))
                E.wait("pe", ("avg", j, 3))
                if j > 0:
                    E.wait("s", ("div", j - 1))
                E.inc(v.tensor_scalar(out=rdpre[:], in0=pD[:, 0:16],
                                      scalar1=DEN_EPS, scalar2=None,
                                      op0=ALU.add), "v")
                E.selfwait("v")
                E.inc(v.reciprocal(out=rden[:], in_=rdpre[:]),
                      "v", ("rden", j))
                E.selfwait("v")
                for h in range(8, 16):
                    ins = v.tensor_scalar(
                        out=attn_bf[:, h * EH:(h + 1) * EH],
                        in0=pK[:, h * EH:(h + 1) * EH],
                        scalar1=rden[:, h:h + 1], scalar2=None, op0=ALU.mult)
                E.inc(ins, "v", ("div", j))
                for r in range(2):
                    E.wait("pe", ("st", j, r))
                    ps = pV[:, 512:1024]
                    v.tensor_tensor(
                        out=srun[0:64, :].rearrange("p (g f) -> p g f", f=P
                                                    )[:, 4 * r:4 * r + 4, 0:64],
                        in0=srun[0:64, :].rearrange("p (g f) -> p g f", f=P
                                                    )[:, 4 * r:4 * r + 4, 0:64],
                        in1=ps[0:64].rearrange("p (g f) -> p g f", f=P
                                               )[:, :, 0:64],
                        op=ALU.add)
                    E.inc(v.tensor_tensor(
                        out=srun[64:128, :].rearrange("p (g f) -> p g f", f=P
                                                      )[:, 4 * r:4 * r + 4,
                                                        64:128],
                        in0=srun[64:128, :].rearrange("p (g f) -> p g f", f=P
                                                      )[:, 4 * r:4 * r + 4,
                                                        64:128],
                        in1=ps[64:128].rearrange("p (g f) -> p g f", f=P
                                                 )[:, :, 64:128],
                        op=ALU.add), "v", ("sadd", j, r))
                E.wait("pe", ("scol", j))
                v.tensor_tensor(
                    out=ssum[0:64, :].rearrange("p (g o) -> p g o", o=1
                                                )[:, 0:16:2, :],
                    in0=ssum[0:64, :].rearrange("p (g o) -> p g o", o=1
                                                )[:, 0:16:2, :],
                    in1=pD[0:64, 32:40].rearrange("p (g o) -> p g o", o=1),
                    op=ALU.add)
                E.inc(v.tensor_tensor(
                    out=ssum[64:128, :].rearrange("p (g o) -> p g o", o=1
                                                  )[:, 1:16:2, :],
                    in0=ssum[64:128, :].rearrange("p (g o) -> p g o", o=1
                                                  )[:, 1:16:2, :],
                    in1=pD[64:128, 32:40].rearrange("p (g o) -> p g o", o=1),
                    op=ALU.add), "v", ("st", j))
                if LVL < 4:
                    continue
                E.wait("pe", ("attnT", j))
                E.inc(v.tensor_copy(out=attnT[:], in_=pT[:]),
                      "v", ("attnT", j))
                E.wait("pe", ("wout", j))
                emit_ln_stats(E, v, stats, mv, pQ[:, 0:512], pQ[:, 512:1024],
                              ("oa", j))
                E.wait("s", ("orstd", j))
                if j > 0:
                    E.wait("p", ("x2", j - 1))
                E.inc(v.tensor_scalar(out=o_n[:], in0=pQ[:],
                                      scalar1=mv[:, 0:1], scalar2=rstd[:],
                                      op0=ALU.subtract, op1=ALU.mult),
                      "v", ("onorm", j))
                if LVL < 5:
                    continue
                E.wait("p", ("gb", j))
                emit_ln_stats(E, v, stats, mv, a_g[:, 0:512], a_g[:, 512:1024],
                              ("aa", j))
            if upto != 'full':
                return
            for ha, t0, tn in halves():
                for tt in range(tn):
                    E.wait("x%d" % (tt % 2), ("x2a", ha, tt))
                    emit_ln_stats(E, v, stats, mv, x2r[tt % 2][:, 0:512],
                                  x2r[tt % 2][:, 512:1024], ("h2a", ha, tt))
                    E.wait("s", ("h2rstd", ha, tt))
                    E.inc(v.tensor_scalar(out=h1_tm[:], in0=x2r[tt % 2],
                                          scalar1=mv[:, 0:1], scalar2=rstd[:],
                                          op0=ALU.subtract, op1=ALU.mult),
                          "v", ("h2", ha, tt))
                    E.wait("pe", ("h2T", ha, tt))
                    E.inc(v.tensor_copy(
                        out=h2T.ap().rearrange("p (c i) -> p c i", c=KC
                                               )[:, :, tt * P:(tt + 1) * P],
                        in_=pT.ap().rearrange("p (c i) -> p c i", c=KC)),
                        "v", ("h2T", ha, tt))
                for sl in range(NSL):
                    kseq = ha * NSL + sl
                    for u in range(NU):
                        pgate, pups = GUP_BANKS[(kseq, u)]
                        E.wait("s", ("silu", kseq, u))
                        for hh in range(2):
                            if len(pups) == 1:
                                upsrc = pups[0][:, hh * 512:(hh + 1) * 512]
                            else:
                                upsrc = pups[hh]
                            ins = v.tensor_tensor(
                                out=hst.rearrange("p (c i) -> p c i", i=HLEN
                                                  )[:, 2 * sl + hh,
                                                    u * 512:(u + 1) * 512],
                                in0=silu_t[:, hh * 512:(hh + 1) * 512],
                                in1=upsrc, op=ALU.mult)
                        E.inc(ins, "v", ("hmul", kseq, u))
                for tt in range(tn):
                    pdn = pQ if tt % 2 == 0 else pK
                    E.wait("pe", ("down", ha, tt))
                    emit_ln_stats(E, v, stats, mv, pdn[:, 0:512],
                                  pdn[:, 512:1024], ("fa", ha, tt))
                    E.wait("s", ("frstd", ha, tt))
                    E.inc(v.tensor_scalar(out=f_n[:], in0=pdn[:],
                                          scalar1=mv[:, 0:1], scalar2=rstd[:],
                                          op0=ALU.subtract, op1=ALU.mult),
                          "v", ("fnorm", ha, tt))

        def body_scalar(E):
            s = E.eng
            E.wait("v", "ceps")
            if upto == 'setup':
                return
            for j in range(T):
                E.wait("v", ("xa", j))
                if j > 0 and LVL >= 5:
                    E.wait("p", ("anorm", j - 1))
                E.inc(emit_rstd(E, s, mv, rstd, c_eps[:]), "s", ("xrstd", j))
                if LVL < 2:
                    continue
                E.wait("pe", ("q", j))
                if j > 0:
                    E.wait("v", ("phik", j - 1))
                s.activation(out=t1[:], in_=pQ[:], func=AF.Relu, scale=-1.0)
                s.activation(out=t2[:], in_=t1[:], func=AF.Exp, scale=-1.0)
                E.inc(s.activation(out=t3[:], in_=pQ[:], func=AF.Relu),
                      "s", ("qel", j))
                E.wait("pe", ("k", j))
                E.wait("v", ("phiq", j))
                s.activation(out=t2[:], in_=pK[:], func=AF.Relu, scale=-1.0)
                s.activation(out=t1[:], in_=t2[:], func=AF.Exp, scale=-1.0)
                E.inc(s.activation(out=t3[:], in_=pK[:], func=AF.Relu),
                      "s", ("kel", j))
                E.wait("pe", ("v", j))
                E.inc(s.activation(out=v_tm[:], in_=pV[:], func=AF.Copy),
                      "s", ("v", j))
                if LVL < 3:
                    continue
                E.wait("v", ("rden", j))
                for h in range(8):
                    ins = s.activation(out=attn_bf[:, h * EH:(h + 1) * EH],
                                       in_=pK[:, h * EH:(h + 1) * EH],
                                       func=AF.Copy, scale=rden[:, h:h + 1])
                E.inc(ins, "s", ("div", j))
                if LVL < 4:
                    continue
                E.wait("v", ("oa", j))
                E.inc(emit_rstd(E, s, mv, rstd, c_eps[:]), "s", ("orstd", j))
                if LVL < 5:
                    continue
                E.wait("v", ("aa", j))
                E.inc(emit_rstd(E, s, mv, rstd, c_eps[:]), "s", ("arstd", j))
                E.wait("p", ("x2", j))
                E.inc(s.dma_start(out=x2_hbm[j * TOK:(j + 1) * TOK, :],
                                  in_=x2sb[j % 2]), "w%d" % (j % 2),
                      ("x2w", j), 16)
            if upto != 'full':
                return
            for ha, t0, tn in halves():
                for tt in range(tn):
                    E.wait("v", ("h2a", ha, tt))
                    E.inc(emit_rstd(E, s, mv, rstd, c_eps[:]),
                          "s", ("h2rstd", ha, tt))
                for sl in range(NSL):
                    kseq = ha * NSL + sl
                    for u in range(NU):
                        pgate, _pups = GUP_BANKS[(kseq, u)]
                        E.wait("pe", ("gup", kseq, u))
                        kk = KSEQ_OF[(kseq, u)]
                        if kk >= 1:
                            E.wait("v", ("hmul",) + UNIT_LIST[kk - 1])
                        E.inc(s.activation(out=silu_t[:], in_=pgate,
                                           func=AF.Silu),
                              "s", ("silu", kseq, u))
                for tt in range(tn):
                    E.wait("v", ("fa", ha, tt))
                    E.inc(emit_rstd(E, s, mv, rstd, c_eps[:]),
                          "s", ("frstd", ha, tt))
                    E.wait("p", ("res", ha, tt))
                    E.inc(s.dma_start(
                        out=out[(t0 + tt) * TOK:(t0 + tt + 1) * TOK, :],
                        in_=outsb[tt % 2]), "o%d" % (tt % 2),
                      ("out", ha, tt), 16)

        def body_pool(E):
            po = E.eng
            if LVL < 5:
                return
            for j in range(T):
                E.wait("v", ("onorm", j))
                if j == 0:
                    E.wait("v", "gbcc")
                po.tensor_tensor(out=a_g[:], in0=o_n[:], in1=gbc[:],
                                 op=ALU.mult)
                E.inc(po.tensor_tensor(out=a_g[:], in0=a_g[:], in1=bbc[:],
                                       op=ALU.add), "p", ("gb", j))
                E.wait("s", ("arstd", j))
                E.inc(po.tensor_scalar(out=o_n[:], in0=a_g[:],
                                       scalar1=mv[:, 0:1], scalar2=rstd[:],
                                       op0=ALU.subtract, op1=ALU.mult),
                      "p", ("anorm", j))
                if j >= 2:
                    E.wait("w%d" % (j % 2), ("x2w", j - 2))
                E.inc(po.tensor_tensor(out=x2sb[j % 2], in0=xt[j % 2],
                                       in1=o_n[:], op=ALU.add),
                      "p", ("x2", j))
            if upto != 'full':
                return
            for ha, t0, tn in halves():
                for sl in range(NSL):
                    kseq = ha * NSL + sl
                    E.wait("ld", ("wg", kseq))
                    E.inc(po.tensor_scalar(out=wg_sl[kseq % 2], in0=wgu_int,
                                           scalar1=WSCALE, scalar2=None,
                                           op0=ALU.mult), "p", ("wgc", kseq))
                    E.wait("ld", ("wu", kseq))
                    E.inc(po.tensor_scalar(out=wu_sl[kseq % 2], in0=wuu_int,
                                           scalar1=WSCALE, scalar2=None,
                                           op0=ALU.mult), "p", ("wuc", kseq))
                if ha == 0:
                    for c in range(HC):
                        E.wait("ld", ("wd", c))
                        E.inc(po.tensor_scalar(out=wd_bf[c], in0=wd_int,
                                               scalar1=WSCALE, scalar2=None,
                                               op0=ALU.mult),
                              "p", ("wdc", c))
                for tt in range(tn):
                    E.wait("v", ("fnorm", ha, tt))
                    E.wait("x%d" % (tt % 2), ("x2b", ha, tt))
                    if (ha, tt) == (0, 0):
                        E.wait("w1", ("x2w", T - 1))
                        E.wait("w0", ("x2w", T - 2))
                    if tt >= 2:
                        E.wait("o%d" % (tt % 2), ("out", ha, tt - 2))
                    elif ha == 1:
                        E.wait("o%d" % (tt % 2), ("out", 0, HT - 2 + tt))
                    E.inc(po.tensor_tensor(out=outsb[tt % 2],
                                           in0=x2r[tt % 2], in1=f_n[:],
                                           op=ALU.add), "p", ("res", ha, tt))

        def body_tensor(E):
            t = E.eng
            E.wait("c", "call")
            for n4 in range(2):
                t.matmul(pQ[:, n4 * 512:(n4 + 1) * 512], onesf[:],
                         t2[0:1, n4 * 512:(n4 + 1) * 512],
                         start=True, stop=True)
            for n4 in range(2):
                ins = t.matmul(pK[:, n4 * 512:(n4 + 1) * 512], onesf[:],
                               t3[0:1, n4 * 512:(n4 + 1) * 512],
                               start=True, stop=True)
            E.inc(ins, "pe", "gbcmm")
            if upto == 'setup':
                return
            for j in range(T):
                E.wait("v", ("h1", j))
                if j > 0:
                    E.wait("v", ("attnT", j - 1) if LVL >= 4 else
                           (("kfm", j - 1) if LVL >= 2 else ("h1T", j - 1)))
                for c in range(KC):
                    ins = t.transpose(pT[:, c * P:(c + 1) * P],
                                      h1_tm[:, c * P:(c + 1) * P], ident[:])
                E.inc(ins, "pe", ("h1T", j))
                if LVL < 2:
                    continue
                E.wait("v", ("h1T", j))
                if j == 0:
                    E.wait("v", ("wq", KC - 1))
                    E.wait("v", "gbcc")
                elif LVL >= 4:
                    E.wait("v", ("onorm", j - 1))
                    E.wait("v", ("oa", j - 1))
                else:
                    E.wait("s", ("qel", j - 1))
                for ec in range(KC):
                    for c in range(KC):
                        t.matmul(pQ[:, ec * P:(ec + 1) * P],
                                 wq_bf[:, c * 3 * D + ec * P:
                                       c * 3 * D + (ec + 1) * P],
                                 h1T[:, c * P:(c + 1) * P],
                                 start=(c == 0), stop=False)
                    ins = t.matmul(pQ[:, ec * P:(ec + 1) * P],
                                   brow[0:1, ec * P:(ec + 1) * P],
                                   onesrow[:], start=False, stop=True)
                E.inc(ins, "pe", ("q", j))
                for half, pdst, name in ((0, pK, "k"), (1, pV, "v")):
                    if j > 0:
                        if half == 0:
                            if LVL >= 3:
                                E.wait("s", ("div", j - 1))
                                E.wait("v", ("div", j - 1))
                            else:
                                E.wait("s", ("kel", j - 1))
                        else:
                            E.wait("s", ("v", j - 1))
                            if LVL >= 3:
                                E.wait("v", ("sadd", j - 1, 1))
                                E.wait("v", ("mask", j - 1, 3))
                    for n4 in range(2):
                        for c in range(KC):
                            t.matmul(pdst[:, n4 * 512:(n4 + 1) * 512],
                                     h1T[:, c * P:(c + 1) * P],
                                     wq_bf[:, c * 3 * D + D + half * D + n4 * 512:
                                           c * 3 * D + D + half * D +
                                           (n4 + 1) * 512],
                                     start=(c == 0), stop=False)
                        ins = t.matmul(pdst[:, n4 * 512:(n4 + 1) * 512],
                                       onesrow[:],
                                       brow[0:1, D + half * D + n4 * 512:
                                            D + half * D + (n4 + 1) * 512],
                                       start=False, stop=True)
                    E.inc(ins, "pe", (name, j))
                E.wait("v", ("phik", j))
                E.wait("v", ("h1T", j))
                for c in range(KC):
                    ins = t.transpose(pT[:, c * P:(c + 1) * P],
                                      phik_tm[:, c * P:(c + 1) * P], ident[:])
                E.inc(ins, "pe", ("kfm", j))
                if LVL < 3:
                    continue
                E.wait("v", ("kfm", j))
                E.wait("v", ("phiq", j))
                E.wait("s", ("v", j))
                E.wait("v", ("scast", j))
                if j > 0:
                    E.wait("v", ("st", j - 1))
                    E.wait("v", ("rden", j - 1))
                E.wait("v", ("phiqz", j))
                for g in range(4):
                    if g > 0:
                        E.wait("v", ("mask", j, g - 1))
                    for hh in range(4):
                        h = g * 4 + hh
                        pr, hlf = h // 2, h % 2
                        ins = t.matmul(
                            pV[:, hh * P:(hh + 1) * P],
                            phik_fm[:, pr * P:(pr + 1) * P],
                            (phiq_e if hlf == 0 else phiq_o)[:, pr * P:(pr + 1) * P],
                            start=True, stop=True)
                    E.inc(ins, "pe", ("at", j, g))
                    E.wait("v", ("mask", j, g))
                    for pp in range(2):
                        pr = 2 * g + pp
                        t.matmul(pD[:, 2 * pr:2 * pr + 2],
                                 phiq[:, pr * P:(pr + 1) * P],
                                 ssum_bf[:, 2 * pr:2 * pr + 2],
                                 start=True, stop=False)
                        t.matmul(pD[:, 2 * pr:2 * pr + 1],
                                 am[:, 2 * pp * P:(2 * pp + 1) * P],
                                 onescol[:], start=False, stop=False)
                        ins = t.matmul(pD[:, 2 * pr + 1:2 * pr + 2],
                                       am[:, (2 * pp + 1) * P:(2 * pp + 2) * P],
                                       onescol[:], start=False, stop=True)
                        t.matmul(pK[:, pr * P:(pr + 1) * P],
                                 phiq[:, pr * P:(pr + 1) * P],
                                 srun_bf[:, pr * P:(pr + 1) * P],
                                 start=True, stop=False)
                        for sub in range(2):
                            h = 2 * pr + sub
                            ins = t.matmul(pK[:, h * EH:(h + 1) * EH],
                                           am[:, (2 * pp + sub) * P:
                                              (2 * pp + sub + 1) * P],
                                           v_tm[:, h * EH:(h + 1) * EH],
                                           start=False, stop=(sub == 1))
                    E.inc(ins, "pe", ("avg", j, g))
                for r in range(2):
                    if r == 0:
                        E.wait("v", ("mask", j, 3))
                        if j > 0:
                            E.wait("v", ("sadd", j - 1, 1))
                    else:
                        E.wait("v", ("sadd", j, 0))
                    for pp in range(4):
                        pr = r * 4 + pp
                        ins = t.matmul(pV[:, 512 + pp * P:512 + (pp + 1) * P],
                                       phik_tm[:, pr * P:(pr + 1) * P],
                                       v_tm[:, pr * P:(pr + 1) * P],
                                       start=True, stop=True)
                    E.inc(ins, "pe", ("st", j, r))
                E.wait("v", ("rden", j))
                if j > 0:
                    E.wait("v", ("st", j - 1))
                for pr in range(NPAIR):
                    ins = t.matmul(pD[:, 32 + pr:33 + pr],
                                   phik_tm[:, pr * P:(pr + 1) * P],
                                   onescol[:], start=True, stop=True)
                E.inc(ins, "pe", ("scol", j))
                if LVL < 4:
                    continue
                E.wait("s", ("div", j))
                E.wait("v", ("div", j))
                E.wait("v", ("kfm", j))
                for c in range(KC):
                    ins = t.transpose(pT[:, c * P:(c + 1) * P],
                                      attn_bf[:, c * P:(c + 1) * P], ident[:])
                E.inc(ins, "pe", ("attnT", j))
                E.wait("v", ("attnT", j))
                if j == 0:
                    E.wait("v", ("wo", KC - 1))
                E.wait("s", ("qel", j))
                for n4 in range(2):
                    for c in range(KC):
                        t.matmul(pQ[:, n4 * 512:(n4 + 1) * 512],
                                 attnT[:, c * P:(c + 1) * P],
                                 wo_bf[:, c * D + n4 * 512:
                                       c * D + (n4 + 1) * 512],
                                 start=(c == 0), stop=False)
                    ins = t.matmul(pQ[:, n4 * 512:(n4 + 1) * 512],
                                   onesrow[:],
                                   borow[0:1, n4 * 512:(n4 + 1) * 512],
                                   start=False, stop=True)
                E.inc(ins, "pe", ("wout", j))
            if upto != 'full':
                return
            for ha, t0, tn in halves():
                for tt in range(tn):
                    E.wait("v", ("h2", ha, tt))
                    if (ha, tt) == (0, 0):
                        E.wait("v", ("attnT", T - 1))
                    elif ha == 1 and tt == 0:
                        E.wait("v", ("hmul",) + H0_LAST_ODD)
                        E.wait("s", ("silu",) + H0_LAST_ODD)
                    else:
                        E.wait("v", ("h2T", ha, tt - 1))
                    for c in range(KC):
                        ins = t.transpose(pT[:, c * P:(c + 1) * P],
                                          h1_tm[:, c * P:(c + 1) * P],
                                          ident[:])
                    E.inc(ins, "pe", ("h2T", ha, tt))
                for sl in range(NSL):
                    kseq = ha * NSL + sl
                    E.wait("p", ("wgc", kseq))
                    E.wait("p", ("wuc", kseq))
                    for u in range(NU):
                        pgate, pups = GUP_BANKS[(kseq, u)]
                        for tc in range(u * 4, min(u * 4 + 4, tn)):
                            E.wait("v", ("h2T", ha, tc))
                        kk = KSEQ_OF[(kseq, u)]
                        if kk >= 2:
                            pku = UNIT_LIST[kk - 2]
                            E.wait("v", ("hmul",) + pku)
                            E.wait("s", ("silu",) + pku)
                        if ha == 1 and kseq == NSL and u < 2:
                            E.wait("v", ("fnorm", 0, HT - 2 + u))
                        h2v = h2T.ap().rearrange("p (c i) -> p c i", c=KC)
                        for hh in range(2):
                            dst = pgate[:, hh * 512:(hh + 1) * 512]
                            for c in range(KC):
                                t.matmul(dst,
                                         wg_sl[kseq % 2][:, c * SL + hh * P:
                                                         c * SL + (hh + 1) * P],
                                         h2v[:, c, u * 512:(u + 1) * 512],
                                         start=(c == 0), stop=(c == KC - 1))
                        for hh in range(2):
                            if len(pups) == 1:
                                udst = pups[0][:, hh * 512:(hh + 1) * 512]
                            else:
                                udst = pups[hh]
                            for c in range(KC):
                                ins = t.matmul(
                                    udst,
                                    wu_sl[kseq % 2][:, c * SL + hh * P:
                                                    c * SL + (hh + 1) * P],
                                    h2v[:, c, u * 512:(u + 1) * 512],
                                    start=(c == 0), stop=(c == KC - 1))
                        E.inc(ins, "pe", ("gup", kseq, u))
                for tt in range(tn):
                    pdn = pQ if tt % 2 == 0 else pK
                    lastk = ha * NSL + NSL - 1
                    E.wait("v", ("hmul", lastk, tt // 4))
                    E.wait("s", ("silu", lastk, tt // 4))
                    if tt == 0:
                        for c in range(HC):
                            E.wait("p", ("wdc", c))
                    if tt >= 2:
                        E.wait("v", ("fnorm", ha, tt - 2))
                    hstv = hst.rearrange("p (c i) -> p c i", i=HLEN)
                    for n4 in range(2):
                        for c in range(HC):
                            ins = t.matmul(
                                pdn[:, n4 * 512:(n4 + 1) * 512],
                                hstv[:, c, tt * P:(tt + 1) * P],
                                wd_bf[c][:, n4 * 512:(n4 + 1) * 512],
                                start=(c == 0), stop=(c == HC - 1))
                    E.inc(ins, "pe", ("down", ha, tt))

        run_bodies([
            (block.sync, body_sync, None),
            (block.vector, body_vector, "v"),
            (block.scalar, body_scalar, "s"),
            (block.gpsimd, body_pool, "p"),
            (block.tensor, body_tensor, None),
        ], sems)
    return nc




def _consts():
    maskT = np.triu(np.ones((P, P), dtype=np.float32))
    return {
        "maskT4": np.ascontiguousarray(np.tile(maskT, (1, 4))),
        "ident": np.eye(P, dtype=BF),
        "onescol": np.ones((P, 1), dtype=BF),
        "onesrow": np.ones((1, P), dtype=BF),
        "onesf": np.ones((1, P), dtype=np.float32),
    }


def kernel(L0_attn_x, L0_attn_Wqkv, L0_attn_bqkv, L0_attn_Wout, L0_attn_bout,
           L0_attn_gamma, L0_attn_beta, L0_ffn_w_gate, L0_ffn_w_up,
           L0_ffn_w_down):
    x = np.ascontiguousarray(np.asarray(L0_attn_x, dtype=np.float32)
                             ).reshape(2 * SEQ_LEN, D)
    wqkvT = np.ascontiguousarray(np.asarray(L0_attn_Wqkv, dtype=np.float32).T)
    woutT = np.ascontiguousarray(np.asarray(L0_attn_Wout, dtype=np.float32).T)
    bqkv = np.ascontiguousarray(
        np.asarray(L0_attn_bqkv, dtype=np.float32).reshape(1, 3 * D))
    bout = np.ascontiguousarray(
        np.asarray(L0_attn_bout, dtype=np.float32).reshape(1, D))
    gamma = np.ascontiguousarray(
        np.asarray(L0_attn_gamma, dtype=np.float32).reshape(1, D))
    beta = np.ascontiguousarray(
        np.asarray(L0_attn_beta, dtype=np.float32).reshape(1, D))
    wgT = np.ascontiguousarray(np.asarray(L0_ffn_w_gate, dtype=np.int32).T)
    wuT = np.ascontiguousarray(np.asarray(L0_ffn_w_up, dtype=np.int32).T)
    wdT = np.ascontiguousarray(np.asarray(L0_ffn_w_down, dtype=np.int32).T)
    consts = _consts()

    NT = T_MAIN * TOK
    xs = [np.ascontiguousarray(x[c * NT:(c + 1) * NT]) for c in range(N_CORES)]

    nc_a = build_state(T_MAIN)
    in_a = [{"x_own": xs[c], "wqkvT": wqkvT, "bqkv": bqkv,
             "ident": consts["ident"], "onescol": consts["onescol"],
             "onesrow": consts["onesrow"]} for c in range(N_CORES)]
    res_a = run_bass_kernel_spmd(nc_a, in_a, core_ids=list(range(N_CORES)))
    stots = [np.asarray(res_a.results[c]["stot"]) for c in range(N_CORES)]
    stall = np.ascontiguousarray(np.concatenate(stots, axis=1))

    masks = []
    for c in range(N_CORES):
        m = np.zeros((P, 8), dtype=np.float32)
        for r in range(N_CORES):
            if r // 4 == c // 4 and r < c:
                m[:, r] = 1.0
        masks.append(m)

    nc_b = build_main(T_MAIN)
    in_b = [{"x_main": xs[c], "wqkvT": wqkvT, "woutT": woutT, "bqkv": bqkv,
             "bout": bout, "gamma": gamma, "beta": beta, "wgT": wgT,
             "wuT": wuT, "wdT": wdT, "stot_all": stall, "mask_bc": masks[c],
             **consts} for c in range(N_CORES)]
    res_b = run_bass_kernel_spmd(nc_b, in_b, core_ids=list(range(N_CORES)))
    outs = [np.asarray(res_b.results[c]["out"]) for c in range(N_CORES)]
    return np.concatenate(outs, axis=0).reshape(2, SEQ_LEN, D)


# version 09
# speedup vs baseline: 1.0201x; 1.0201x over previous
from contextlib import ExitStack, contextmanager

import numpy as np
import ml_dtypes
import concourse.bass as bass
import concourse.mybir as mybir
from concourse.bass_utils import run_bass_kernel_spmd

F32 = mybir.dt.float32
BF16 = mybir.dt.bfloat16
I32 = mybir.dt.int32
AF = mybir.ActivationFunctionType
ALU = mybir.AluOpType

D = 1024
HID = 3072
NPAIR = 8
EH = 64
P = 128
KC = D // P
HC = HID // P
LN_EPS = 1e-5
FEAT_EPS = 1e-4
DEN_EPS = 8.0 * FEAT_EPS
WSCALE = 1.0 / 127.0

N_CORES = 8
SEQ_LEN = 8192
T_MAIN = 16
TOK = P

BF = ml_dtypes.bfloat16




class _MockInstr:
    def then_inc(self, *a, **k):
        return self

    def __getattr__(self, name):
        return lambda *a, **k: self


class _MockEng:
    def __getattr__(self, name):
        return lambda *a, **k: _MockInstr()


_EDGE_OPS = frozenset((
    "activation", "add", "mul", "copy", "square", "sqrt", "sign", "memzero",
    "memset", "tensor_copy", "tensor_tensor", "tensor_add", "tensor_sub",
    "tensor_mul", "tensor_max", "tensor_scalar", "tensor_single_scalar",
    "tensor_scalar_add", "tensor_scalar_sub", "tensor_scalar_mul",
    "tensor_scalar_max", "tensor_scalar_min", "tensor_relu", "tensor_reduce",
    "reduce_max", "reduce_sum", "scalar_tensor_tensor", "tensor_tensor_scan",
    "bn_stats", "bn_aggr", "reciprocal", "select", "copy_predicated",
    "iota", "pool", "pool_avg", "pool_max", "stream_shuffle", "engine_nop",
))


class _Edged:

    def __init__(self, env, eng, sem_name):
        self._env = env
        self._eng = eng
        self._sem = sem_name

    def __getattr__(self, name):
        env = self._env
        if name in _EDGE_OPS:
            def f(*a, **k):
                c = env.counts.get(self._sem, 0)
                if env.emit:
                    if c > 0:
                        self._eng.wait_ge(env.sems[self._sem], c)
                    instr = getattr(self._eng, name)(*a, **k)
                    instr.then_inc(env.sems[self._sem], 1)
                else:
                    instr = _MockInstr()
                env.counts[self._sem] = c + 1
                return instr
            return f
        if not env.emit:
            return lambda *a, **k: _MockInstr()
        return getattr(self._eng, name)


class Env:

    AUTO = frozenset(("v", "s", "p"))

    def __init__(self, plan, emit, eng, sems, auto_sem=None):
        self.plan = plan
        self.counts = {}
        self.emit = emit
        self.sems = sems
        self.raw_eng = eng
        if auto_sem is not None:
            self.eng = _Edged(self, eng, auto_sem)
        else:
            self.eng = eng if emit else _MockEng()

    def _bump(self, sem, key, n):
        c = self.counts.get(sem, 0) + n
        self.counts[sem] = c
        if key is not None:
            self._mark(sem, key, c)
        return c

    def _mark(self, sem, key, c):
        if not self.emit:
            assert (sem, key) not in self.plan, ("dup mark", sem, key)
            self.plan[(sem, key)] = c
        else:
            assert self.plan.get((sem, key)) == c, (
                "plan/emit drift", sem, key, self.plan.get((sem, key)), c)

    def inc(self, instr, sem, key=None, n=1):
        if sem in self.AUTO:
            if key is not None:
                self._mark(sem, key, self.counts.get(sem, 0))
            return instr
        self._bump(sem, key, n)
        if self.emit:
            instr.then_inc(self.sems[sem], n)
        return instr

    def wait(self, sem, key):
        if self.emit:
            eng = self.raw_eng if self.raw_eng is not None else None
            eng.wait_ge(self.sems[sem], self.plan[(sem, key)])

    def selfwait(self, sem):
        if self.emit:
            self.raw_eng.wait_ge(self.sems[sem], self.counts[sem])

    def wait_val(self, sem, val):
        if self.emit:
            self.raw_eng.wait_ge(self.sems[sem], val)

    @contextmanager
    def gated(self, reg, j):
        if not self.emit:
            yield
            return
        before = dict(self.counts)
        with self.raw_eng.If_cmp(reg, j, "IS_GT"):
            yield
        with self.raw_eng.Else():
            for sem, c in self.counts.items():
                d = c - before.get(sem, 0)
                if d:
                    self.raw_eng.sem_inc(self.sems[sem], d)


def run_bodies(bodies, sems):
    plan = {}
    for item in bodies:
        attach, body, auto = item
        body(Env(plan, False, None, sems, auto))
    for item in bodies:
        attach, body, auto = item
        attach(lambda eng, body=body, auto=auto: body(
            Env(plan, True, eng, sems, auto)))
    return plan




def emit_ln_stats(E, v, stats, mv, src_lo, src_hi, key):
    v.bn_stats(out=stats[:, 0:6], in_=src_lo)
    E.inc(v.bn_stats(out=stats[:, 6:12], in_=src_hi), "v")
    E.selfwait("v")
    E.inc(v.bn_aggr(out=mv[:], in_=stats[:, 0:12]), "v", key)
    E.selfwait("v")


def emit_rstd(E, s, mv, rstd, eps_ap):
    s.activation(out=rstd[:], in_=mv[:, 1:2], func=AF.Ln, bias=eps_ap)
    return s.activation(out=rstd[:], in_=rstd[:], func=AF.Exp, scale=-0.5)


def sadd_views(dst, src, r, blk_src):
    raise NotImplementedError




def build_state(T):
    nc = bass.Bass("TRN2", target_bir_lowering=False)
    NT = T * TOK
    x_in = nc.dram_tensor("x_own", [NT, D], F32, kind="ExternalInput").ap()
    wqkvT = nc.dram_tensor("wqkvT", [D, 3 * D], F32, kind="ExternalInput").ap()
    bqkv = nc.dram_tensor("bqkv", [1, 3 * D], F32, kind="ExternalInput").ap()
    ident_in = nc.dram_tensor("ident", [P, P], BF16, kind="ExternalInput").ap()
    onescol_in = nc.dram_tensor("onescol", [P, 1], BF16, kind="ExternalInput").ap()
    onesrow_in = nc.dram_tensor("onesrow", [1, P], BF16, kind="ExternalInput").ap()
    stot_out = nc.dram_tensor("stot", [P, 528], F32, kind="ExternalOutput").ap()

    with ExitStack() as ctx:
        ident = ctx.enter_context(nc.sbuf_tensor("identS", [P, P], BF16))
        onescol = ctx.enter_context(nc.sbuf_tensor("onescolS", [P, 1], BF16))
        onesrow = ctx.enter_context(nc.sbuf_tensor("onesrowS", [1, P], BF16))
        brow = ctx.enter_context(nc.sbuf_tensor("brow", [1, 3 * D], BF16))
        wkv_bf = ctx.enter_context(nc.sbuf_tensor("wkv_bf", [P, KC * 2 * D], BF16))
        wld_t = ctx.enter_context(nc.sbuf_tensor("wld", [P, 2 * 3 * D], F32))
        xt_t = ctx.enter_context(nc.sbuf_tensor("xt", [P, 2 * D], F32))
        h1_tm = ctx.enter_context(nc.sbuf_tensor("h1_tm", [P, D], BF16))
        h1T = ctx.enter_context(nc.sbuf_tensor("h1T", [P, D], BF16))
        phik_tm = ctx.enter_context(nc.sbuf_tensor("phik_tm", [P, D], BF16))
        v_tm = ctx.enter_context(nc.sbuf_tensor("v_tm", [P, D], BF16))
        t1 = ctx.enter_context(nc.sbuf_tensor("t1", [P, D], F32))
        t2 = ctx.enter_context(nc.sbuf_tensor("t2", [P, D], F32))
        t3 = ctx.enter_context(nc.sbuf_tensor("t3", [P, D], F32))
        stats = ctx.enter_context(nc.sbuf_tensor("stats", [P, 12], F32))
        mv = ctx.enter_context(nc.sbuf_tensor("mv", [P, 2], F32))
        rstd = ctx.enter_context(nc.sbuf_tensor("rstd", [P, 1], F32))
        c_eps = ctx.enter_context(nc.sbuf_tensor("c_eps", [P, 1], F32))
        sacc = ctx.enter_context(nc.sbuf_tensor("sacc", [P, 528], F32))
        pT = ctx.enter_context(nc.psum_tensor("pT", [P, 1024], BF16))
        pK = ctx.enter_context(nc.psum_tensor("pK", [P, 1024], F32))
        pV = ctx.enter_context(nc.psum_tensor("pV", [P, 1024], F32))
        pS = ctx.enter_context(nc.psum_tensor("pS", [P, 512], F32))
        pC = ctx.enter_context(nc.psum_tensor("pC", [P, 512], F32))
        h_x0 = ctx.enter_context(nc.semaphore("m_x0"))
        h_x1 = ctx.enter_context(nc.semaphore("m_x1"))
        h_c = ctx.enter_context(nc.semaphore("m_c"))
        h_ld = ctx.enter_context(nc.semaphore("m_ld"))
        h_pe = ctx.enter_context(nc.semaphore("m_pe"))
        h_v = ctx.enter_context(nc.semaphore("m_v"))
        h_s = ctx.enter_context(nc.semaphore("m_s"))
        h_out = ctx.enter_context(nc.semaphore("m_out"))
        block = ctx.enter_context(nc.Block())

        wld = [wld_t.ap()[:, 0:3 * D], wld_t.ap()[:, 3 * D:6 * D]]
        xt = [xt_t.ap()[:, 0:D], xt_t.ap()[:, D:2 * D]]
        sems = {"x0": h_x0, "x1": h_x1, "c": h_c, "ld": h_ld, "pe": h_pe,
                "v": h_v, "s": h_s, "out": h_out}

        def body_sync(E):
            sy = E.eng
            for name, dst, srcap in (
                ("ident", ident[:], ident_in),
                ("onescol", onescol[:], onescol_in),
                ("onesrow", onesrow[:], onesrow_in),
            ):
                E.inc(sy.dma_start(out=dst, in_=srcap), "c", name, 16)
            E.inc(sy.dma_start(out=wld[0][0:1, :], in_=bqkv), "c", "call", 16)
            for c in range(KC):
                E.wait("v", "bq" if c == 0 else ("wcv", c - 1))
                E.inc(sy.dma_start(out=wld[0],
                                   in_=wqkvT[c * P:(c + 1) * P, :]),
                      "ld", ("w", c), 16)
            for j in range(T):
                if j >= 2:
                    E.wait("pe", ("h1T", j - 2))
                E.inc(sy.dma_start(out=xt[j % 2],
                                   in_=x_in[j * TOK:(j + 1) * TOK, :]),
                      "x%d" % (j % 2), ("x", j), 16)
            E.wait_val("out", 16)

        def body_vector(E):
            v = E.eng
            E.wait("c", "call")
            E.inc(v.tensor_copy(out=brow[:], in_=wld[0][0:1, :]), "v", "bq")
            for c in range(KC):
                E.wait("ld", ("w", c))
                E.inc(v.tensor_copy(out=wkv_bf[:, c * 2 * D:(c + 1) * 2 * D],
                                    in_=wld[0][:, D:3 * D]),
                      "v", ("wcv", c))
            E.inc(v.memset(sacc[:], 0.0), "v", "sz")
            E.inc(v.memset(c_eps[:], LN_EPS), "v", "ceps")
            for j in range(T):
                E.wait("x%d" % (j % 2), ("x", j))
                emit_ln_stats(E, v, stats, mv, xt[j % 2][:, 0:512],
                              xt[j % 2][:, 512:1024], ("xa", j))
                E.wait("s", ("rstd", j))
                E.inc(v.tensor_scalar(out=h1_tm[:], in0=xt[j % 2],
                                      scalar1=mv[:, 0:1], scalar2=rstd[:],
                                      op0=ALU.subtract, op1=ALU.mult),
                      "v", ("h1", j))
                E.wait("pe", ("h1T", j))
                E.inc(v.tensor_copy(out=h1T[:], in_=pT[:]), "v", ("h1T", j))
                E.wait("s", ("kel", j))
                E.inc(v.scalar_tensor_tensor(out=phik_tm[:], in0=t2[:],
                                             scalar=FEAT_EPS, in1=t3[:],
                                             op0=ALU.add, op1=ALU.add),
                      "v", ("phik", j))
                for r in range(2):
                    E.wait("pe", ("st", j, r))
                    ps = (pS if r == 0 else pC)[:]
                    sview = sacc[:, 0:512]
                    v.tensor_tensor(
                        out=sview[0:64].rearrange("p (g f) -> p g f", f=64
                                                  )[:, 4 * r:4 * r + 4, :],
                        in0=sview[0:64].rearrange("p (g f) -> p g f", f=64
                                                  )[:, 4 * r:4 * r + 4, :],
                        in1=ps[0:64].rearrange("p (g f) -> p g f", f=P
                                               )[:, :, 0:64],
                        op=ALU.add)
                    E.inc(v.tensor_tensor(
                        out=sview[64:128].rearrange("p (g f) -> p g f", f=64
                                                    )[:, 4 * r:4 * r + 4, :],
                        in0=sview[64:128].rearrange("p (g f) -> p g f", f=64
                                                    )[:, 4 * r:4 * r + 4, :],
                        in1=ps[64:128].rearrange("p (g f) -> p g f", f=P
                                                 )[:, :, 64:128],
                        op=ALU.add), "v", ("sadd", j, r))
                E.wait("pe", ("scol", j))
                sc = sacc[:, 512:528]
                v.tensor_tensor(
                    out=sc[0:64].rearrange("p (g o) -> p g o", o=1)[:, 0:16:2, :],
                    in0=sc[0:64].rearrange("p (g o) -> p g o", o=1)[:, 0:16:2, :],
                    in1=pK[0:64, 0:8].rearrange("p (g o) -> p g o", o=1),
                    op=ALU.add)
                E.inc(v.tensor_tensor(
                    out=sc[64:128].rearrange("p (g o) -> p g o", o=1)[:, 1:16:2, :],
                    in0=sc[64:128].rearrange("p (g o) -> p g o", o=1)[:, 1:16:2, :],
                    in1=pK[64:128, 0:8].rearrange("p (g o) -> p g o", o=1),
                    op=ALU.add), "v", ("scadd", j))

        def body_scalar(E):
            s = E.eng
            E.wait("v", "ceps")
            for j in range(T):
                E.wait("v", ("xa", j))
                E.inc(emit_rstd(E, s, mv, rstd, c_eps[:]), "s", ("rstd", j))
                E.wait("pe", ("k", j))
                s.activation(out=t1[:], in_=pK[:], func=AF.Relu, scale=-1.0)
                s.activation(out=t2[:], in_=t1[:], func=AF.Exp, scale=-1.0)
                E.inc(s.activation(out=t3[:], in_=pK[:], func=AF.Relu),
                      "s", ("kel", j))
                E.wait("pe", ("v", j))
                E.inc(s.activation(out=v_tm[:], in_=pV[:], func=AF.Copy),
                      "s", ("v", j))
            E.wait("v", ("scadd", T - 1))
            E.inc(s.dma_start(out=stot_out, in_=sacc[:]), "out", "res", 16)

        def body_tensor(E):
            t = E.eng
            for j in range(T):
                E.wait("v", ("h1", j))
                if j == 0:
                    E.wait("c", "call")
                if j > 0:
                    E.wait("v", ("h1T", j - 1))
                for c in range(KC):
                    ins = t.transpose(pT[:, c * P:(c + 1) * P],
                                      h1_tm[:, c * P:(c + 1) * P], ident[:])
                E.inc(ins, "pe", ("h1T", j))
                E.wait("v", ("h1T", j))
                if j == 0:
                    E.wait("v", ("wcv", KC - 1))
                for half, pdst, name in ((0, pK, "k"), (1, pV, "v")):
                    if j > 0:
                        if half == 0:
                            E.wait("s", ("kel", j - 1))
                            E.wait("v", ("scadd", j - 1))
                        else:
                            E.wait("s", ("v", j - 1))
                            E.wait("pe", ("st", j - 1, 1))
                    for n4 in range(2):
                        for c in range(KC):
                            t.matmul(pdst[:, n4 * 512:(n4 + 1) * 512],
                                     h1T[:, c * P:(c + 1) * P],
                                     wkv_bf[:, c * 2 * D + half * D + n4 * 512:
                                            c * 2 * D + half * D + (n4 + 1) * 512],
                                     start=(c == 0), stop=False)
                        ins = t.matmul(pdst[:, n4 * 512:(n4 + 1) * 512],
                                       onesrow[:],
                                       brow[0:1, D + half * D + n4 * 512:
                                            D + half * D + (n4 + 1) * 512],
                                       start=False, stop=True)
                    E.inc(ins, "pe", (name, j))
                E.wait("v", ("phik", j))
                E.wait("s", ("v", j))
                for r in range(2):
                    ps = pS if r == 0 else pC
                    if j > 0:
                        E.wait("v", ("sadd", j - 1, r))
                    for pp in range(4):
                        pr = r * 4 + pp
                        ins = t.matmul(ps[:, pp * P:(pp + 1) * P],
                                       phik_tm[:, pr * P:(pr + 1) * P],
                                       v_tm[:, pr * P:(pr + 1) * P],
                                       start=True, stop=True)
                    E.inc(ins, "pe", ("st", j, r))
                E.wait("s", ("kel", j))
                if j > 0:
                    E.wait("v", ("scadd", j - 1))
                for pr in range(NPAIR):
                    ins = t.matmul(pK[:, pr:pr + 1],
                                   phik_tm[:, pr * P:(pr + 1) * P],
                                   onescol[:], start=True, stop=True)
                E.inc(ins, "pe", ("scol", j))

        run_bodies([
            (block.sync, body_sync, None),
            (block.vector, body_vector, "v"),
            (block.scalar, body_scalar, "s"),
            (block.tensor, body_tensor, None),
        ], sems)
    return nc




def build_main(T, upto='full', fused=True):
    STAGES = ['setup', 'h1', 'qkv', 'attn', 'wout', 'main', 'full']
    LVL = STAGES.index(upto)
    NPRE = 3 * T if fused else 0
    nc = bass.Bass("TRN2", target_bir_lowering=False)
    NT = T * TOK
    HT = (T + 1) // 2
    HLEN = HT * TOK
    SL = 256
    NSL = HID // SL

    x_in = nc.dram_tensor("x_main", [NT, D], F32, kind="ExternalInput").ap()
    wqkvT_in = nc.dram_tensor("wqkvT", [D, 3 * D], F32, kind="ExternalInput").ap()
    woutT_in = nc.dram_tensor("woutT", [D, D], F32, kind="ExternalInput").ap()
    bqkv_in = nc.dram_tensor("bqkv", [1, 3 * D], F32, kind="ExternalInput").ap()
    bout_in = nc.dram_tensor("bout", [1, D], F32, kind="ExternalInput").ap()
    gamma_in = nc.dram_tensor("gamma", [1, D], F32, kind="ExternalInput").ap()
    beta_in = nc.dram_tensor("beta", [1, D], F32, kind="ExternalInput").ap()
    wgT_in = nc.dram_tensor("wgT", [D, HID], I32, kind="ExternalInput").ap()
    wuT_in = nc.dram_tensor("wuT", [D, HID], I32, kind="ExternalInput").ap()
    wdT_in = nc.dram_tensor("wdT", [HID, D], I32, kind="ExternalInput").ap()
    maskT4_in = nc.dram_tensor("maskT4", [P, 512], F32, kind="ExternalInput").ap()
    ident_in = nc.dram_tensor("ident", [P, P], BF16, kind="ExternalInput").ap()
    onescol_in = nc.dram_tensor("onescol", [P, 1], BF16, kind="ExternalInput").ap()
    onesrow_in = nc.dram_tensor("onesrow", [1, P], BF16, kind="ExternalInput").ap()
    onesf_in = nc.dram_tensor("onesf", [1, P], F32, kind="ExternalInput").ap()
    if fused:
        x_pre = nc.dram_tensor("x_pre", [3 * T * TOK, D], F32,
                               kind="ExternalInput").ap()
        pflag_in = nc.dram_tensor("pflag", [P, 3 * T], F32,
                                  kind="ExternalInput").ap()
    else:
        stot_in = nc.dram_tensor("stot_all", [P, 8 * 528], F32,
                                 kind="ExternalInput").ap()
        maskbc_in = nc.dram_tensor("mask_bc", [P, 8], F32,
                                   kind="ExternalInput").ap()

    out = nc.dram_tensor("out", [NT, D], F32, kind="ExternalOutput").ap()
    x2_hbm = nc.dram_tensor("x2_scratch", [NT, D], F32, kind="Internal").ap()

    with ExitStack() as ctx:
        maskT4 = ctx.enter_context(nc.sbuf_tensor("maskT4S", [P, 512], F32))
        ident = ctx.enter_context(nc.sbuf_tensor("identS", [P, P], BF16))
        onescol = ctx.enter_context(nc.sbuf_tensor("onescolS", [P, 1], BF16))
        onesrow = ctx.enter_context(nc.sbuf_tensor("onesrowS", [1, P], BF16))
        onesf = ctx.enter_context(nc.sbuf_tensor("onesfS", [1, P], F32))
        brow = ctx.enter_context(nc.sbuf_tensor("brow", [1, 3 * D], BF16))
        borow = ctx.enter_context(nc.sbuf_tensor("borow", [1, D], BF16))
        stats = ctx.enter_context(nc.sbuf_tensor("stats", [P, 12], F32))
        mv = ctx.enter_context(nc.sbuf_tensor("mv", [P, 2], F32))
        rstd = ctx.enter_context(nc.sbuf_tensor("rstd", [P, 1], F32))
        c_eps = ctx.enter_context(nc.sbuf_tensor("c_eps", [P, 1], F32))
        pflag = ctx.enter_context(nc.sbuf_tensor("pflagS", [P, 3 * T], F32))
        rden = ctx.enter_context(nc.sbuf_tensor("rden", [P, 16], F32))
        rdpre = ctx.enter_context(nc.sbuf_tensor("rdpre", [P, 16], F32))
        gbc = ctx.enter_context(nc.sbuf_tensor("gbc", [P, D], F32))
        bbc = ctx.enter_context(nc.sbuf_tensor("bbc", [P, D], F32))
        wq_bf = ctx.enter_context(nc.sbuf_tensor("wq_bf", [P, KC * 3 * D], BF16))
        wo_bf = ctx.enter_context(nc.sbuf_tensor("wo_bf", [P, KC * D], BF16))
        stall = ctx.enter_context(nc.sbuf_tensor("stall", [P, 8 * 528], F32))
        mbc = ctx.enter_context(nc.sbuf_tensor("mbc", [P, 8], F32))
        sini = ctx.enter_context(nc.sbuf_tensor("sini", [P, 528], F32))
        srun = ctx.enter_context(nc.sbuf_tensor("srun", [P, 8 * P], F32))
        ssum = ctx.enter_context(nc.sbuf_tensor("ssum", [P, 16], F32))
        srun_bf = ctx.enter_context(nc.sbuf_tensor("srun_bf", [P, 8 * P], BF16))
        ssum_bf = ctx.enter_context(nc.sbuf_tensor("ssum_bf", [P, 16], BF16))
        xt_t = ctx.enter_context(nc.sbuf_tensor("xt", [P, 2 * D], F32))
        h1_tm = ctx.enter_context(nc.sbuf_tensor("h1_tm", [P, D], BF16))
        h1T = ctx.enter_context(nc.sbuf_tensor("h1T", [P, D], BF16))
        phiq = ctx.enter_context(nc.sbuf_tensor("phiq", [P, D], BF16))
        phiq_e = ctx.enter_context(nc.sbuf_tensor("phiq_e", [P, D], BF16))
        phiq_o = ctx.enter_context(nc.sbuf_tensor("phiq_o", [P, D], BF16))
        phik_tm = ctx.enter_context(nc.sbuf_tensor("phik_tm", [P, D], BF16))
        phik_fm = ctx.enter_context(nc.sbuf_tensor("phik_fm", [P, D], BF16))
        v_tm = ctx.enter_context(nc.sbuf_tensor("v_tm", [P, D], BF16))
        am = ctx.enter_context(nc.sbuf_tensor("am", [P, 512], BF16))
        attnT = ctx.enter_context(nc.sbuf_tensor("attnT", [P, D], BF16))
        attn_bf = ctx.enter_context(nc.sbuf_tensor("attn_bf", [P, D], BF16))
        t1 = ctx.enter_context(nc.sbuf_tensor("t1", [P, D], F32))
        t2 = ctx.enter_context(nc.sbuf_tensor("t2", [P, D], F32))
        t3 = ctx.enter_context(nc.sbuf_tensor("t3", [P, D], F32))
        o_n = ctx.enter_context(nc.sbuf_tensor("o_n", [P, D], F32))
        a_g = ctx.enter_context(nc.sbuf_tensor("a_g", [P, D], F32))
        x2sb_t = ctx.enter_context(nc.sbuf_tensor("x2sb", [P, 2 * D], F32))
        h2T = ctx.enter_context(nc.sbuf_tensor("h2T", [P, KC * 1024], BF16))
        wg_sl_t = ctx.enter_context(nc.sbuf_tensor("wg_sl", [P, 2 * KC * SL], BF16))
        wu_sl_t = ctx.enter_context(nc.sbuf_tensor("wu_sl", [P, 2 * KC * SL], BF16))
        silu_t = ctx.enter_context(nc.sbuf_tensor("silu_t", [P, 1024], F32))
        f_n = ctx.enter_context(nc.sbuf_tensor("f_n", [P, D], F32))
        wdx = ctx.enter_context(nc.sbuf_tensor("wdx", [P, 1024], BF16))
        pT = ctx.enter_context(nc.psum_tensor("pT", [P, 1024], BF16))
        pQ = ctx.enter_context(nc.psum_tensor("pQ", [P, 1024], F32))
        pK = ctx.enter_context(nc.psum_tensor("pK", [P, 1024], F32))
        pV = ctx.enter_context(nc.psum_tensor("pV", [P, 1024], F32))
        pD = ctx.enter_context(nc.psum_tensor("pD", [P, 512], F32))
        h_x0 = ctx.enter_context(nc.semaphore("s_x0"))
        h_x1 = ctx.enter_context(nc.semaphore("s_x1"))
        h_c = ctx.enter_context(nc.semaphore("s_c"))
        h_ld = ctx.enter_context(nc.semaphore("s_ld"))
        h_pe = ctx.enter_context(nc.semaphore("s_pe"))
        h_v = ctx.enter_context(nc.semaphore("s_v"))
        h_s = ctx.enter_context(nc.semaphore("s_s"))
        h_p = ctx.enter_context(nc.semaphore("s_p"))
        h_w0 = ctx.enter_context(nc.semaphore("s_w0"))
        h_w1 = ctx.enter_context(nc.semaphore("s_w1"))
        h_o0 = ctx.enter_context(nc.semaphore("s_o0"))
        h_o1 = ctx.enter_context(nc.semaphore("s_o1"))
        h_p0 = ctx.enter_context(nc.semaphore("s_p0"))
        h_p1 = ctx.enter_context(nc.semaphore("s_p1"))
        h_g = ctx.enter_context(nc.semaphore("s_g"))
        block = ctx.enter_context(nc.Block())

        xt = [xt_t.ap()[:, 0:D], xt_t.ap()[:, D:2 * D]]
        x2sb = [x2sb_t.ap()[:, 0:D], x2sb_t.ap()[:, D:2 * D]]
        wg_sl = [wg_sl_t.ap()[:, 0:KC * SL], wg_sl_t.ap()[:, KC * SL:]]
        wu_sl = [wu_sl_t.ap()[:, 0:KC * SL], wu_sl_t.ap()[:, KC * SL:]]
        sems = {"x0": h_x0, "x1": h_x1, "c": h_c, "ld": h_ld, "pe": h_pe,
                "v": h_v, "s": h_s, "p": h_p, "w0": h_w0, "w1": h_w1,
                "o0": h_o0, "o1": h_o1, "px0": h_p0, "px1": h_p1,
                "g": h_g}
        if upto == 'full':
            assert HLEN == 1024 and T == 16, "V1 layout sized for T=16"

        wld = h2T.ap().bitcast(F32)[:, 0:3 * D]
        hst = wq_bf.ap()
        wgu_int = stall.ap().bitcast(I32)[:, 0:KC * SL]
        wuu_int = stall.ap().bitcast(I32)[:, KC * SL:2 * KC * SL]
        wd_int = srun.ap().bitcast(I32)[:, 0:1024]
        wd_bf = []
        for tens in (t1, t2, t3, o_n, a_g, gbc, bbc):
            b = tens.ap().bitcast(BF16)
            wd_bf.append(b[:, 0:1024])
            wd_bf.append(b[:, 1024:2048])
        wd_bf.append(sini.ap().bitcast(BF16)[:, 0:1024])
        wd_bf.append(srun_bf.ap()[:, 0:1024])
        for tens in (phiq, phik_tm, phik_fm, v_tm, wdx, h1T, attnT,
                     attn_bf):
            wd_bf.append(tens.ap())
        assert len(wd_bf) == HC
        x2r = [xt[0], xt[1]]
        outsb = [x2sb[0], x2sb[1]]
        xp = [x2sb[0], x2sb[1]]
        pT32 = pT.ap().bitcast(F32)

        NU = 2
        UNIT_LIST = []
        for ha in range(2):
            for sl in range(NSL):
                for u in range(NU):
                    UNIT_LIST.append((ha * NSL + sl, u))
        KSEQ_OF = {ku: i for i, ku in enumerate(UNIT_LIST)}
        GUP_BANKS = {}
        for i, ku in enumerate(UNIT_LIST):
            if i % 2 == 0:
                GUP_BANKS[ku] = (pQ[:], [pK[:]])
            else:
                GUP_BANKS[ku] = (pV[:], [pT32, pD[:]])

        def halves():
            for ha in range(2):
                yield (ha, 0, HT) if ha == 0 else (ha, HT, T - HT)

        LASTK = [NSL - 1, 2 * NSL - 1]
        H0_UNITS = [ku for ku in UNIT_LIST if ku[0] < NSL]
        H0_LAST_ODD = [ku for ku in H0_UNITS if KSEQ_OF[ku] % 2 == 1][-1]

        def body_sync(E):
            sy = E.eng
            cl = [("maskT4", maskT4[:], maskT4_in),
                  ("ident", ident[:], ident_in),
                  ("onescol", onescol[:], onescol_in),
                  ("onesrow", onesrow[:], onesrow_in),
                  ("onesf", onesf[:], onesf_in)]
            if fused:
                cl += [("pflag", pflag[:], pflag_in)]
            else:
                cl += [("stall", stall[:], stot_in),
                       ("mbc", mbc[:], maskbc_in)]
            for name, dst, srcap in cl:
                E.inc(sy.dma_start(out=dst, in_=srcap), "c", name, 16)
            E.inc(sy.dma_start(out=wld[0:1, :], in_=bqkv_in), "c", "bq", 16)
            E.inc(sy.dma_start(out=t1[0:1, 0:D], in_=bout_in), "c", "bo", 16)
            E.inc(sy.dma_start(out=t2[0:1, 0:D], in_=gamma_in), "c", "ga", 16)
            E.inc(sy.dma_start(out=t3[0:1, 0:D], in_=beta_in), "c", "call", 16)
            for c in range(KC):
                if c == 0:
                    E.wait("v", "rows")
                    E.wait("g", "gbcmm")
                else:
                    E.wait("v", ("wq", c - 1))
                E.inc(sy.dma_start(out=wld[:, :],
                                   in_=wqkvT_in[c * P:(c + 1) * P, :]),
                      "ld", ("wq", c), 16)
            for c in range(KC):
                E.wait("v", ("wq", KC - 1) if c == 0 else ("wo", c - 1))
                E.inc(sy.dma_start(out=wld[:, 0:D],
                                   in_=woutT_in[c * P:(c + 1) * P, :]),
                      "ld", ("wo", c), 16)
            if upto == 'setup':
                return
            for j in range(min(2, T)):
                E.inc(sy.dma_start(out=xt[j % 2],
                                   in_=x_in[j * TOK:(j + 1) * TOK, :]),
                      "x%d" % (j % 2), ("x", j), 16)
            if fused:
                for pj in range(NPRE):
                    if pj >= 2:
                        E.wait("v", ("ph1", pj - 2))
                        E.wait("px%d" % (pj % 2), ("px", pj - 2))
                    E.inc(sy.dma_start(
                        out=xp[pj % 2],
                        in_=x_pre[pj * TOK:(pj + 1) * TOK, :]),
                        "px%d" % (pj % 2), ("px", pj), 16)
            for j in range(2, T):
                if LVL >= 5:
                    E.wait("p", ("x2", j - 2))
                else:
                    E.wait("v", ("h1", j - 2))
                E.inc(sy.dma_start(out=xt[j % 2],
                                   in_=x_in[j * TOK:(j + 1) * TOK, :]),
                      "x%d" % (j % 2), ("x", j), 16)
            if upto != 'full':
                if upto == 'main':
                    for jp in range(max(0, T - 2), T):
                        E.wait("w%d" % (jp % 2), ("x2w", jp))
                return
            buffree = [("p", ("x2", T - 1)), ("p", ("x2", T - 1))]
            for ha, t0, tn in halves():
                for tt in range(tn):
                    tg = t0 + tt
                    E.wait("w%d" % (tg % 2), ("x2w", tg))
                    E.wait(*buffree[tt % 2])
                    E.inc(sy.dma_start(out=x2r[tt % 2],
                                       in_=x2_hbm[tg * TOK:(tg + 1) * TOK, :]),
                          "x%d" % (tt % 2), ("x2a", ha, tt), 16)
                    buffree[tt % 2] = ("v", ("h2", ha, tt))
                for sl in range(NSL):
                    kseq = ha * NSL + sl
                    if kseq == 0:
                        E.wait("v", "sexp")
                        E.wait("v", ("wo", KC - 1))
                    elif kseq == NSL:
                        E.wait("p", ("wdc", HC - 1))
                        E.wait("p", ("wuc", kseq - 1))
                    else:
                        E.wait("p", ("wuc", kseq - 1))
                    if kseq >= 2:
                        E.wait("pe", ("gup", kseq - 2, NU - 1))
                    E.inc(sy.dma_start(
                        out=wgu_int.rearrange("p (c h) -> p c h", c=KC),
                        in_=wgT_in[:, sl * SL:(sl + 1) * SL].rearrange(
                            "(c p) h -> p c h", p=P)), "ld", ("wg", kseq), 16)
                    E.wait("p", ("wgc", kseq))
                    if kseq >= 2:
                        E.wait("pe", ("gup", kseq - 2, NU - 1))
                    E.inc(sy.dma_start(
                        out=wuu_int.rearrange("p (c h) -> p c h", c=KC),
                        in_=wuT_in[:, sl * SL:(sl + 1) * SL].rearrange(
                            "(c p) h -> p c h", p=P)), "ld", ("wu", kseq), 16)
                if ha == 0:
                    for c in range(HC):
                        if c == 0:
                            E.wait("p", ("x2", T - 1))
                            E.wait("v", ("st", T - 1))
                            E.wait("w0", ("x2w", T - 2))
                            E.wait("w1", ("x2w", T - 1))
                            E.wait("p", ("wuc", NSL - 1))
                        else:
                            E.wait("p", ("wdc", c - 1))
                        E.inc(sy.dma_start(out=wd_int,
                                           in_=wdT_in[c * P:(c + 1) * P, :]),
                              "ld", ("wd", c), 16)
                for tt in range(tn):
                    tg = t0 + tt
                    E.wait(*buffree[tt % 2])
                    E.inc(sy.dma_start(out=x2r[tt % 2],
                                       in_=x2_hbm[tg * TOK:(tg + 1) * TOK, :]),
                          "x%d" % (tt % 2), ("x2b", ha, tt), 16)
                    buffree[tt % 2] = ("p", ("res", ha, tt))
            E.wait("o0", ("out", 1, T - HT - 2))
            E.wait("o1", ("out", 1, T - HT - 1))

        def body_vector(E):
            v = E.eng
            E.wait("c", "call")
            v.tensor_copy(out=brow[:], in_=wld[0:1, :])
            E.inc(v.tensor_copy(out=borow[:], in_=t1[0:1, 0:D]), "v", "rows")
            E.wait("g", "gbcmm")
            v.tensor_copy(out=gbc[:], in_=pQ[:])
            E.inc(v.tensor_copy(out=bbc[:], in_=pK[:]), "v", "gbcc")
            for c in range(KC):
                E.wait("ld", ("wq", c))
                E.inc(v.tensor_copy(out=wq_bf[:, c * 3 * D:(c + 1) * 3 * D],
                                    in_=wld[:, :]), "v", ("wq", c))
            for c in range(KC):
                E.wait("ld", ("wo", c))
                E.inc(v.tensor_copy(out=wo_bf[:, c * D:(c + 1) * D],
                                    in_=wld[:, 0:D]), "v", ("wo", c))
            E.inc(v.memset(c_eps[:], LN_EPS), "v", "ceps")
            v.memset(phiq_e[:], 0.0)
            E.inc(v.memset(phiq_o[:], 0.0), "v", "phiqz0")
            if fused:
                v.memset(h1_tm[:], 0.0)
                v.memset(h1T[:], 0.0)
                v.memset(phik_tm[:], 0.0)
                E.inc(v.memset(v_tm[:], 0.0), "v", "pz")
            if fused:
                v.memset(srun[:], 0.0)
                E.inc(v.memset(ssum[:], 0.0), "v", "sexp")
            else:
                v.tensor_scalar(out=sini[:], in0=stall[:, 0:528],
                                scalar1=mbc[:, 0:1], scalar2=None,
                                op0=ALU.mult)
                for r in range(1, 8):
                    v.tensor_scalar(out=t1[:, 0:528],
                                    in0=stall[:, r * 528:(r + 1) * 528],
                                    scalar1=mbc[:, r:r + 1], scalar2=None,
                                    op0=ALU.mult)
                    v.tensor_tensor(out=sini[:], in0=sini[:],
                                    in1=t1[:, 0:528], op=ALU.add)
                v.memset(srun[:], 0.0)
                v.tensor_copy(
                    out=srun[0:64, :].rearrange("p (g f) -> p g f", f=P
                                                )[:, :, 0:64],
                    in_=sini[0:64, 0:512].rearrange("p (g f) -> p g f", f=64))
                v.tensor_copy(
                    out=srun[64:128, :].rearrange("p (g f) -> p g f", f=P
                                                  )[:, :, 64:128],
                    in_=sini[64:128, 0:512].rearrange("p (g f) -> p g f",
                                                      f=64))
                E.inc(v.tensor_copy(out=ssum[:], in_=sini[:, 512:528]),
                      "v", "sexp")
            if upto == 'setup':
                return
            if fused:
                for pj in range(NPRE):
                    E.wait("px%d" % (pj % 2), ("px", pj))
                    emit_ln_stats(E, v, stats, mv, xp[pj % 2][:, 0:512],
                                  xp[pj % 2][:, 512:1024], ("pxa", pj))
                    E.wait("s", ("prstd", pj))
                    E.inc(v.tensor_scalar(out=h1_tm[:], in0=xp[pj % 2],
                                          scalar1=mv[:, 0:1],
                                          scalar2=rstd[:],
                                          op0=ALU.subtract, op1=ALU.mult),
                          "v", ("ph1", pj))
                    E.wait("pe", ("pxh1T", pj))
                    E.inc(v.tensor_copy(out=h1T[:], in_=pT[:]),
                          "v", ("ph1T", pj))
                    E.wait("s", ("pkel", pj))
                    E.inc(v.scalar_tensor_tensor(
                        out=phik_tm[:], in0=t2[:], scalar=FEAT_EPS,
                        in1=t3[:], op0=ALU.add, op1=ALU.add),
                        "v", ("pphik", pj))
                    for r in range(2):
                        E.wait("pe", ("pxst", pj, r))
                        ps = pV[:, 512:1024]
                        v.scalar_tensor_tensor(
                            out=srun[0:64, :].rearrange(
                                "p (g f) -> p g f", f=P
                            )[:, 4 * r:4 * r + 4, 0:64],
                            in0=ps[0:64].rearrange("p (g f) -> p g f",
                                                   f=P)[:, :, 0:64],
                            scalar=pflag[0:64, pj:pj + 1],
                            in1=srun[0:64, :].rearrange(
                                "p (g f) -> p g f", f=P
                            )[:, 4 * r:4 * r + 4, 0:64],
                            op0=ALU.mult, op1=ALU.add)
                        E.inc(v.scalar_tensor_tensor(
                            out=srun[64:128, :].rearrange(
                                "p (g f) -> p g f", f=P
                            )[:, 4 * r:4 * r + 4, 64:128],
                            in0=ps[64:128].rearrange("p (g f) -> p g f",
                                                     f=P)[:, :, 64:128],
                            scalar=pflag[64:128, pj:pj + 1],
                            in1=srun[64:128, :].rearrange(
                                "p (g f) -> p g f", f=P
                            )[:, 4 * r:4 * r + 4, 64:128],
                            op0=ALU.mult, op1=ALU.add), "v",
                            ("psadd", pj, r))
                    E.wait("pe", ("pxscol", pj))
                    v.scalar_tensor_tensor(
                        out=ssum[0:64, :].rearrange("p (g o) -> p g o",
                                                    o=1)[:, 0:16:2, :],
                        in0=pD[0:64, 32:40].rearrange("p (g o) -> p g o",
                                                      o=1),
                        scalar=pflag[0:64, pj:pj + 1],
                        in1=ssum[0:64, :].rearrange("p (g o) -> p g o",
                                                    o=1)[:, 0:16:2, :],
                        op0=ALU.mult, op1=ALU.add)
                    E.inc(v.scalar_tensor_tensor(
                        out=ssum[64:128, :].rearrange("p (g o) -> p g o",
                                                      o=1)[:, 1:16:2, :],
                        in0=pD[64:128, 32:40].rearrange(
                            "p (g o) -> p g o", o=1),
                        scalar=pflag[64:128, pj:pj + 1],
                        in1=ssum[64:128, :].rearrange("p (g o) -> p g o",
                                                      o=1)[:, 1:16:2, :],
                        op0=ALU.mult, op1=ALU.add), "v", ("pst", pj))
                E._mark("v", "preend", E.counts.get("v", 0))
            for j in range(T):
                E.wait("x%d" % (j % 2), ("x", j))
                if j > 0 and LVL >= 5:
                    E.wait("p", ("anorm", j - 1))
                emit_ln_stats(E, v, stats, mv, xt[j % 2][:, 0:512],
                              xt[j % 2][:, 512:1024], ("xa", j))
                E.wait("s", ("xrstd", j))
                E.inc(v.tensor_scalar(out=h1_tm[:], in0=xt[j % 2],
                                      scalar1=mv[:, 0:1], scalar2=rstd[:],
                                      op0=ALU.subtract, op1=ALU.mult),
                      "v", ("h1", j))
                E.wait("pe", ("h1T", j))
                E.inc(v.tensor_copy(out=h1T[:], in_=pT[:]), "v", ("h1T", j))
                if LVL < 2:
                    continue
                if LVL >= 3:
                    if j > 0:
                        E.wait("pe", ("scol", j - 1))
                    v.tensor_copy(out=srun_bf[:], in_=srun[:])
                    E.inc(v.tensor_copy(out=ssum_bf[:], in_=ssum[:]),
                          "v", ("scast", j))
                E.wait("s", ("qel", j))
                E.inc(v.scalar_tensor_tensor(out=phiq[:], in0=t2[:],
                                             scalar=FEAT_EPS, in1=t3[:],
                                             op0=ALU.add, op1=ALU.add),
                      "v", ("phiq", j))
                if LVL >= 3:
                    if j > 0:
                        E.wait("pe", ("at", j - 1, 3))
                    v.tensor_copy(out=phiq_e[0:64, :], in_=phiq[0:64, :])
                    E.inc(v.tensor_copy(out=phiq_o[64:128, :],
                                        in_=phiq[64:128, :]),
                          "v", ("phiqz", j))
                E.wait("s", ("kel", j))
                E.inc(v.scalar_tensor_tensor(out=phik_tm[:], in0=t1[:],
                                             scalar=FEAT_EPS, in1=t3[:],
                                             op0=ALU.add, op1=ALU.add),
                      "v", ("phik", j))
                E.wait("pe", ("kfm", j))
                E.inc(v.tensor_copy(out=phik_fm[:], in_=pT[:]),
                      "v", ("kfm", j))
                if LVL < 3:
                    continue
                if j == 0:
                    E.wait("c", "call")
                for g in range(4):
                    E.wait("pe", ("at", j, g))
                    if g > 0:
                        E.wait("pe", ("avg", j, g - 1))
                    E.inc(v.tensor_tensor(out=am[:], in0=pV[:, 0:512],
                                          in1=maskT4[:], op=ALU.mult),
                          "v", ("mask", j, g))
                E.wait("pe", ("avg", j, 3))
                if j > 0:
                    E.wait("s", ("div", j - 1))
                E.inc(v.tensor_scalar(out=rdpre[:], in0=pD[:, 0:16],
                                      scalar1=DEN_EPS, scalar2=None,
                                      op0=ALU.add), "v")
                E.selfwait("v")
                E.inc(v.reciprocal(out=rden[:], in_=rdpre[:]),
                      "v", ("rden", j))
                E.selfwait("v")
                for h in range(8, 16):
                    ins = v.tensor_scalar(
                        out=attn_bf[:, h * EH:(h + 1) * EH],
                        in0=pK[:, h * EH:(h + 1) * EH],
                        scalar1=rden[:, h:h + 1], scalar2=None, op0=ALU.mult)
                E.inc(ins, "v", ("div", j))
                for r in range(2):
                    E.wait("pe", ("st", j, r))
                    ps = pV[:, 512:1024]
                    v.tensor_tensor(
                        out=srun[0:64, :].rearrange("p (g f) -> p g f", f=P
                                                    )[:, 4 * r:4 * r + 4, 0:64],
                        in0=srun[0:64, :].rearrange("p (g f) -> p g f", f=P
                                                    )[:, 4 * r:4 * r + 4, 0:64],
                        in1=ps[0:64].rearrange("p (g f) -> p g f", f=P
                                               )[:, :, 0:64],
                        op=ALU.add)
                    E.inc(v.tensor_tensor(
                        out=srun[64:128, :].rearrange("p (g f) -> p g f", f=P
                                                      )[:, 4 * r:4 * r + 4,
                                                        64:128],
                        in0=srun[64:128, :].rearrange("p (g f) -> p g f", f=P
                                                      )[:, 4 * r:4 * r + 4,
                                                        64:128],
                        in1=ps[64:128].rearrange("p (g f) -> p g f", f=P
                                                 )[:, :, 64:128],
                        op=ALU.add), "v", ("sadd", j, r))
                E.wait("pe", ("scol", j))
                v.tensor_tensor(
                    out=ssum[0:64, :].rearrange("p (g o) -> p g o", o=1
                                                )[:, 0:16:2, :],
                    in0=ssum[0:64, :].rearrange("p (g o) -> p g o", o=1
                                                )[:, 0:16:2, :],
                    in1=pD[0:64, 32:40].rearrange("p (g o) -> p g o", o=1),
                    op=ALU.add)
                E.inc(v.tensor_tensor(
                    out=ssum[64:128, :].rearrange("p (g o) -> p g o", o=1
                                                  )[:, 1:16:2, :],
                    in0=ssum[64:128, :].rearrange("p (g o) -> p g o", o=1
                                                  )[:, 1:16:2, :],
                    in1=pD[64:128, 32:40].rearrange("p (g o) -> p g o", o=1),
                    op=ALU.add), "v", ("st", j))
                if LVL < 4:
                    continue
                E.wait("pe", ("attnT", j))
                E.inc(v.tensor_copy(out=attnT[:], in_=pT[:]),
                      "v", ("attnT", j))
                E.wait("pe", ("wout", j))
                emit_ln_stats(E, v, stats, mv, pQ[:, 0:512], pQ[:, 512:1024],
                              ("oa", j))
                E.wait("s", ("orstd", j))
                if j > 0:
                    E.wait("p", ("x2", j - 1))
                E.inc(v.tensor_scalar(out=o_n[:], in0=pQ[:],
                                      scalar1=mv[:, 0:1], scalar2=rstd[:],
                                      op0=ALU.subtract, op1=ALU.mult),
                      "v", ("onorm", j))
                if LVL < 5:
                    continue
                E.wait("p", ("gb", j))
                emit_ln_stats(E, v, stats, mv, a_g[:, 0:512], a_g[:, 512:1024],
                              ("aa", j))
            if upto != 'full':
                return
            for ha, t0, tn in halves():
                for tt in range(tn):
                    E.wait("x%d" % (tt % 2), ("x2a", ha, tt))
                    emit_ln_stats(E, v, stats, mv, x2r[tt % 2][:, 0:512],
                                  x2r[tt % 2][:, 512:1024], ("h2a", ha, tt))
                    E.wait("s", ("h2rstd", ha, tt))
                    E.inc(v.tensor_scalar(out=h1_tm[:], in0=x2r[tt % 2],
                                          scalar1=mv[:, 0:1], scalar2=rstd[:],
                                          op0=ALU.subtract, op1=ALU.mult),
                          "v", ("h2", ha, tt))
                    E.wait("pe", ("h2T", ha, tt))
                    E.inc(v.tensor_copy(
                        out=h2T.ap().rearrange("p (c i) -> p c i", c=KC
                                               )[:, :, tt * P:(tt + 1) * P],
                        in_=pT.ap().rearrange("p (c i) -> p c i", c=KC)),
                        "v", ("h2T", ha, tt))
                for sl in range(NSL):
                    kseq = ha * NSL + sl
                    for u in range(NU):
                        pgate, pups = GUP_BANKS[(kseq, u)]
                        E.wait("s", ("silu", kseq, u))
                        for hh in range(2):
                            if len(pups) == 1:
                                upsrc = pups[0][:, hh * 512:(hh + 1) * 512]
                            else:
                                upsrc = pups[hh]
                            ins = v.tensor_tensor(
                                out=hst.rearrange("p (c i) -> p c i", i=HLEN
                                                  )[:, 2 * sl + hh,
                                                    u * 512:(u + 1) * 512],
                                in0=silu_t[:, hh * 512:(hh + 1) * 512],
                                in1=upsrc, op=ALU.mult)
                        E.inc(ins, "v", ("hmul", kseq, u))
                for tt in range(tn):
                    pdn = pQ if tt % 2 == 0 else pK
                    E.wait("pe", ("down", ha, tt))
                    emit_ln_stats(E, v, stats, mv, pdn[:, 0:512],
                                  pdn[:, 512:1024], ("fa", ha, tt))
                    E.wait("s", ("frstd", ha, tt))
                    E.inc(v.tensor_scalar(out=f_n[:], in0=pdn[:],
                                          scalar1=mv[:, 0:1], scalar2=rstd[:],
                                          op0=ALU.subtract, op1=ALU.mult),
                          "v", ("fnorm", ha, tt))

        def body_scalar(E):
            s = E.eng
            E.wait("v", "ceps")
            if upto == 'setup':
                return
            if fused:
                for pj in range(NPRE):
                    E.wait("v", ("pxa", pj))
                    E.inc(emit_rstd(E, s, mv, rstd, c_eps[:]),
                          "s", ("prstd", pj))
                    E.wait("pe", ("pxk", pj))
                    if pj > 0:
                        E.wait("v", ("pphik", pj - 1))
                    s.activation(out=t1[:], in_=pK[:], func=AF.Relu,
                                 scale=-1.0)
                    s.activation(out=t2[:], in_=t1[:], func=AF.Exp,
                                 scale=-1.0)
                    E.inc(s.activation(out=t3[:], in_=pK[:],
                                       func=AF.Relu), "s", ("pkel", pj))
                    E.wait("pe", ("pxv", pj))
                    if pj > 0:
                        E.wait("pe", ("pxst", pj - 1, 1))
                    E.inc(s.activation(out=v_tm[:], in_=pV[:],
                                       func=AF.Copy), "s", ("pv", pj))
                E._mark("s", "preend", E.counts.get("s", 0))
            for j in range(T):
                E.wait("v", ("xa", j))
                if j > 0 and LVL >= 5:
                    E.wait("p", ("anorm", j - 1))
                E.inc(emit_rstd(E, s, mv, rstd, c_eps[:]), "s", ("xrstd", j))
                if LVL < 2:
                    continue
                E.wait("pe", ("q", j))
                if j > 0:
                    E.wait("v", ("phik", j - 1))
                elif fused:
                    E.wait("v", "preend")
                s.activation(out=t1[:], in_=pQ[:], func=AF.Relu, scale=-1.0)
                s.activation(out=t2[:], in_=t1[:], func=AF.Exp, scale=-1.0)
                E.inc(s.activation(out=t3[:], in_=pQ[:], func=AF.Relu),
                      "s", ("qel", j))
                E.wait("pe", ("k", j))
                E.wait("v", ("phiq", j))
                s.activation(out=t2[:], in_=pK[:], func=AF.Relu, scale=-1.0)
                s.activation(out=t1[:], in_=t2[:], func=AF.Exp, scale=-1.0)
                E.inc(s.activation(out=t3[:], in_=pK[:], func=AF.Relu),
                      "s", ("kel", j))
                E.wait("pe", ("v", j))
                E.inc(s.activation(out=v_tm[:], in_=pV[:], func=AF.Copy),
                      "s", ("v", j))
                if LVL < 3:
                    continue
                E.wait("v", ("rden", j))
                for h in range(8):
                    ins = s.activation(out=attn_bf[:, h * EH:(h + 1) * EH],
                                       in_=pK[:, h * EH:(h + 1) * EH],
                                       func=AF.Copy, scale=rden[:, h:h + 1])
                E.inc(ins, "s", ("div", j))
                if LVL < 4:
                    continue
                E.wait("v", ("oa", j))
                E.inc(emit_rstd(E, s, mv, rstd, c_eps[:]), "s", ("orstd", j))
                if LVL < 5:
                    continue
                E.wait("v", ("aa", j))
                E.inc(emit_rstd(E, s, mv, rstd, c_eps[:]), "s", ("arstd", j))
                E.wait("p", ("x2", j))
                E.inc(s.dma_start(out=x2_hbm[j * TOK:(j + 1) * TOK, :],
                                  in_=x2sb[j % 2]), "w%d" % (j % 2),
                      ("x2w", j), 16)
            if upto != 'full':
                return
            for ha, t0, tn in halves():
                for tt in range(tn):
                    E.wait("v", ("h2a", ha, tt))
                    E.inc(emit_rstd(E, s, mv, rstd, c_eps[:]),
                          "s", ("h2rstd", ha, tt))
                for sl in range(NSL):
                    kseq = ha * NSL + sl
                    for u in range(NU):
                        pgate, _pups = GUP_BANKS[(kseq, u)]
                        E.wait("pe", ("gup", kseq, u))
                        kk = KSEQ_OF[(kseq, u)]
                        if kk >= 1:
                            E.wait("v", ("hmul",) + UNIT_LIST[kk - 1])
                        E.inc(s.activation(out=silu_t[:], in_=pgate,
                                           func=AF.Silu),
                              "s", ("silu", kseq, u))
                for tt in range(tn):
                    E.wait("v", ("fa", ha, tt))
                    E.inc(emit_rstd(E, s, mv, rstd, c_eps[:]),
                          "s", ("frstd", ha, tt))
                    E.wait("p", ("res", ha, tt))
                    E.inc(s.dma_start(
                        out=out[(t0 + tt) * TOK:(t0 + tt + 1) * TOK, :],
                        in_=outsb[tt % 2]), "o%d" % (tt % 2),
                      ("out", ha, tt), 16)

        def body_pool(E):
            po = E.eng
            if LVL < 5:
                return
            for j in range(T):
                E.wait("v", ("onorm", j))
                if j == 0:
                    E.wait("v", "gbcc")
                po.tensor_tensor(out=a_g[:], in0=o_n[:], in1=gbc[:],
                                 op=ALU.mult)
                E.inc(po.tensor_tensor(out=a_g[:], in0=a_g[:], in1=bbc[:],
                                       op=ALU.add), "p", ("gb", j))
                E.wait("s", ("arstd", j))
                E.inc(po.tensor_scalar(out=o_n[:], in0=a_g[:],
                                       scalar1=mv[:, 0:1], scalar2=rstd[:],
                                       op0=ALU.subtract, op1=ALU.mult),
                      "p", ("anorm", j))
                if j >= 2:
                    E.wait("w%d" % (j % 2), ("x2w", j - 2))
                elif fused:
                    E.wait("px%d" % (j % 2), ("px", NPRE - 2 + j))
                E.inc(po.tensor_tensor(out=x2sb[j % 2], in0=xt[j % 2],
                                       in1=o_n[:], op=ALU.add),
                      "p", ("x2", j))
            if upto != 'full':
                return
            for ha, t0, tn in halves():
                for sl in range(NSL):
                    kseq = ha * NSL + sl
                    E.wait("ld", ("wg", kseq))
                    E.inc(po.tensor_scalar(out=wg_sl[kseq % 2], in0=wgu_int,
                                           scalar1=WSCALE, scalar2=None,
                                           op0=ALU.mult), "p", ("wgc", kseq))
                    E.wait("ld", ("wu", kseq))
                    E.inc(po.tensor_scalar(out=wu_sl[kseq % 2], in0=wuu_int,
                                           scalar1=WSCALE, scalar2=None,
                                           op0=ALU.mult), "p", ("wuc", kseq))
                if ha == 0:
                    for c in range(HC):
                        E.wait("ld", ("wd", c))
                        E.inc(po.tensor_scalar(out=wd_bf[c], in0=wd_int,
                                               scalar1=WSCALE, scalar2=None,
                                               op0=ALU.mult),
                              "p", ("wdc", c))
                for tt in range(tn):
                    E.wait("v", ("fnorm", ha, tt))
                    E.wait("x%d" % (tt % 2), ("x2b", ha, tt))
                    if (ha, tt) == (0, 0):
                        E.wait("w1", ("x2w", T - 1))
                        E.wait("w0", ("x2w", T - 2))
                    if tt >= 2:
                        E.wait("o%d" % (tt % 2), ("out", ha, tt - 2))
                    elif ha == 1:
                        E.wait("o%d" % (tt % 2), ("out", 0, HT - 2 + tt))
                    E.inc(po.tensor_tensor(out=outsb[tt % 2],
                                           in0=x2r[tt % 2], in1=f_n[:],
                                           op=ALU.add), "p", ("res", ha, tt))

        def body_tensor(E):
            t = E.eng
            E.wait("c", "call")
            for n4 in range(2):
                t.matmul(pQ[:, n4 * 512:(n4 + 1) * 512], onesf[:],
                         t2[0:1, n4 * 512:(n4 + 1) * 512],
                         start=True, stop=True)
            for n4 in range(2):
                ins = t.matmul(pK[:, n4 * 512:(n4 + 1) * 512], onesf[:],
                               t3[0:1, n4 * 512:(n4 + 1) * 512],
                               start=True, stop=True)
            E.inc(ins, "g", "gbcmm")
            if upto == 'setup':
                return
            if fused:
                for pj in range(NPRE):
                    E.wait("v", ("ph1", pj))
                    if pj == 0:
                        E.wait("c", "call")
                    else:
                        E.wait("v", ("ph1T", pj - 1))
                    for c in range(KC):
                        ins = t.transpose(pT[:, c * P:(c + 1) * P],
                                          h1_tm[:, c * P:(c + 1) * P],
                                          ident[:])
                    E.inc(ins, "pe", ("pxh1T", pj))
                    for half, pdst, name in ((0, pK, "pxk"), (1, pV, "pxv")):
                        E.wait("v", ("ph1T", pj))
                        if pj == 0:
                            E.wait("v", ("wq", KC - 1))
                            E.wait("v", "gbcc")
                        else:
                            if half == 0:
                                E.wait("s", ("pkel", pj - 1))
                            else:
                                E.wait("s", ("pv", pj - 1))
                                E.wait("v", ("psadd", pj - 1, 1))
                        for n4 in range(2):
                            for c in range(KC):
                                t.matmul(
                                    pdst[:, n4 * 512:(n4 + 1) * 512],
                                    h1T[:, c * P:(c + 1) * P],
                                    wq_bf[:, c * 3 * D + D + half * D +
                                          n4 * 512:
                                          c * 3 * D + D + half * D +
                                          (n4 + 1) * 512],
                                    start=(c == 0), stop=False)
                            ins = t.matmul(
                                pdst[:, n4 * 512:(n4 + 1) * 512],
                                onesrow[:],
                                brow[0:1, D + half * D + n4 * 512:
                                     D + half * D + (n4 + 1) * 512],
                                start=False, stop=True)
                        E.inc(ins, "pe", (name, pj))
                    for r in range(2):
                        E.wait("v", ("pphik", pj))
                        E.wait("s", ("pv", pj))
                        if r == 0:
                            if pj > 0:
                                E.wait("v", ("psadd", pj - 1, 1))
                        else:
                            E.wait("v", ("psadd", pj, 0))
                        for pp in range(4):
                            pr = r * 4 + pp
                            ins = t.matmul(
                                pV[:, 512 + pp * P:512 + (pp + 1) * P],
                                phik_tm[:, pr * P:(pr + 1) * P],
                                v_tm[:, pr * P:(pr + 1) * P],
                                start=True, stop=True)
                        E.inc(ins, "pe", ("pxst", pj, r))
                    if pj > 0:
                        E.wait("v", ("pst", pj - 1))
                    E.wait("v", ("pphik", pj))
                    for pr in range(NPAIR):
                        ins = t.matmul(pD[:, 32 + pr:33 + pr],
                                       phik_tm[:, pr * P:(pr + 1) * P],
                                       onescol[:], start=True, stop=True)
                    E.inc(ins, "pe", ("pxscol", pj))
                E._mark("pe", "preend", E.counts.get("pe", 0))
            for j in range(T):
                E.wait("v", ("h1", j))
                if j > 0:
                    E.wait("v", ("attnT", j - 1) if LVL >= 4 else
                           (("kfm", j - 1) if LVL >= 2 else ("h1T", j - 1)))
                elif fused:
                    E.wait("v", "preend")
                for c in range(KC):
                    ins = t.transpose(pT[:, c * P:(c + 1) * P],
                                      h1_tm[:, c * P:(c + 1) * P], ident[:])
                E.inc(ins, "pe", ("h1T", j))
                if LVL < 2:
                    continue
                E.wait("v", ("h1T", j))
                if j == 0:
                    E.wait("v", ("wq", KC - 1))
                    E.wait("v", "gbcc")
                elif LVL >= 4:
                    E.wait("v", ("onorm", j - 1))
                    E.wait("v", ("oa", j - 1))
                else:
                    E.wait("s", ("qel", j - 1))
                for ec in range(KC):
                    for c in range(KC):
                        t.matmul(pQ[:, ec * P:(ec + 1) * P],
                                 wq_bf[:, c * 3 * D + ec * P:
                                       c * 3 * D + (ec + 1) * P],
                                 h1T[:, c * P:(c + 1) * P],
                                 start=(c == 0), stop=False)
                    ins = t.matmul(pQ[:, ec * P:(ec + 1) * P],
                                   brow[0:1, ec * P:(ec + 1) * P],
                                   onesrow[:], start=False, stop=True)
                E.inc(ins, "pe", ("q", j))
                for half, pdst, name in ((0, pK, "k"), (1, pV, "v")):
                    if j == 0 and fused:
                        if half == 0:
                            E.wait("s", "preend")
                        else:
                            E.wait("s", "preend")
                            E.wait("v", "preend")
                    if j > 0:
                        if half == 0:
                            if LVL >= 3:
                                E.wait("s", ("div", j - 1))
                                E.wait("v", ("div", j - 1))
                            else:
                                E.wait("s", ("kel", j - 1))
                        else:
                            E.wait("s", ("v", j - 1))
                            if LVL >= 3:
                                E.wait("v", ("sadd", j - 1, 1))
                                E.wait("v", ("mask", j - 1, 3))
                    for n4 in range(2):
                        for c in range(KC):
                            t.matmul(pdst[:, n4 * 512:(n4 + 1) * 512],
                                     h1T[:, c * P:(c + 1) * P],
                                     wq_bf[:, c * 3 * D + D + half * D + n4 * 512:
                                           c * 3 * D + D + half * D +
                                           (n4 + 1) * 512],
                                     start=(c == 0), stop=False)
                        ins = t.matmul(pdst[:, n4 * 512:(n4 + 1) * 512],
                                       onesrow[:],
                                       brow[0:1, D + half * D + n4 * 512:
                                            D + half * D + (n4 + 1) * 512],
                                       start=False, stop=True)
                    E.inc(ins, "pe", (name, j))
                E.wait("v", ("phik", j))
                E.wait("v", ("h1T", j))
                for c in range(KC):
                    ins = t.transpose(pT[:, c * P:(c + 1) * P],
                                      phik_tm[:, c * P:(c + 1) * P], ident[:])
                E.inc(ins, "pe", ("kfm", j))
                if LVL < 3:
                    continue
                E.wait("v", ("kfm", j))
                E.wait("v", ("phiq", j))
                E.wait("s", ("v", j))
                E.wait("v", ("scast", j))
                if j > 0:
                    E.wait("v", ("st", j - 1))
                    E.wait("v", ("rden", j - 1))
                elif fused:
                    E.wait("v", "preend")
                E.wait("v", ("phiqz", j))
                for g in range(4):
                    if g > 0:
                        E.wait("v", ("mask", j, g - 1))
                    for hh in range(4):
                        h = g * 4 + hh
                        pr, hlf = h // 2, h % 2
                        ins = t.matmul(
                            pV[:, hh * P:(hh + 1) * P],
                            phik_fm[:, pr * P:(pr + 1) * P],
                            (phiq_e if hlf == 0 else phiq_o)[:, pr * P:(pr + 1) * P],
                            start=True, stop=True)
                    E.inc(ins, "pe", ("at", j, g))
                    E.wait("v", ("mask", j, g))
                    for pp in range(2):
                        pr = 2 * g + pp
                        t.matmul(pD[:, 2 * pr:2 * pr + 2],
                                 phiq[:, pr * P:(pr + 1) * P],
                                 ssum_bf[:, 2 * pr:2 * pr + 2],
                                 start=True, stop=False)
                        t.matmul(pD[:, 2 * pr:2 * pr + 1],
                                 am[:, 2 * pp * P:(2 * pp + 1) * P],
                                 onescol[:], start=False, stop=False)
                        ins = t.matmul(pD[:, 2 * pr + 1:2 * pr + 2],
                                       am[:, (2 * pp + 1) * P:(2 * pp + 2) * P],
                                       onescol[:], start=False, stop=True)
                        t.matmul(pK[:, pr * P:(pr + 1) * P],
                                 phiq[:, pr * P:(pr + 1) * P],
                                 srun_bf[:, pr * P:(pr + 1) * P],
                                 start=True, stop=False)
                        for sub in range(2):
                            h = 2 * pr + sub
                            ins = t.matmul(pK[:, h * EH:(h + 1) * EH],
                                           am[:, (2 * pp + sub) * P:
                                              (2 * pp + sub + 1) * P],
                                           v_tm[:, h * EH:(h + 1) * EH],
                                           start=False, stop=(sub == 1))
                    E.inc(ins, "pe", ("avg", j, g))
                for r in range(2):
                    if r == 0:
                        E.wait("v", ("mask", j, 3))
                        if j > 0:
                            E.wait("v", ("sadd", j - 1, 1))
                        elif fused:
                            E.wait("v", "preend")
                    else:
                        E.wait("v", ("sadd", j, 0))
                    for pp in range(4):
                        pr = r * 4 + pp
                        ins = t.matmul(pV[:, 512 + pp * P:512 + (pp + 1) * P],
                                       phik_tm[:, pr * P:(pr + 1) * P],
                                       v_tm[:, pr * P:(pr + 1) * P],
                                       start=True, stop=True)
                    E.inc(ins, "pe", ("st", j, r))
                E.wait("v", ("rden", j))
                if j > 0:
                    E.wait("v", ("st", j - 1))
                elif fused:
                    E.wait("v", "preend")
                for pr in range(NPAIR):
                    ins = t.matmul(pD[:, 32 + pr:33 + pr],
                                   phik_tm[:, pr * P:(pr + 1) * P],
                                   onescol[:], start=True, stop=True)
                E.inc(ins, "pe", ("scol", j))
                if LVL < 4:
                    continue
                E.wait("s", ("div", j))
                E.wait("v", ("div", j))
                E.wait("v", ("kfm", j))
                for c in range(KC):
                    ins = t.transpose(pT[:, c * P:(c + 1) * P],
                                      attn_bf[:, c * P:(c + 1) * P], ident[:])
                E.inc(ins, "pe", ("attnT", j))
                E.wait("v", ("attnT", j))
                if j == 0:
                    E.wait("v", ("wo", KC - 1))
                E.wait("s", ("qel", j))
                for n4 in range(2):
                    for c in range(KC):
                        t.matmul(pQ[:, n4 * 512:(n4 + 1) * 512],
                                 attnT[:, c * P:(c + 1) * P],
                                 wo_bf[:, c * D + n4 * 512:
                                       c * D + (n4 + 1) * 512],
                                 start=(c == 0), stop=False)
                    ins = t.matmul(pQ[:, n4 * 512:(n4 + 1) * 512],
                                   onesrow[:],
                                   borow[0:1, n4 * 512:(n4 + 1) * 512],
                                   start=False, stop=True)
                E.inc(ins, "pe", ("wout", j))
            if upto != 'full':
                return
            for ha, t0, tn in halves():
                for tt in range(tn):
                    E.wait("v", ("h2", ha, tt))
                    if (ha, tt) == (0, 0):
                        E.wait("v", ("attnT", T - 1))
                    elif ha == 1 and tt == 0:
                        E.wait("v", ("hmul",) + H0_LAST_ODD)
                        E.wait("s", ("silu",) + H0_LAST_ODD)
                    else:
                        E.wait("v", ("h2T", ha, tt - 1))
                    for c in range(KC):
                        ins = t.transpose(pT[:, c * P:(c + 1) * P],
                                          h1_tm[:, c * P:(c + 1) * P],
                                          ident[:])
                    E.inc(ins, "pe", ("h2T", ha, tt))
                for sl in range(NSL):
                    kseq = ha * NSL + sl
                    E.wait("p", ("wgc", kseq))
                    E.wait("p", ("wuc", kseq))
                    for u in range(NU):
                        pgate, pups = GUP_BANKS[(kseq, u)]
                        for tc in range(u * 4, min(u * 4 + 4, tn)):
                            E.wait("v", ("h2T", ha, tc))
                        kk = KSEQ_OF[(kseq, u)]
                        if kk >= 2:
                            pku = UNIT_LIST[kk - 2]
                            E.wait("v", ("hmul",) + pku)
                            E.wait("s", ("silu",) + pku)
                        if ha == 1 and kseq == NSL and u < 2:
                            E.wait("v", ("fnorm", 0, HT - 2 + u))
                        h2v = h2T.ap().rearrange("p (c i) -> p c i", c=KC)
                        for hh in range(2):
                            dst = pgate[:, hh * 512:(hh + 1) * 512]
                            for c in range(KC):
                                t.matmul(dst,
                                         wg_sl[kseq % 2][:, c * SL + hh * P:
                                                         c * SL + (hh + 1) * P],
                                         h2v[:, c, u * 512:(u + 1) * 512],
                                         start=(c == 0), stop=(c == KC - 1))
                        for hh in range(2):
                            if len(pups) == 1:
                                udst = pups[0][:, hh * 512:(hh + 1) * 512]
                            else:
                                udst = pups[hh]
                            for c in range(KC):
                                ins = t.matmul(
                                    udst,
                                    wu_sl[kseq % 2][:, c * SL + hh * P:
                                                    c * SL + (hh + 1) * P],
                                    h2v[:, c, u * 512:(u + 1) * 512],
                                    start=(c == 0), stop=(c == KC - 1))
                        E.inc(ins, "pe", ("gup", kseq, u))
                for tt in range(tn):
                    pdn = pQ if tt % 2 == 0 else pK
                    lastk = ha * NSL + NSL - 1
                    E.wait("v", ("hmul", lastk, tt // 4))
                    E.wait("s", ("silu", lastk, tt // 4))
                    if tt == 0:
                        for c in range(HC):
                            E.wait("p", ("wdc", c))
                    if tt >= 2:
                        E.wait("v", ("fnorm", ha, tt - 2))
                    hstv = hst.rearrange("p (c i) -> p c i", i=HLEN)
                    for n4 in range(2):
                        for c in range(HC):
                            ins = t.matmul(
                                pdn[:, n4 * 512:(n4 + 1) * 512],
                                hstv[:, c, tt * P:(tt + 1) * P],
                                wd_bf[c][:, n4 * 512:(n4 + 1) * 512],
                                start=(c == 0), stop=(c == HC - 1))
                    E.inc(ins, "pe", ("down", ha, tt))

        run_bodies([
            (block.sync, body_sync, None),
            (block.vector, body_vector, "v"),
            (block.scalar, body_scalar, "s"),
            (block.gpsimd, body_pool, "p"),
            (block.tensor, body_tensor, None),
        ], sems)
    return nc




def _consts():
    maskT = np.triu(np.ones((P, P), dtype=np.float32))
    return {
        "maskT4": np.ascontiguousarray(np.tile(maskT, (1, 4))),
        "ident": np.eye(P, dtype=BF),
        "onescol": np.ones((P, 1), dtype=BF),
        "onesrow": np.ones((1, P), dtype=BF),
        "onesf": np.ones((1, P), dtype=np.float32),
    }


def kernel(L0_attn_x, L0_attn_Wqkv, L0_attn_bqkv, L0_attn_Wout, L0_attn_bout,
           L0_attn_gamma, L0_attn_beta, L0_ffn_w_gate, L0_ffn_w_up,
           L0_ffn_w_down):
    x = np.ascontiguousarray(np.asarray(L0_attn_x, dtype=np.float32)
                             ).reshape(2 * SEQ_LEN, D)
    wqkvT = np.ascontiguousarray(np.asarray(L0_attn_Wqkv, dtype=np.float32).T)
    woutT = np.ascontiguousarray(np.asarray(L0_attn_Wout, dtype=np.float32).T)
    bqkv = np.ascontiguousarray(
        np.asarray(L0_attn_bqkv, dtype=np.float32).reshape(1, 3 * D))
    bout = np.ascontiguousarray(
        np.asarray(L0_attn_bout, dtype=np.float32).reshape(1, D))
    gamma = np.ascontiguousarray(
        np.asarray(L0_attn_gamma, dtype=np.float32).reshape(1, D))
    beta = np.ascontiguousarray(
        np.asarray(L0_attn_beta, dtype=np.float32).reshape(1, D))
    wgT = np.ascontiguousarray(np.asarray(L0_ffn_w_gate, dtype=np.int32).T)
    wuT = np.ascontiguousarray(np.asarray(L0_ffn_w_up, dtype=np.int32).T)
    wdT = np.ascontiguousarray(np.asarray(L0_ffn_w_down, dtype=np.int32).T)
    consts = _consts()

    NT = T_MAIN * TOK
    xs = [np.ascontiguousarray(x[c * NT:(c + 1) * NT]) for c in range(N_CORES)]

    nc_b = build_main(T_MAIN, fused=True)
    in_b = []
    for c in range(N_CORES):
        seq = c // 4
        xseq = x[seq * SEQ_LEN:(seq + 1) * SEQ_LEN]
        npre = (c % 4) * T_MAIN
        pf = np.zeros((P, 3 * T_MAIN), dtype=np.float32)
        pf[:, :npre] = 1.0
        in_b.append({"x_main": xs[c], "x_pre": np.ascontiguousarray(xseq[0:3 * NT]),
                     "pflag": pf,
                     "wqkvT": wqkvT, "woutT": woutT, "bqkv": bqkv,
                     "bout": bout, "gamma": gamma, "beta": beta, "wgT": wgT,
                     "wuT": wuT, "wdT": wdT, **consts})
    res_b = run_bass_kernel_spmd(nc_b, in_b, core_ids=list(range(N_CORES)))
    outs = [np.asarray(res_b.results[c]["out"]) for c in range(N_CORES)]
    return np.concatenate(outs, axis=0).reshape(2, SEQ_LEN, D)


# version 11
# speedup vs baseline: 1.6108x; 1.5790x over previous
from contextlib import ExitStack, contextmanager

import numpy as np
import ml_dtypes
import concourse.bass as bass
import concourse.mybir as mybir
from concourse.bass_utils import run_bass_kernel_spmd

F32 = mybir.dt.float32
BF16 = mybir.dt.bfloat16
I32 = mybir.dt.int32
AF = mybir.ActivationFunctionType
ALU = mybir.AluOpType

D = 1024
HID = 3072
NPAIR = 8
EH = 64
P = 128
KC = D // P
HC = HID // P
LN_EPS = 1e-5
FEAT_EPS = 1e-4
DEN_EPS = 8.0 * FEAT_EPS
WSCALE = 1.0 / 127.0

N_CORES = 8
SEQ_LEN = 8192
T_MAIN = 16
TOK = P

BF = ml_dtypes.bfloat16




class _MockInstr:
    def then_inc(self, *a, **k):
        return self

    def __getattr__(self, name):
        return lambda *a, **k: self


class _MockEng:
    def __getattr__(self, name):
        return lambda *a, **k: _MockInstr()


_EDGE_OPS = frozenset((
    "activation", "add", "mul", "copy", "square", "sqrt", "sign", "memzero",
    "memset", "tensor_copy", "tensor_tensor", "tensor_add", "tensor_sub",
    "tensor_mul", "tensor_max", "tensor_scalar", "tensor_single_scalar",
    "tensor_scalar_add", "tensor_scalar_sub", "tensor_scalar_mul",
    "tensor_scalar_max", "tensor_scalar_min", "tensor_relu", "tensor_reduce",
    "reduce_max", "reduce_sum", "scalar_tensor_tensor", "tensor_tensor_scan",
    "bn_stats", "bn_aggr", "reciprocal", "select", "copy_predicated",
    "iota", "pool", "pool_avg", "pool_max", "stream_shuffle", "engine_nop",
))


class _Edged:

    def __init__(self, env, eng, sem_name):
        self._env = env
        self._eng = eng
        self._sem = sem_name

    def __getattr__(self, name):
        env = self._env
        if name in _EDGE_OPS:
            def f(*a, **k):
                c = env.counts.get(self._sem, 0)
                if env.emit:
                    if c > 0:
                        self._eng.wait_ge(env.sems[self._sem], c)
                    instr = getattr(self._eng, name)(*a, **k)
                    instr.then_inc(env.sems[self._sem], 1)
                else:
                    instr = _MockInstr()
                env.counts[self._sem] = c + 1
                return instr
            return f
        if not env.emit:
            return lambda *a, **k: _MockInstr()
        return getattr(self._eng, name)


class Env:

    AUTO = frozenset(("v", "s", "p"))

    def __init__(self, plan, emit, eng, sems, auto_sem=None):
        self.plan = plan
        self.counts = {}
        self.emit = emit
        self.sems = sems
        self.raw_eng = eng
        if auto_sem is not None:
            self.eng = _Edged(self, eng, auto_sem)
        else:
            self.eng = eng if emit else _MockEng()

    def _bump(self, sem, key, n):
        c = self.counts.get(sem, 0) + n
        self.counts[sem] = c
        if key is not None:
            self._mark(sem, key, c)
        return c

    def _mark(self, sem, key, c):
        if not self.emit:
            assert (sem, key) not in self.plan, ("dup mark", sem, key)
            self.plan[(sem, key)] = c
        else:
            assert self.plan.get((sem, key)) == c, (
                "plan/emit drift", sem, key, self.plan.get((sem, key)), c)

    def inc(self, instr, sem, key=None, n=1):
        if sem in self.AUTO:
            if key is not None:
                self._mark(sem, key, self.counts.get(sem, 0))
            return instr
        self._bump(sem, key, n)
        if self.emit:
            instr.then_inc(self.sems[sem], n)
        return instr

    def wait(self, sem, key):
        if self.emit:
            eng = self.raw_eng if self.raw_eng is not None else None
            eng.wait_ge(self.sems[sem], self.plan[(sem, key)])

    def selfwait(self, sem):
        if self.emit:
            self.raw_eng.wait_ge(self.sems[sem], self.counts[sem])

    def wait_val(self, sem, val):
        if self.emit:
            self.raw_eng.wait_ge(self.sems[sem], val)

    @contextmanager
    def gated(self, reg, j):
        if not self.emit:
            yield
            return
        before = dict(self.counts)
        with self.raw_eng.If_cmp(reg, j, "IS_GT"):
            yield
        with self.raw_eng.Else():
            for sem, c in self.counts.items():
                d = c - before.get(sem, 0)
                if d:
                    self.raw_eng.sem_inc(self.sems[sem], d)


def run_bodies(bodies, sems):
    plan = {}
    for item in bodies:
        attach, body, auto = item
        body(Env(plan, False, None, sems, auto))
    for item in bodies:
        attach, body, auto = item
        attach(lambda eng, body=body, auto=auto: body(
            Env(plan, True, eng, sems, auto)))
    return plan




def emit_ln_stats(E, v, stats, mv, src_lo, src_hi, key):
    v.bn_stats(out=stats[:, 0:6], in_=src_lo)
    E.inc(v.bn_stats(out=stats[:, 6:12], in_=src_hi), "v")
    E.selfwait("v")
    E.inc(v.bn_aggr(out=mv[:], in_=stats[:, 0:12]), "v", key)
    E.selfwait("v")


def emit_rstd(E, s, mv, rstd, eps_ap):
    s.activation(out=rstd[:], in_=mv[:, 1:2], func=AF.Ln, bias=eps_ap)
    return s.activation(out=rstd[:], in_=rstd[:], func=AF.Exp, scale=-0.5)


def sadd_views(dst, src, r, blk_src):
    raise NotImplementedError




def build_state(T):
    nc = bass.Bass("TRN2", target_bir_lowering=False)
    NT = T * TOK
    x_in = nc.dram_tensor("x_own", [NT, D], F32, kind="ExternalInput").ap()
    wqkvT = nc.dram_tensor("wqkvT", [D, 3 * D], F32, kind="ExternalInput").ap()
    bqkv = nc.dram_tensor("bqkv", [1, 3 * D], F32, kind="ExternalInput").ap()
    ident_in = nc.dram_tensor("ident", [P, P], BF16, kind="ExternalInput").ap()
    onescol_in = nc.dram_tensor("onescol", [P, 1], BF16, kind="ExternalInput").ap()
    onesrow_in = nc.dram_tensor("onesrow", [1, P], BF16, kind="ExternalInput").ap()
    stot_out = nc.dram_tensor("stot", [P, 528], F32, kind="ExternalOutput").ap()

    with ExitStack() as ctx:
        ident = ctx.enter_context(nc.sbuf_tensor("identS", [P, P], BF16))
        onescol = ctx.enter_context(nc.sbuf_tensor("onescolS", [P, 1], BF16))
        onesrow = ctx.enter_context(nc.sbuf_tensor("onesrowS", [1, P], BF16))
        brow = ctx.enter_context(nc.sbuf_tensor("brow", [1, 3 * D], BF16))
        wkv_bf = ctx.enter_context(nc.sbuf_tensor("wkv_bf", [P, KC * 2 * D], BF16))
        wld_t = ctx.enter_context(nc.sbuf_tensor("wld", [P, 2 * 3 * D], F32))
        xt_t = ctx.enter_context(nc.sbuf_tensor("xt", [P, 2 * D], F32))
        h1_tm = ctx.enter_context(nc.sbuf_tensor("h1_tm", [P, D], BF16))
        h1T = ctx.enter_context(nc.sbuf_tensor("h1T", [P, D], BF16))
        phik_tm = ctx.enter_context(nc.sbuf_tensor("phik_tm", [P, D], BF16))
        v_tm = ctx.enter_context(nc.sbuf_tensor("v_tm", [P, D], BF16))
        t1 = ctx.enter_context(nc.sbuf_tensor("t1", [P, D], F32))
        t2 = ctx.enter_context(nc.sbuf_tensor("t2", [P, D], F32))
        t3 = ctx.enter_context(nc.sbuf_tensor("t3", [P, D], F32))
        stats = ctx.enter_context(nc.sbuf_tensor("stats", [P, 12], F32))
        mv = ctx.enter_context(nc.sbuf_tensor("mv", [P, 2], F32))
        rstd = ctx.enter_context(nc.sbuf_tensor("rstd", [P, 1], F32))
        c_eps = ctx.enter_context(nc.sbuf_tensor("c_eps", [P, 1], F32))
        sacc = ctx.enter_context(nc.sbuf_tensor("sacc", [P, 528], F32))
        pT = ctx.enter_context(nc.psum_tensor("pT", [P, 1024], BF16))
        pK = ctx.enter_context(nc.psum_tensor("pK", [P, 1024], F32))
        pV = ctx.enter_context(nc.psum_tensor("pV", [P, 1024], F32))
        pS = ctx.enter_context(nc.psum_tensor("pS", [P, 512], F32))
        pC = ctx.enter_context(nc.psum_tensor("pC", [P, 512], F32))
        h_x0 = ctx.enter_context(nc.semaphore("m_x0"))
        h_x1 = ctx.enter_context(nc.semaphore("m_x1"))
        h_c = ctx.enter_context(nc.semaphore("m_c"))
        h_ld = ctx.enter_context(nc.semaphore("m_ld"))
        h_pe = ctx.enter_context(nc.semaphore("m_pe"))
        h_v = ctx.enter_context(nc.semaphore("m_v"))
        h_s = ctx.enter_context(nc.semaphore("m_s"))
        h_out = ctx.enter_context(nc.semaphore("m_out"))
        block = ctx.enter_context(nc.Block())

        wld = [wld_t.ap()[:, 0:3 * D], wld_t.ap()[:, 3 * D:6 * D]]
        xt = [xt_t.ap()[:, 0:D], xt_t.ap()[:, D:2 * D]]
        sems = {"x0": h_x0, "x1": h_x1, "c": h_c, "ld": h_ld, "pe": h_pe,
                "v": h_v, "s": h_s, "out": h_out}

        def body_sync(E):
            sy = E.eng
            for name, dst, srcap in (
                ("ident", ident[:], ident_in),
                ("onescol", onescol[:], onescol_in),
                ("onesrow", onesrow[:], onesrow_in),
            ):
                E.inc(sy.dma_start(out=dst, in_=srcap), "c", name, 16)
            E.inc(sy.dma_start(out=wld[0][0:1, :], in_=bqkv), "c", "call", 16)
            for c in range(KC):
                E.wait("v", "bq" if c == 0 else ("wcv", c - 1))
                E.inc(sy.dma_start(out=wld[0],
                                   in_=wqkvT[c * P:(c + 1) * P, :]),
                      "ld", ("w", c), 16)
            for j in range(T):
                if j >= 2:
                    E.wait("pe", ("h1T", j - 2))
                E.inc(sy.dma_start(out=xt[j % 2],
                                   in_=x_in[j * TOK:(j + 1) * TOK, :]),
                      "x%d" % (j % 2), ("x", j), 16)
            E.wait_val("out", 16)

        def body_vector(E):
            v = E.eng
            E.wait("c", "call")
            E.inc(v.tensor_copy(out=brow[:], in_=wld[0][0:1, :]), "v", "bq")
            for c in range(KC):
                E.wait("ld", ("w", c))
                E.inc(v.tensor_copy(out=wkv_bf[:, c * 2 * D:(c + 1) * 2 * D],
                                    in_=wld[0][:, D:3 * D]),
                      "v", ("wcv", c))
            E.inc(v.memset(sacc[:], 0.0), "v", "sz")
            E.inc(v.memset(c_eps[:], LN_EPS), "v", "ceps")
            for j in range(T):
                E.wait("x%d" % (j % 2), ("x", j))
                emit_ln_stats(E, v, stats, mv, xt[j % 2][:, 0:512],
                              xt[j % 2][:, 512:1024], ("xa", j))
                E.wait("s", ("rstd", j))
                E.inc(v.tensor_scalar(out=h1_tm[:], in0=xt[j % 2],
                                      scalar1=mv[:, 0:1], scalar2=rstd[:],
                                      op0=ALU.subtract, op1=ALU.mult),
                      "v", ("h1", j))
                E.wait("pe", ("h1T", j))
                E.inc(v.tensor_copy(out=h1T[:], in_=pT[:]), "v", ("h1T", j))
                E.wait("s", ("kel", j))
                E.inc(v.scalar_tensor_tensor(out=phik_tm[:], in0=t2[:],
                                             scalar=FEAT_EPS, in1=t3[:],
                                             op0=ALU.add, op1=ALU.add),
                      "v", ("phik", j))
                for r in range(2):
                    E.wait("pe", ("st", j, r))
                    ps = (pS if r == 0 else pC)[:]
                    sview = sacc[:, 0:512]
                    v.tensor_tensor(
                        out=sview[0:64].rearrange("p (g f) -> p g f", f=64
                                                  )[:, 4 * r:4 * r + 4, :],
                        in0=sview[0:64].rearrange("p (g f) -> p g f", f=64
                                                  )[:, 4 * r:4 * r + 4, :],
                        in1=ps[0:64].rearrange("p (g f) -> p g f", f=P
                                               )[:, :, 0:64],
                        op=ALU.add)
                    E.inc(v.tensor_tensor(
                        out=sview[64:128].rearrange("p (g f) -> p g f", f=64
                                                    )[:, 4 * r:4 * r + 4, :],
                        in0=sview[64:128].rearrange("p (g f) -> p g f", f=64
                                                    )[:, 4 * r:4 * r + 4, :],
                        in1=ps[64:128].rearrange("p (g f) -> p g f", f=P
                                                 )[:, :, 64:128],
                        op=ALU.add), "v", ("sadd", j, r))
                E.wait("pe", ("scol", j))
                sc = sacc[:, 512:528]
                v.tensor_tensor(
                    out=sc[0:64].rearrange("p (g o) -> p g o", o=1)[:, 0:16:2, :],
                    in0=sc[0:64].rearrange("p (g o) -> p g o", o=1)[:, 0:16:2, :],
                    in1=pK[0:64, 0:8].rearrange("p (g o) -> p g o", o=1),
                    op=ALU.add)
                E.inc(v.tensor_tensor(
                    out=sc[64:128].rearrange("p (g o) -> p g o", o=1)[:, 1:16:2, :],
                    in0=sc[64:128].rearrange("p (g o) -> p g o", o=1)[:, 1:16:2, :],
                    in1=pK[64:128, 0:8].rearrange("p (g o) -> p g o", o=1),
                    op=ALU.add), "v", ("scadd", j))

        def body_scalar(E):
            s = E.eng
            E.wait("v", "ceps")
            for j in range(T):
                E.wait("v", ("xa", j))
                E.inc(emit_rstd(E, s, mv, rstd, c_eps[:]), "s", ("rstd", j))
                E.wait("pe", ("k", j))
                s.activation(out=t1[:], in_=pK[:], func=AF.Relu, scale=-1.0)
                s.activation(out=t2[:], in_=t1[:], func=AF.Exp, scale=-1.0)
                E.inc(s.activation(out=t3[:], in_=pK[:], func=AF.Relu),
                      "s", ("kel", j))
                E.wait("pe", ("v", j))
                E.inc(s.activation(out=v_tm[:], in_=pV[:], func=AF.Copy),
                      "s", ("v", j))
            E.wait("v", ("scadd", T - 1))
            E.inc(s.dma_start(out=stot_out, in_=sacc[:]), "out", "res", 16)

        def body_tensor(E):
            t = E.eng
            for j in range(T):
                E.wait("v", ("h1", j))
                if j == 0:
                    E.wait("c", "call")
                if j > 0:
                    E.wait("v", ("h1T", j - 1))
                for c in range(KC):
                    ins = t.transpose(pT[:, c * P:(c + 1) * P],
                                      h1_tm[:, c * P:(c + 1) * P], ident[:])
                E.inc(ins, "pe", ("h1T", j))
                E.wait("v", ("h1T", j))
                if j == 0:
                    E.wait("v", ("wcv", KC - 1))
                for half, pdst, name in ((0, pK, "k"), (1, pV, "v")):
                    if j > 0:
                        if half == 0:
                            E.wait("s", ("kel", j - 1))
                            E.wait("v", ("scadd", j - 1))
                        else:
                            E.wait("s", ("v", j - 1))
                            E.wait("pe", ("st", j - 1, 1))
                    for n4 in range(2):
                        for c in range(KC):
                            t.matmul(pdst[:, n4 * 512:(n4 + 1) * 512],
                                     h1T[:, c * P:(c + 1) * P],
                                     wkv_bf[:, c * 2 * D + half * D + n4 * 512:
                                            c * 2 * D + half * D + (n4 + 1) * 512],
                                     start=(c == 0), stop=False)
                        ins = t.matmul(pdst[:, n4 * 512:(n4 + 1) * 512],
                                       onesrow[:],
                                       brow[0:1, D + half * D + n4 * 512:
                                            D + half * D + (n4 + 1) * 512],
                                       start=False, stop=True)
                    E.inc(ins, "pe", (name, j))
                E.wait("v", ("phik", j))
                E.wait("s", ("v", j))
                for r in range(2):
                    ps = pS if r == 0 else pC
                    if j > 0:
                        E.wait("v", ("sadd", j - 1, r))
                    for pp in range(4):
                        pr = r * 4 + pp
                        ins = t.matmul(ps[:, pp * P:(pp + 1) * P],
                                       phik_tm[:, pr * P:(pr + 1) * P],
                                       v_tm[:, pr * P:(pr + 1) * P],
                                       start=True, stop=True)
                    E.inc(ins, "pe", ("st", j, r))
                E.wait("s", ("kel", j))
                if j > 0:
                    E.wait("v", ("scadd", j - 1))
                for pr in range(NPAIR):
                    ins = t.matmul(pK[:, pr:pr + 1],
                                   phik_tm[:, pr * P:(pr + 1) * P],
                                   onescol[:], start=True, stop=True)
                E.inc(ins, "pe", ("scol", j))

        run_bodies([
            (block.sync, body_sync, None),
            (block.vector, body_vector, "v"),
            (block.scalar, body_scalar, "s"),
            (block.tensor, body_tensor, None),
        ], sems)
    return nc




def build_main(T, upto='full', fused=True):
    STAGES = ['setup', 'h1', 'qkv', 'attn', 'wout', 'main', 'full']
    LVL = STAGES.index(upto)
    NPRE = 3 * T if fused else 0
    nc = bass.Bass("TRN2", target_bir_lowering=False)
    NT = T * TOK
    HT = (T + 1) // 2
    HLEN = HT * TOK
    SL = 256
    NSL = HID // SL

    x_in = nc.dram_tensor("x_main", [NT, D], F32, kind="ExternalInput").ap()
    wqkvT_in = nc.dram_tensor("wqkvT", [D, 3 * D], F32, kind="ExternalInput").ap()
    woutT_in = nc.dram_tensor("woutT", [D, D], F32, kind="ExternalInput").ap()
    bqkv_in = nc.dram_tensor("bqkv", [1, 3 * D], F32, kind="ExternalInput").ap()
    bout_in = nc.dram_tensor("bout", [1, D], F32, kind="ExternalInput").ap()
    gamma_in = nc.dram_tensor("gamma", [1, D], F32, kind="ExternalInput").ap()
    beta_in = nc.dram_tensor("beta", [1, D], F32, kind="ExternalInput").ap()
    wgT_in = nc.dram_tensor("wgT", [D, HID], I32, kind="ExternalInput").ap()
    wuT_in = nc.dram_tensor("wuT", [D, HID], I32, kind="ExternalInput").ap()
    wdT_in = nc.dram_tensor("wdT", [HID, D], I32, kind="ExternalInput").ap()
    maskT4_in = nc.dram_tensor("maskT4", [P, 512], F32, kind="ExternalInput").ap()
    ident_in = nc.dram_tensor("ident", [P, P], BF16, kind="ExternalInput").ap()
    onescol_in = nc.dram_tensor("onescol", [P, 1], BF16, kind="ExternalInput").ap()
    onesrow_in = nc.dram_tensor("onesrow", [1, P], BF16, kind="ExternalInput").ap()
    onesf_in = nc.dram_tensor("onesf", [1, P], F32, kind="ExternalInput").ap()
    if fused:
        x_pre = nc.dram_tensor("x_pre", [3 * T * TOK, D], F32,
                               kind="ExternalInput").ap()
        pflag_in = nc.dram_tensor("pflag", [P, 3 * T], F32,
                                  kind="ExternalInput").ap()
    else:
        stot_in = nc.dram_tensor("stot_all", [P, 8 * 528], F32,
                                 kind="ExternalInput").ap()
        maskbc_in = nc.dram_tensor("mask_bc", [P, 8], F32,
                                   kind="ExternalInput").ap()

    out = nc.dram_tensor("out", [NT, D], F32, kind="ExternalOutput").ap()
    x2_hbm = nc.dram_tensor("x2_scratch", [NT, D], F32, kind="Internal").ap()

    with ExitStack() as ctx:
        maskT4 = ctx.enter_context(nc.sbuf_tensor("maskT4S", [P, 512], F32))
        ident = ctx.enter_context(nc.sbuf_tensor("identS", [P, P], BF16))
        onescol = ctx.enter_context(nc.sbuf_tensor("onescolS", [P, 1], BF16))
        onesrow = ctx.enter_context(nc.sbuf_tensor("onesrowS", [1, P], BF16))
        onesf = ctx.enter_context(nc.sbuf_tensor("onesfS", [1, P], F32))
        brow = ctx.enter_context(nc.sbuf_tensor("brow", [1, 3 * D], BF16))
        borow = ctx.enter_context(nc.sbuf_tensor("borow", [1, D], BF16))
        stats = ctx.enter_context(nc.sbuf_tensor("stats", [P, 12], F32))
        mv = ctx.enter_context(nc.sbuf_tensor("mv", [P, 2], F32))
        rstd = ctx.enter_context(nc.sbuf_tensor("rstd", [P, 1], F32))
        c_eps = ctx.enter_context(nc.sbuf_tensor("c_eps", [P, 1], F32))
        pflag = ctx.enter_context(nc.sbuf_tensor("pflagS", [P, 3 * T], F32))
        rden = ctx.enter_context(nc.sbuf_tensor("rden", [P, 16], F32))
        rdpre = ctx.enter_context(nc.sbuf_tensor("rdpre", [P, 16], F32))
        gbc = ctx.enter_context(nc.sbuf_tensor("gbc", [P, D], F32))
        bbc = ctx.enter_context(nc.sbuf_tensor("bbc", [P, D], F32))
        wq_bf = ctx.enter_context(nc.sbuf_tensor("wq_bf", [P, KC * 3 * D], BF16))
        wo_bf = ctx.enter_context(nc.sbuf_tensor("wo_bf", [P, KC * D], BF16))
        stall = ctx.enter_context(nc.sbuf_tensor("stall", [P, 8 * 528], F32))
        mbc = ctx.enter_context(nc.sbuf_tensor("mbc", [P, 8], F32))
        sini = ctx.enter_context(nc.sbuf_tensor("sini", [P, 528], F32))
        srun = ctx.enter_context(nc.sbuf_tensor("srun", [P, 8 * P], F32))
        ssum = ctx.enter_context(nc.sbuf_tensor("ssum", [P, 16], F32))
        srun_bf = ctx.enter_context(nc.sbuf_tensor("srun_bf", [P, 8 * P], BF16))
        ssum_bf = ctx.enter_context(nc.sbuf_tensor("ssum_bf", [P, 16], BF16))
        xt_t = ctx.enter_context(nc.sbuf_tensor("xt", [P, 2 * D], F32))
        h1_tm = ctx.enter_context(nc.sbuf_tensor("h1_tm", [P, D], BF16))
        h1T = ctx.enter_context(nc.sbuf_tensor("h1T", [P, D], BF16))
        phiq = ctx.enter_context(nc.sbuf_tensor("phiq", [P, D], BF16))
        phiq_e = ctx.enter_context(nc.sbuf_tensor("phiq_e", [P, D], BF16))
        phiq_o = ctx.enter_context(nc.sbuf_tensor("phiq_o", [P, D], BF16))
        phik_tm = ctx.enter_context(nc.sbuf_tensor("phik_tm", [P, D], BF16))
        phik_fm = ctx.enter_context(nc.sbuf_tensor("phik_fm", [P, D], BF16))
        v_tm = ctx.enter_context(nc.sbuf_tensor("v_tm", [P, D], BF16))
        am = ctx.enter_context(nc.sbuf_tensor("am", [P, 512], BF16))
        attnT = ctx.enter_context(nc.sbuf_tensor("attnT", [P, D], BF16))
        attn_bf = ctx.enter_context(nc.sbuf_tensor("attn_bf", [P, D], BF16))
        t1 = ctx.enter_context(nc.sbuf_tensor("t1", [P, D], F32))
        t2 = ctx.enter_context(nc.sbuf_tensor("t2", [P, D], F32))
        t3 = ctx.enter_context(nc.sbuf_tensor("t3", [P, D], F32))
        o_n = ctx.enter_context(nc.sbuf_tensor("o_n", [P, D], F32))
        a_g = ctx.enter_context(nc.sbuf_tensor("a_g", [P, D], F32))
        x2sb_t = ctx.enter_context(nc.sbuf_tensor("x2sb", [P, 2 * D], F32))
        h2T = ctx.enter_context(nc.sbuf_tensor("h2T", [P, KC * 1024], BF16))
        wg_sl_t = ctx.enter_context(nc.sbuf_tensor("wg_sl", [P, 2 * KC * SL], BF16))
        wu_sl_t = ctx.enter_context(nc.sbuf_tensor("wu_sl", [P, 2 * KC * SL], BF16))
        silu_t = ctx.enter_context(nc.sbuf_tensor("silu_t", [P, 1024], F32))
        f_n = ctx.enter_context(nc.sbuf_tensor("f_n", [P, D], F32))
        wdx = ctx.enter_context(nc.sbuf_tensor("wdx", [P, 1024], BF16))
        pT = ctx.enter_context(nc.psum_tensor("pT", [P, 1024], BF16))
        pQ = ctx.enter_context(nc.psum_tensor("pQ", [P, 1024], F32))
        pK = ctx.enter_context(nc.psum_tensor("pK", [P, 1024], F32))
        pV = ctx.enter_context(nc.psum_tensor("pV", [P, 1024], F32))
        pD = ctx.enter_context(nc.psum_tensor("pD", [P, 512], F32))
        h_x0 = ctx.enter_context(nc.semaphore("s_x0"))
        h_x1 = ctx.enter_context(nc.semaphore("s_x1"))
        h_c = ctx.enter_context(nc.semaphore("s_c"))
        h_ld = ctx.enter_context(nc.semaphore("s_ld"))
        h_pe = ctx.enter_context(nc.semaphore("s_pe"))
        h_v = ctx.enter_context(nc.semaphore("s_v"))
        h_s = ctx.enter_context(nc.semaphore("s_s"))
        h_p = ctx.enter_context(nc.semaphore("s_p"))
        h_w0 = ctx.enter_context(nc.semaphore("s_w0"))
        h_w1 = ctx.enter_context(nc.semaphore("s_w1"))
        h_o0 = ctx.enter_context(nc.semaphore("s_o0"))
        h_o1 = ctx.enter_context(nc.semaphore("s_o1"))
        h_p0 = ctx.enter_context(nc.semaphore("s_p0"))
        h_p1 = ctx.enter_context(nc.semaphore("s_p1"))
        h_g = ctx.enter_context(nc.semaphore("s_g"))
        block = ctx.enter_context(nc.Block())

        xt = [xt_t.ap()[:, 0:D], xt_t.ap()[:, D:2 * D]]
        x2sb = [x2sb_t.ap()[:, 0:D], x2sb_t.ap()[:, D:2 * D]]
        wg_sl = [wg_sl_t.ap()[:, 0:KC * SL], wg_sl_t.ap()[:, KC * SL:]]
        wu_sl = [wu_sl_t.ap()[:, 0:KC * SL], wu_sl_t.ap()[:, KC * SL:]]
        sems = {"x0": h_x0, "x1": h_x1, "c": h_c, "ld": h_ld, "pe": h_pe,
                "v": h_v, "s": h_s, "p": h_p, "w0": h_w0, "w1": h_w1,
                "o0": h_o0, "o1": h_o1, "px0": h_p0, "px1": h_p1,
                "g": h_g}
        if upto == 'full':
            assert HLEN == 1024 and T == 16, "V1 layout sized for T=16"

        wld = h2T.ap().bitcast(F32)[:, 0:3 * D]
        hst = wq_bf.ap()
        wgu_int = stall.ap().bitcast(I32)[:, 0:KC * SL]
        wuu_int = stall.ap().bitcast(I32)[:, KC * SL:2 * KC * SL]
        wd_int = srun.ap().bitcast(I32)[:, 0:1024]
        wd_bf = []
        for tens in (t1, t2, t3, o_n, a_g, gbc, bbc):
            b = tens.ap().bitcast(BF16)
            wd_bf.append(b[:, 0:1024])
            wd_bf.append(b[:, 1024:2048])
        wd_bf.append(sini.ap().bitcast(BF16)[:, 0:1024])
        wd_bf.append(srun_bf.ap()[:, 0:1024])
        for tens in (phiq, phik_tm, phik_fm, v_tm, wdx, h1T, attnT,
                     attn_bf):
            wd_bf.append(tens.ap())
        assert len(wd_bf) == HC
        x2r = [xt[0], xt[1]]
        outsb = [x2sb[0], x2sb[1]]
        xp = [x2sb[0], x2sb[1]]
        pT32 = pT.ap().bitcast(F32)

        NU = 2
        UNIT_LIST = []
        for ha in range(2):
            for sl in range(NSL):
                for u in range(NU):
                    UNIT_LIST.append((ha * NSL + sl, u))
        KSEQ_OF = {ku: i for i, ku in enumerate(UNIT_LIST)}
        GUP_BANKS = {}
        for i, ku in enumerate(UNIT_LIST):
            if i % 2 == 0:
                GUP_BANKS[ku] = (pQ[:], [pK[:]])
            else:
                GUP_BANKS[ku] = (pV[:], [pT32, pD[:]])

        def halves():
            for ha in range(2):
                yield (ha, 0, HT) if ha == 0 else (ha, HT, T - HT)

        LASTK = [NSL - 1, 2 * NSL - 1]
        H0_UNITS = [ku for ku in UNIT_LIST if ku[0] < NSL]
        H0_LAST_ODD = [ku for ku in H0_UNITS if KSEQ_OF[ku] % 2 == 1][-1]

        def body_sync(E):
            sy = E.eng
            cl = [("maskT4", maskT4[:], maskT4_in),
                  ("ident", ident[:], ident_in),
                  ("onescol", onescol[:], onescol_in),
                  ("onesrow", onesrow[:], onesrow_in),
                  ("onesf", onesf[:], onesf_in)]
            if fused:
                cl += [("pflag", pflag[:], pflag_in)]
            else:
                cl += [("stall", stall[:], stot_in),
                       ("mbc", mbc[:], maskbc_in)]
            for name, dst, srcap in cl:
                E.inc(sy.dma_start(out=dst, in_=srcap), "c", name, 16)
            E.inc(sy.dma_start(out=wld[0:1, :], in_=bqkv_in), "c", "bq", 16)
            E.inc(sy.dma_start(out=t1[0:1, 0:D], in_=bout_in), "c", "bo", 16)
            E.inc(sy.dma_start(out=t2[0:1, 0:D], in_=gamma_in), "c", "ga", 16)
            E.inc(sy.dma_start(out=t3[0:1, 0:D], in_=beta_in), "c", "call", 16)
            for c in range(KC):
                if c == 0:
                    E.wait("v", "rows")
                    E.wait("g", "gbcmm")
                else:
                    E.wait("v", ("wq", c - 1))
                E.inc(sy.dma_start(out=wld[:, :],
                                   in_=wqkvT_in[c * P:(c + 1) * P, :]),
                      "ld", ("wq", c), 16)
            for c in range(KC):
                E.wait("v", ("wq", KC - 1) if c == 0 else ("wo", c - 1))
                E.inc(sy.dma_start(out=wld[:, 0:D],
                                   in_=woutT_in[c * P:(c + 1) * P, :]),
                      "ld", ("wo", c), 16)
            if upto == 'setup':
                return
            for j in range(min(2, T)):
                E.inc(sy.dma_start(out=xt[j % 2],
                                   in_=x_in[j * TOK:(j + 1) * TOK, :]),
                      "x%d" % (j % 2), ("x", j), 16)
            if fused:
                for pj in range(NPRE):
                    if pj >= 2:
                        E.wait("v", ("ph1", pj - 2))
                        E.wait("px%d" % (pj % 2), ("px", pj - 2))
                    E.inc(sy.dma_start(
                        out=xp[pj % 2],
                        in_=x_pre[pj * TOK:(pj + 1) * TOK, :]),
                        "px%d" % (pj % 2), ("px", pj), 16)
            for j in range(2, T):
                if LVL >= 5:
                    E.wait("v", ("x2", j - 2))
                else:
                    E.wait("v", ("h1", j - 2))
                E.inc(sy.dma_start(out=xt[j % 2],
                                   in_=x_in[j * TOK:(j + 1) * TOK, :]),
                      "x%d" % (j % 2), ("x", j), 16)
            if upto != 'full':
                if upto == 'main':
                    for jp in range(max(0, T - 2), T):
                        E.wait("w%d" % (jp % 2), ("x2w", jp))
                return
            buffree = [("v", ("x2", T - 1)), ("v", ("x2", T - 1))]
            PREV_CONV = [None]
            for ha, t0, tn in halves():
                for tt in range(tn):
                    tg = t0 + tt
                    E.wait("w%d" % (tg % 2), ("x2w", tg))
                    E.wait(*buffree[tt % 2])
                    E.inc(sy.dma_start(out=x2r[tt % 2],
                                       in_=x2_hbm[tg * TOK:(tg + 1) * TOK, :]),
                          "x%d" % (tt % 2), ("x2a", ha, tt), 16)
                    buffree[tt % 2] = ("v", ("h2", ha, tt))
                for sl in range(NSL):
                    kseq = ha * NSL + sl
                    if kseq == 0:
                        E.wait("v", "sexp")
                        E.wait("v", ("wo", KC - 1))
                    else:
                        E.wait("s", PREV_CONV[0])
                    if kseq >= 2:
                        E.wait("pe", ("gup", kseq - 2, NU - 1))
                    E.inc(sy.dma_start(
                        out=wgu_int.rearrange("p (c h) -> p c h", c=KC),
                        in_=wgT_in[:, sl * SL:(sl + 1) * SL].rearrange(
                            "(c p) h -> p c h", p=P)), "ld", ("wg", kseq), 16)
                    E.wait("s", ("wgc", kseq))
                    if kseq >= 2:
                        E.wait("pe", ("gup", kseq - 2, NU - 1))
                    E.inc(sy.dma_start(
                        out=wuu_int.rearrange("p (c h) -> p c h", c=KC),
                        in_=wuT_in[:, sl * SL:(sl + 1) * SL].rearrange(
                            "(c p) h -> p c h", p=P)), "ld", ("wu", kseq), 16)
                    PREV_CONV[0] = ("wuc", kseq)
                    if ha == 0:
                        for c in (2 * sl, 2 * sl + 1):
                            E.wait("s", PREV_CONV[0])
                            if c == 0:
                                E.wait("v", ("x2", T - 1))
                                E.wait("v", ("st", T - 1))
                                E.wait("w0", ("x2w", T - 2))
                                E.wait("w1", ("x2w", T - 1))
                            E.inc(sy.dma_start(
                                out=wd_int,
                                in_=wdT_in[c * P:(c + 1) * P, :]),
                                "ld", ("wd", c), 16)
                            PREV_CONV[0] = ("wdc", c)
                for tt in range(tn):
                    tg = t0 + tt
                    E.wait(*buffree[tt % 2])
                    E.inc(sy.dma_start(out=x2r[tt % 2],
                                       in_=x2_hbm[tg * TOK:(tg + 1) * TOK, :]),
                          "x%d" % (tt % 2), ("x2b", ha, tt), 16)
                    buffree[tt % 2] = ("v", ("res", ha, tt))
            E.wait("o0", ("out", 1, T - HT - 2))
            E.wait("o1", ("out", 1, T - HT - 1))

        def body_vector(E):
            v = E.eng
            E.wait("c", "call")
            v.tensor_copy(out=brow[:], in_=wld[0:1, :])
            E.inc(v.tensor_copy(out=borow[:], in_=t1[0:1, 0:D]), "v", "rows")
            E.wait("g", "gbcmm")
            v.tensor_copy(out=gbc[:], in_=pQ[:])
            E.inc(v.tensor_copy(out=bbc[:], in_=pK[:]), "v", "gbcc")
            for c in range(KC):
                E.wait("ld", ("wq", c))
                E.inc(v.tensor_copy(out=wq_bf[:, c * 3 * D:(c + 1) * 3 * D],
                                    in_=wld[:, :]), "v", ("wq", c))
            for c in range(KC):
                E.wait("ld", ("wo", c))
                E.inc(v.tensor_copy(out=wo_bf[:, c * D:(c + 1) * D],
                                    in_=wld[:, 0:D]), "v", ("wo", c))
            E.inc(v.memset(c_eps[:], LN_EPS), "v", "ceps")
            v.memset(phiq_e[:], 0.0)
            E.inc(v.memset(phiq_o[:], 0.0), "v", "phiqz0")
            if fused:
                v.memset(h1_tm[:], 0.0)
                v.memset(h1T[:], 0.0)
                v.memset(phik_tm[:], 0.0)
                E.inc(v.memset(v_tm[:], 0.0), "v", "pz")
            if fused:
                v.memset(srun[:], 0.0)
                E.inc(v.memset(ssum[:], 0.0), "v", "sexp")
            else:
                v.tensor_scalar(out=sini[:], in0=stall[:, 0:528],
                                scalar1=mbc[:, 0:1], scalar2=None,
                                op0=ALU.mult)
                for r in range(1, 8):
                    v.tensor_scalar(out=t1[:, 0:528],
                                    in0=stall[:, r * 528:(r + 1) * 528],
                                    scalar1=mbc[:, r:r + 1], scalar2=None,
                                    op0=ALU.mult)
                    v.tensor_tensor(out=sini[:], in0=sini[:],
                                    in1=t1[:, 0:528], op=ALU.add)
                v.memset(srun[:], 0.0)
                v.tensor_copy(
                    out=srun[0:64, :].rearrange("p (g f) -> p g f", f=P
                                                )[:, :, 0:64],
                    in_=sini[0:64, 0:512].rearrange("p (g f) -> p g f", f=64))
                v.tensor_copy(
                    out=srun[64:128, :].rearrange("p (g f) -> p g f", f=P
                                                  )[:, :, 64:128],
                    in_=sini[64:128, 0:512].rearrange("p (g f) -> p g f",
                                                      f=64))
                E.inc(v.tensor_copy(out=ssum[:], in_=sini[:, 512:528]),
                      "v", "sexp")
            if upto == 'setup':
                return
            if fused:
                for pj in range(NPRE):
                    E.wait("px%d" % (pj % 2), ("px", pj))
                    emit_ln_stats(E, v, stats, mv, xp[pj % 2][:, 0:512],
                                  xp[pj % 2][:, 512:1024], ("pxa", pj))
                    E.wait("s", ("prstd", pj))
                    E.inc(v.tensor_scalar(out=h1_tm[:], in0=xp[pj % 2],
                                          scalar1=mv[:, 0:1],
                                          scalar2=rstd[:],
                                          op0=ALU.subtract, op1=ALU.mult),
                          "v", ("ph1", pj))
                    E.wait("pe", ("pxh1T", pj))
                    E.inc(v.tensor_copy(out=h1T[:], in_=pT[:]),
                          "v", ("ph1T", pj))
                    E.wait("s", ("pkel", pj))
                    E.inc(v.scalar_tensor_tensor(
                        out=phik_tm[:], in0=t2[:], scalar=FEAT_EPS,
                        in1=t3[:], op0=ALU.add, op1=ALU.add),
                        "v", ("pphik", pj))
                    for r in range(2):
                        E.wait("pe", ("pxst", pj, r))
                        ps = pV[:, 512:1024]
                        v.scalar_tensor_tensor(
                            out=srun[0:64, :].rearrange(
                                "p (g f) -> p g f", f=P
                            )[:, 4 * r:4 * r + 4, 0:64],
                            in0=ps[0:64].rearrange("p (g f) -> p g f",
                                                   f=P)[:, :, 0:64],
                            scalar=pflag[0:64, pj:pj + 1],
                            in1=srun[0:64, :].rearrange(
                                "p (g f) -> p g f", f=P
                            )[:, 4 * r:4 * r + 4, 0:64],
                            op0=ALU.mult, op1=ALU.add)
                        E.inc(v.scalar_tensor_tensor(
                            out=srun[64:128, :].rearrange(
                                "p (g f) -> p g f", f=P
                            )[:, 4 * r:4 * r + 4, 64:128],
                            in0=ps[64:128].rearrange("p (g f) -> p g f",
                                                     f=P)[:, :, 64:128],
                            scalar=pflag[64:128, pj:pj + 1],
                            in1=srun[64:128, :].rearrange(
                                "p (g f) -> p g f", f=P
                            )[:, 4 * r:4 * r + 4, 64:128],
                            op0=ALU.mult, op1=ALU.add), "v",
                            ("psadd", pj, r))
                    E.wait("pe", ("pxscol", pj))
                    v.scalar_tensor_tensor(
                        out=ssum[0:64, :].rearrange("p (g o) -> p g o",
                                                    o=1)[:, 0:16:2, :],
                        in0=pD[0:64, 32:40].rearrange("p (g o) -> p g o",
                                                      o=1),
                        scalar=pflag[0:64, pj:pj + 1],
                        in1=ssum[0:64, :].rearrange("p (g o) -> p g o",
                                                    o=1)[:, 0:16:2, :],
                        op0=ALU.mult, op1=ALU.add)
                    E.inc(v.scalar_tensor_tensor(
                        out=ssum[64:128, :].rearrange("p (g o) -> p g o",
                                                      o=1)[:, 1:16:2, :],
                        in0=pD[64:128, 32:40].rearrange(
                            "p (g o) -> p g o", o=1),
                        scalar=pflag[64:128, pj:pj + 1],
                        in1=ssum[64:128, :].rearrange("p (g o) -> p g o",
                                                      o=1)[:, 1:16:2, :],
                        op0=ALU.mult, op1=ALU.add), "v", ("pst", pj))
                E._mark("v", "preend", E.counts.get("v", 0))
            for j in range(T):
                E.wait("x%d" % (j % 2), ("x", j))
                emit_ln_stats(E, v, stats, mv, xt[j % 2][:, 0:512],
                              xt[j % 2][:, 512:1024], ("xa", j))
                E.wait("s", ("xrstd", j))
                E.inc(v.tensor_scalar(out=h1_tm[:], in0=xt[j % 2],
                                      scalar1=mv[:, 0:1], scalar2=rstd[:],
                                      op0=ALU.subtract, op1=ALU.mult),
                      "v", ("h1", j))
                E.wait("pe", ("h1T", j))
                E.inc(v.tensor_copy(out=h1T[:], in_=pT[:]), "v", ("h1T", j))
                if LVL < 2:
                    continue
                if LVL >= 3:
                    if j > 0:
                        E.wait("pe", ("scol", j - 1))
                    v.tensor_copy(out=srun_bf[:], in_=srun[:])
                    E.inc(v.tensor_copy(out=ssum_bf[:], in_=ssum[:]),
                          "v", ("scast", j))
                E.wait("s", ("qel", j))
                E.inc(v.scalar_tensor_tensor(out=phiq[:], in0=t2[:],
                                             scalar=FEAT_EPS, in1=t3[:],
                                             op0=ALU.add, op1=ALU.add),
                      "v", ("phiq", j))
                if LVL >= 3:
                    if j > 0:
                        E.wait("pe", ("at", j - 1, 3))
                    v.tensor_copy(out=phiq_e[0:64, :], in_=phiq[0:64, :])
                    E.inc(v.tensor_copy(out=phiq_o[64:128, :],
                                        in_=phiq[64:128, :]),
                          "v", ("phiqz", j))
                E.wait("s", ("kel", j))
                E.inc(v.scalar_tensor_tensor(out=phik_tm[:], in0=t1[:],
                                             scalar=FEAT_EPS, in1=t3[:],
                                             op0=ALU.add, op1=ALU.add),
                      "v", ("phik", j))
                E.wait("pe", ("kfm", j))
                E.inc(v.tensor_copy(out=phik_fm[:], in_=pT[:]),
                      "v", ("kfm", j))
                if LVL < 3:
                    continue
                if j == 0:
                    E.wait("c", "call")
                for g in range(4):
                    E.wait("pe", ("at", j, g))
                    if g > 0:
                        E.wait("pe", ("avg", j, g - 1))
                    E.inc(v.tensor_tensor(out=am[:], in0=pV[:, 0:512],
                                          in1=maskT4[:], op=ALU.mult),
                          "v", ("mask", j, g))
                E.wait("pe", ("avg", j, 3))
                if j > 0:
                    E.wait("s", ("div", j - 1))
                E.inc(v.tensor_scalar(out=rdpre[:], in0=pD[:, 0:16],
                                      scalar1=DEN_EPS, scalar2=None,
                                      op0=ALU.add), "v")
                E.selfwait("v")
                E.inc(v.reciprocal(out=rden[:], in_=rdpre[:]),
                      "v", ("rden", j))
                E.selfwait("v")
                for h in range(8, 16):
                    ins = v.tensor_scalar(
                        out=attn_bf[:, h * EH:(h + 1) * EH],
                        in0=pK[:, h * EH:(h + 1) * EH],
                        scalar1=rden[:, h:h + 1], scalar2=None, op0=ALU.mult)
                E.inc(ins, "v", ("div", j))
                for r in range(2):
                    E.wait("pe", ("st", j, r))
                    ps = pV[:, 512:1024]
                    v.tensor_tensor(
                        out=srun[0:64, :].rearrange("p (g f) -> p g f", f=P
                                                    )[:, 4 * r:4 * r + 4, 0:64],
                        in0=srun[0:64, :].rearrange("p (g f) -> p g f", f=P
                                                    )[:, 4 * r:4 * r + 4, 0:64],
                        in1=ps[0:64].rearrange("p (g f) -> p g f", f=P
                                               )[:, :, 0:64],
                        op=ALU.add)
                    E.inc(v.tensor_tensor(
                        out=srun[64:128, :].rearrange("p (g f) -> p g f", f=P
                                                      )[:, 4 * r:4 * r + 4,
                                                        64:128],
                        in0=srun[64:128, :].rearrange("p (g f) -> p g f", f=P
                                                      )[:, 4 * r:4 * r + 4,
                                                        64:128],
                        in1=ps[64:128].rearrange("p (g f) -> p g f", f=P
                                                 )[:, :, 64:128],
                        op=ALU.add), "v", ("sadd", j, r))
                E.wait("pe", ("scol", j))
                v.tensor_tensor(
                    out=ssum[0:64, :].rearrange("p (g o) -> p g o", o=1
                                                )[:, 0:16:2, :],
                    in0=ssum[0:64, :].rearrange("p (g o) -> p g o", o=1
                                                )[:, 0:16:2, :],
                    in1=pD[0:64, 32:40].rearrange("p (g o) -> p g o", o=1),
                    op=ALU.add)
                E.inc(v.tensor_tensor(
                    out=ssum[64:128, :].rearrange("p (g o) -> p g o", o=1
                                                  )[:, 1:16:2, :],
                    in0=ssum[64:128, :].rearrange("p (g o) -> p g o", o=1
                                                  )[:, 1:16:2, :],
                    in1=pD[64:128, 32:40].rearrange("p (g o) -> p g o", o=1),
                    op=ALU.add), "v", ("st", j))
                if LVL < 4:
                    continue
                E.wait("pe", ("attnT", j))
                E.inc(v.tensor_copy(out=attnT[:], in_=pT[:]),
                      "v", ("attnT", j))
                E.wait("pe", ("wout", j))
                emit_ln_stats(E, v, stats, mv, pQ[:, 0:512], pQ[:, 512:1024],
                              ("oa", j))
                E.wait("s", ("orstd", j))
                E.inc(v.tensor_scalar(out=o_n[:], in0=pQ[:],
                                      scalar1=mv[:, 0:1], scalar2=rstd[:],
                                      op0=ALU.subtract, op1=ALU.mult),
                      "v", ("onorm", j))
                if LVL < 5:
                    continue
                v.tensor_tensor(out=a_g[:], in0=o_n[:], in1=gbc[:],
                                op=ALU.mult)
                E.inc(v.tensor_tensor(out=a_g[:], in0=a_g[:], in1=bbc[:],
                                      op=ALU.add), "v", ("gb", j))
                emit_ln_stats(E, v, stats, mv, a_g[:, 0:512], a_g[:, 512:1024],
                              ("aa", j))
                E.wait("s", ("arstd", j))
                E.inc(v.tensor_scalar(out=o_n[:], in0=a_g[:],
                                      scalar1=mv[:, 0:1], scalar2=rstd[:],
                                      op0=ALU.subtract, op1=ALU.mult),
                      "v", ("anorm", j))
                if j >= 2:
                    E.wait("w%d" % (j % 2), ("x2w", j - 2))
                elif fused:
                    E.wait("px%d" % (j % 2), ("px", NPRE - 2 + j))
                E.inc(v.tensor_tensor(out=x2sb[j % 2], in0=xt[j % 2],
                                      in1=o_n[:], op=ALU.add),
                      "v", ("x2", j))
            if upto != 'full':
                return
            for ha, t0, tn in halves():
                for tt in range(tn):
                    E.wait("x%d" % (tt % 2), ("x2a", ha, tt))
                    emit_ln_stats(E, v, stats, mv, x2r[tt % 2][:, 0:512],
                                  x2r[tt % 2][:, 512:1024], ("h2a", ha, tt))
                    E.wait("s", ("h2rstd", ha, tt))
                    E.inc(v.tensor_scalar(out=h1_tm[:], in0=x2r[tt % 2],
                                          scalar1=mv[:, 0:1], scalar2=rstd[:],
                                          op0=ALU.subtract, op1=ALU.mult),
                          "v", ("h2", ha, tt))
                    E.wait("pe", ("h2T", ha, tt))
                    E.inc(v.tensor_copy(
                        out=h2T.ap().rearrange("p (c i) -> p c i", c=KC
                                               )[:, :, tt * P:(tt + 1) * P],
                        in_=pT.ap().rearrange("p (c i) -> p c i", c=KC)),
                        "v", ("h2T", ha, tt))
                for sl in range(NSL):
                    kseq = ha * NSL + sl
                    for u in range(NU):
                        pgate, pups = GUP_BANKS[(kseq, u)]
                        E.wait("s", ("silu", kseq, u))
                        for hh in range(2):
                            if len(pups) == 1:
                                upsrc = pups[0][:, hh * 512:(hh + 1) * 512]
                            else:
                                upsrc = pups[hh]
                            ins = v.tensor_tensor(
                                out=hst.rearrange("p (c i) -> p c i", i=HLEN
                                                  )[:, 2 * sl + hh,
                                                    u * 512:(u + 1) * 512],
                                in0=silu_t[:, hh * 512:(hh + 1) * 512],
                                in1=upsrc, op=ALU.mult)
                        E.inc(ins, "v", ("hmul", kseq, u))
                for tt in range(tn):
                    pdn = pQ if tt % 2 == 0 else pK
                    E.wait("pe", ("down", ha, tt))
                    emit_ln_stats(E, v, stats, mv, pdn[:, 0:512],
                                  pdn[:, 512:1024], ("fa", ha, tt))
                    E.wait("s", ("frstd", ha, tt))
                    E.inc(v.tensor_scalar(out=f_n[:], in0=pdn[:],
                                          scalar1=mv[:, 0:1], scalar2=rstd[:],
                                          op0=ALU.subtract, op1=ALU.mult),
                          "v", ("fnorm", ha, tt))
                    E.wait("x%d" % (tt % 2), ("x2b", ha, tt))
                    if (ha, tt) == (0, 0):
                        E.wait("w1", ("x2w", T - 1))
                        E.wait("w0", ("x2w", T - 2))
                    if tt >= 2:
                        E.wait("o%d" % (tt % 2), ("out", ha, tt - 2))
                    elif ha == 1:
                        E.wait("o%d" % (tt % 2), ("out", 0, HT - 2 + tt))
                    E.inc(v.tensor_tensor(out=outsb[tt % 2],
                                          in0=x2r[tt % 2], in1=f_n[:],
                                          op=ALU.add), "v", ("res", ha, tt))

        def body_scalar(E):
            s = E.eng
            E.wait("v", "ceps")
            if upto == 'setup':
                return
            if fused:
                for pj in range(NPRE):
                    E.wait("v", ("pxa", pj))
                    E.inc(emit_rstd(E, s, mv, rstd, c_eps[:]),
                          "s", ("prstd", pj))
                    E.wait("pe", ("pxk", pj))
                    if pj > 0:
                        E.wait("v", ("pphik", pj - 1))
                    s.activation(out=t1[:], in_=pK[:], func=AF.Relu,
                                 scale=-1.0)
                    s.activation(out=t2[:], in_=t1[:], func=AF.Exp,
                                 scale=-1.0)
                    E.inc(s.activation(out=t3[:], in_=pK[:],
                                       func=AF.Relu), "s", ("pkel", pj))
                    E.wait("pe", ("pxv", pj))
                    if pj > 0:
                        E.wait("pe", ("pxst", pj - 1, 1))
                    E.inc(s.activation(out=v_tm[:], in_=pV[:],
                                       func=AF.Copy), "s", ("pv", pj))
                E._mark("s", "preend", E.counts.get("s", 0))
            for j in range(T):
                E.wait("v", ("xa", j))
                if j > 0 and LVL >= 5:
                    E.wait("v", ("anorm", j - 1))
                E.inc(emit_rstd(E, s, mv, rstd, c_eps[:]), "s", ("xrstd", j))
                if LVL < 2:
                    continue
                E.wait("pe", ("q", j))
                if j > 0:
                    E.wait("v", ("phik", j - 1))
                elif fused:
                    E.wait("v", "preend")
                s.activation(out=t1[:], in_=pQ[:], func=AF.Relu, scale=-1.0)
                s.activation(out=t2[:], in_=t1[:], func=AF.Exp, scale=-1.0)
                E.inc(s.activation(out=t3[:], in_=pQ[:], func=AF.Relu),
                      "s", ("qel", j))
                E.wait("pe", ("k", j))
                E.wait("v", ("phiq", j))
                s.activation(out=t2[:], in_=pK[:], func=AF.Relu, scale=-1.0)
                s.activation(out=t1[:], in_=t2[:], func=AF.Exp, scale=-1.0)
                E.inc(s.activation(out=t3[:], in_=pK[:], func=AF.Relu),
                      "s", ("kel", j))
                E.wait("pe", ("v", j))
                E.inc(s.activation(out=v_tm[:], in_=pV[:], func=AF.Copy),
                      "s", ("v", j))
                if LVL < 3:
                    continue
                E.wait("v", ("rden", j))
                for h in range(8):
                    ins = s.activation(out=attn_bf[:, h * EH:(h + 1) * EH],
                                       in_=pK[:, h * EH:(h + 1) * EH],
                                       func=AF.Copy, scale=rden[:, h:h + 1])
                E.inc(ins, "s", ("div", j))
                if LVL < 4:
                    continue
                E.wait("v", ("oa", j))
                E.inc(emit_rstd(E, s, mv, rstd, c_eps[:]), "s", ("orstd", j))
                if LVL < 5:
                    continue
                E.wait("v", ("aa", j))
                E.inc(emit_rstd(E, s, mv, rstd, c_eps[:]), "s", ("arstd", j))
                E.wait("v", ("x2", j))
                E.inc(s.dma_start(out=x2_hbm[j * TOK:(j + 1) * TOK, :],
                                  in_=x2sb[j % 2]), "w%d" % (j % 2),
                      ("x2w", j), 16)
            if upto != 'full':
                return
            def emit_silu(kq, u):
                pgate, _pups = GUP_BANKS[(kq, u)]
                E.wait("pe", ("gup", kq, u))
                kk = KSEQ_OF[(kq, u)]
                if kk >= 1:
                    E.wait("v", ("hmul",) + UNIT_LIST[kk - 1])
                E.inc(s.activation(out=silu_t[:], in_=pgate, func=AF.Silu),
                      "s", ("silu", kq, u))

            for ha, t0, tn in halves():
                for tt in range(tn):
                    E.wait("v", ("h2a", ha, tt))
                    E.inc(emit_rstd(E, s, mv, rstd, c_eps[:]),
                          "s", ("h2rstd", ha, tt))
                for sl in range(NSL):
                    kseq = ha * NSL + sl
                    E.wait("ld", ("wg", kseq))
                    E.inc(s.activation(out=wg_sl[kseq % 2], in_=wgu_int,
                                       func=AF.Copy, scale=WSCALE),
                          "s", ("wgc", kseq))
                    E.wait("ld", ("wu", kseq))
                    E.inc(s.activation(out=wu_sl[kseq % 2], in_=wuu_int,
                                       func=AF.Copy, scale=WSCALE),
                          "s", ("wuc", kseq))
                    if ha == 0:
                        for c in (2 * sl, 2 * sl + 1):
                            E.wait("ld", ("wd", c))
                            E.inc(s.activation(out=wd_bf[c], in_=wd_int,
                                               func=AF.Copy, scale=WSCALE),
                                  "s", ("wdc", c))
                    if sl > 0:
                        for u in range(NU):
                            emit_silu(ha * NSL + sl - 1, u)
                for u in range(NU):
                    emit_silu(ha * NSL + NSL - 1, u)
                for tt in range(tn):
                    E.wait("v", ("fa", ha, tt))
                    E.inc(emit_rstd(E, s, mv, rstd, c_eps[:]),
                          "s", ("frstd", ha, tt))
                    E.wait("v", ("res", ha, tt))
                    E.inc(s.dma_start(
                        out=out[(t0 + tt) * TOK:(t0 + tt + 1) * TOK, :],
                        in_=outsb[tt % 2]), "o%d" % (tt % 2),
                      ("out", ha, tt), 16)

        def body_tensor(E):
            t = E.eng
            E.wait("c", "call")
            for n4 in range(2):
                t.matmul(pQ[:, n4 * 512:(n4 + 1) * 512], onesf[:],
                         t2[0:1, n4 * 512:(n4 + 1) * 512],
                         start=True, stop=True)
            for n4 in range(2):
                ins = t.matmul(pK[:, n4 * 512:(n4 + 1) * 512], onesf[:],
                               t3[0:1, n4 * 512:(n4 + 1) * 512],
                               start=True, stop=True)
            E.inc(ins, "g", "gbcmm")
            if upto == 'setup':
                return
            if fused:
                for pj in range(NPRE):
                    E.wait("v", ("ph1", pj))
                    if pj == 0:
                        E.wait("c", "call")
                    else:
                        E.wait("v", ("ph1T", pj - 1))
                    for c in range(KC):
                        ins = t.transpose(pT[:, c * P:(c + 1) * P],
                                          h1_tm[:, c * P:(c + 1) * P],
                                          ident[:])
                    E.inc(ins, "pe", ("pxh1T", pj))
                    for half, pdst, name in ((0, pK, "pxk"), (1, pV, "pxv")):
                        E.wait("v", ("ph1T", pj))
                        if pj == 0:
                            E.wait("v", ("wq", KC - 1))
                            E.wait("v", "gbcc")
                        else:
                            if half == 0:
                                E.wait("s", ("pkel", pj - 1))
                            else:
                                E.wait("s", ("pv", pj - 1))
                                E.wait("v", ("psadd", pj - 1, 1))
                        for n4 in range(2):
                            for c in range(KC):
                                t.matmul(
                                    pdst[:, n4 * 512:(n4 + 1) * 512],
                                    h1T[:, c * P:(c + 1) * P],
                                    wq_bf[:, c * 3 * D + D + half * D +
                                          n4 * 512:
                                          c * 3 * D + D + half * D +
                                          (n4 + 1) * 512],
                                    start=(c == 0), stop=False)
                            ins = t.matmul(
                                pdst[:, n4 * 512:(n4 + 1) * 512],
                                onesrow[:],
                                brow[0:1, D + half * D + n4 * 512:
                                     D + half * D + (n4 + 1) * 512],
                                start=False, stop=True)
                        E.inc(ins, "pe", (name, pj))
                    for r in range(2):
                        E.wait("v", ("pphik", pj))
                        E.wait("s", ("pv", pj))
                        if r == 0:
                            if pj > 0:
                                E.wait("v", ("psadd", pj - 1, 1))
                        else:
                            E.wait("v", ("psadd", pj, 0))
                        for pp in range(4):
                            pr = r * 4 + pp
                            ins = t.matmul(
                                pV[:, 512 + pp * P:512 + (pp + 1) * P],
                                phik_tm[:, pr * P:(pr + 1) * P],
                                v_tm[:, pr * P:(pr + 1) * P],
                                start=True, stop=True)
                        E.inc(ins, "pe", ("pxst", pj, r))
                    if pj > 0:
                        E.wait("v", ("pst", pj - 1))
                    E.wait("v", ("pphik", pj))
                    for pr in range(NPAIR):
                        ins = t.matmul(pD[:, 32 + pr:33 + pr],
                                       phik_tm[:, pr * P:(pr + 1) * P],
                                       onescol[:], start=True, stop=True)
                    E.inc(ins, "pe", ("pxscol", pj))
                E._mark("pe", "preend", E.counts.get("pe", 0))
            for j in range(T):
                E.wait("v", ("h1", j))
                if j > 0:
                    E.wait("v", ("attnT", j - 1) if LVL >= 4 else
                           (("kfm", j - 1) if LVL >= 2 else ("h1T", j - 1)))
                elif fused:
                    E.wait("v", "preend")
                for c in range(KC):
                    ins = t.transpose(pT[:, c * P:(c + 1) * P],
                                      h1_tm[:, c * P:(c + 1) * P], ident[:])
                E.inc(ins, "pe", ("h1T", j))
                if LVL < 2:
                    continue
                E.wait("v", ("h1T", j))
                if j == 0:
                    E.wait("v", ("wq", KC - 1))
                    E.wait("v", "gbcc")
                elif LVL >= 4:
                    E.wait("v", ("onorm", j - 1))
                    E.wait("v", ("oa", j - 1))
                else:
                    E.wait("s", ("qel", j - 1))
                for ec in range(KC):
                    for c in range(KC):
                        t.matmul(pQ[:, ec * P:(ec + 1) * P],
                                 wq_bf[:, c * 3 * D + ec * P:
                                       c * 3 * D + (ec + 1) * P],
                                 h1T[:, c * P:(c + 1) * P],
                                 start=(c == 0), stop=False)
                    ins = t.matmul(pQ[:, ec * P:(ec + 1) * P],
                                   brow[0:1, ec * P:(ec + 1) * P],
                                   onesrow[:], start=False, stop=True)
                E.inc(ins, "pe", ("q", j))
                for half, pdst, name in ((0, pK, "k"), (1, pV, "v")):
                    if j == 0 and fused:
                        if half == 0:
                            E.wait("s", "preend")
                        else:
                            E.wait("s", "preend")
                            E.wait("v", "preend")
                    if j > 0:
                        if half == 0:
                            if LVL >= 3:
                                E.wait("s", ("div", j - 1))
                                E.wait("v", ("div", j - 1))
                            else:
                                E.wait("s", ("kel", j - 1))
                        else:
                            E.wait("s", ("v", j - 1))
                            if LVL >= 3:
                                E.wait("v", ("sadd", j - 1, 1))
                                E.wait("v", ("mask", j - 1, 3))
                    for n4 in range(2):
                        for c in range(KC):
                            t.matmul(pdst[:, n4 * 512:(n4 + 1) * 512],
                                     h1T[:, c * P:(c + 1) * P],
                                     wq_bf[:, c * 3 * D + D + half * D + n4 * 512:
                                           c * 3 * D + D + half * D +
                                           (n4 + 1) * 512],
                                     start=(c == 0), stop=False)
                        ins = t.matmul(pdst[:, n4 * 512:(n4 + 1) * 512],
                                       onesrow[:],
                                       brow[0:1, D + half * D + n4 * 512:
                                            D + half * D + (n4 + 1) * 512],
                                       start=False, stop=True)
                    E.inc(ins, "pe", (name, j))
                E.wait("v", ("phik", j))
                E.wait("v", ("h1T", j))
                for c in range(KC):
                    ins = t.transpose(pT[:, c * P:(c + 1) * P],
                                      phik_tm[:, c * P:(c + 1) * P], ident[:])
                E.inc(ins, "pe", ("kfm", j))
                if LVL < 3:
                    continue
                E.wait("v", ("kfm", j))
                E.wait("v", ("phiq", j))
                E.wait("s", ("v", j))
                E.wait("v", ("scast", j))
                if j > 0:
                    E.wait("v", ("st", j - 1))
                    E.wait("v", ("rden", j - 1))
                elif fused:
                    E.wait("v", "preend")
                E.wait("v", ("phiqz", j))
                for g in range(4):
                    if g > 0:
                        E.wait("v", ("mask", j, g - 1))
                    for hh in range(4):
                        h = g * 4 + hh
                        pr, hlf = h // 2, h % 2
                        ins = t.matmul(
                            pV[:, hh * P:(hh + 1) * P],
                            phik_fm[:, pr * P:(pr + 1) * P],
                            (phiq_e if hlf == 0 else phiq_o)[:, pr * P:(pr + 1) * P],
                            start=True, stop=True)
                    E.inc(ins, "pe", ("at", j, g))
                    E.wait("v", ("mask", j, g))
                    for pp in range(2):
                        pr = 2 * g + pp
                        t.matmul(pD[:, 2 * pr:2 * pr + 2],
                                 phiq[:, pr * P:(pr + 1) * P],
                                 ssum_bf[:, 2 * pr:2 * pr + 2],
                                 start=True, stop=False)
                        t.matmul(pD[:, 2 * pr:2 * pr + 1],
                                 am[:, 2 * pp * P:(2 * pp + 1) * P],
                                 onescol[:], start=False, stop=False)
                        ins = t.matmul(pD[:, 2 * pr + 1:2 * pr + 2],
                                       am[:, (2 * pp + 1) * P:(2 * pp + 2) * P],
                                       onescol[:], start=False, stop=True)
                        t.matmul(pK[:, pr * P:(pr + 1) * P],
                                 phiq[:, pr * P:(pr + 1) * P],
                                 srun_bf[:, pr * P:(pr + 1) * P],
                                 start=True, stop=False)
                        for sub in range(2):
                            h = 2 * pr + sub
                            ins = t.matmul(pK[:, h * EH:(h + 1) * EH],
                                           am[:, (2 * pp + sub) * P:
                                              (2 * pp + sub + 1) * P],
                                           v_tm[:, h * EH:(h + 1) * EH],
                                           start=False, stop=(sub == 1))
                    E.inc(ins, "pe", ("avg", j, g))
                for r in range(2):
                    if r == 0:
                        E.wait("v", ("mask", j, 3))
                        if j > 0:
                            E.wait("v", ("sadd", j - 1, 1))
                        elif fused:
                            E.wait("v", "preend")
                    else:
                        E.wait("v", ("sadd", j, 0))
                    for pp in range(4):
                        pr = r * 4 + pp
                        ins = t.matmul(pV[:, 512 + pp * P:512 + (pp + 1) * P],
                                       phik_tm[:, pr * P:(pr + 1) * P],
                                       v_tm[:, pr * P:(pr + 1) * P],
                                       start=True, stop=True)
                    E.inc(ins, "pe", ("st", j, r))
                E.wait("v", ("rden", j))
                if j > 0:
                    E.wait("v", ("st", j - 1))
                elif fused:
                    E.wait("v", "preend")
                for pr in range(NPAIR):
                    ins = t.matmul(pD[:, 32 + pr:33 + pr],
                                   phik_tm[:, pr * P:(pr + 1) * P],
                                   onescol[:], start=True, stop=True)
                E.inc(ins, "pe", ("scol", j))
                if LVL < 4:
                    continue
                E.wait("s", ("div", j))
                E.wait("v", ("div", j))
                E.wait("v", ("kfm", j))
                for c in range(KC):
                    ins = t.transpose(pT[:, c * P:(c + 1) * P],
                                      attn_bf[:, c * P:(c + 1) * P], ident[:])
                E.inc(ins, "pe", ("attnT", j))
                E.wait("v", ("attnT", j))
                if j == 0:
                    E.wait("v", ("wo", KC - 1))
                E.wait("s", ("qel", j))
                for n4 in range(2):
                    for c in range(KC):
                        t.matmul(pQ[:, n4 * 512:(n4 + 1) * 512],
                                 attnT[:, c * P:(c + 1) * P],
                                 wo_bf[:, c * D + n4 * 512:
                                       c * D + (n4 + 1) * 512],
                                 start=(c == 0), stop=False)
                    ins = t.matmul(pQ[:, n4 * 512:(n4 + 1) * 512],
                                   onesrow[:],
                                   borow[0:1, n4 * 512:(n4 + 1) * 512],
                                   start=False, stop=True)
                E.inc(ins, "pe", ("wout", j))
            if upto != 'full':
                return
            for ha, t0, tn in halves():
                for tt in range(tn):
                    E.wait("v", ("h2", ha, tt))
                    if (ha, tt) == (0, 0):
                        E.wait("v", ("attnT", T - 1))
                    elif ha == 1 and tt == 0:
                        E.wait("v", ("hmul",) + H0_LAST_ODD)
                        E.wait("s", ("silu",) + H0_LAST_ODD)
                    else:
                        E.wait("v", ("h2T", ha, tt - 1))
                    for c in range(KC):
                        ins = t.transpose(pT[:, c * P:(c + 1) * P],
                                          h1_tm[:, c * P:(c + 1) * P],
                                          ident[:])
                    E.inc(ins, "pe", ("h2T", ha, tt))
                for sl in range(NSL):
                    kseq = ha * NSL + sl
                    E.wait("s", ("wgc", kseq))
                    E.wait("s", ("wuc", kseq))
                    for u in range(NU):
                        pgate, pups = GUP_BANKS[(kseq, u)]
                        for tc in range(u * 4, min(u * 4 + 4, tn)):
                            E.wait("v", ("h2T", ha, tc))
                        kk = KSEQ_OF[(kseq, u)]
                        if kk >= 2:
                            pku = UNIT_LIST[kk - 2]
                            E.wait("v", ("hmul",) + pku)
                            E.wait("s", ("silu",) + pku)
                        if ha == 1 and kseq == NSL and u < 2:
                            E.wait("v", ("fnorm", 0, HT - 2 + u))
                        h2v = h2T.ap().rearrange("p (c i) -> p c i", c=KC)
                        for hh in range(2):
                            dst = pgate[:, hh * 512:(hh + 1) * 512]
                            for c in range(KC):
                                t.matmul(dst,
                                         wg_sl[kseq % 2][:, c * SL + hh * P:
                                                         c * SL + (hh + 1) * P],
                                         h2v[:, c, u * 512:(u + 1) * 512],
                                         start=(c == 0), stop=(c == KC - 1))
                        for hh in range(2):
                            if len(pups) == 1:
                                udst = pups[0][:, hh * 512:(hh + 1) * 512]
                            else:
                                udst = pups[hh]
                            for c in range(KC):
                                ins = t.matmul(
                                    udst,
                                    wu_sl[kseq % 2][:, c * SL + hh * P:
                                                    c * SL + (hh + 1) * P],
                                    h2v[:, c, u * 512:(u + 1) * 512],
                                    start=(c == 0), stop=(c == KC - 1))
                        E.inc(ins, "pe", ("gup", kseq, u))
                for tt in range(tn):
                    pdn = pQ if tt % 2 == 0 else pK
                    lastk = ha * NSL + NSL - 1
                    E.wait("v", ("hmul", lastk, tt // 4))
                    E.wait("s", ("silu", lastk, tt // 4))
                    if tt == 0:
                        E.wait("s", ("wdc", HC - 1))
                    if tt >= 2:
                        E.wait("v", ("fnorm", ha, tt - 2))
                    hstv = hst.rearrange("p (c i) -> p c i", i=HLEN)
                    for n4 in range(2):
                        for c in range(HC):
                            ins = t.matmul(
                                pdn[:, n4 * 512:(n4 + 1) * 512],
                                hstv[:, c, tt * P:(tt + 1) * P],
                                wd_bf[c][:, n4 * 512:(n4 + 1) * 512],
                                start=(c == 0), stop=(c == HC - 1))
                    E.inc(ins, "pe", ("down", ha, tt))

        run_bodies([
            (block.sync, body_sync, None),
            (block.vector, body_vector, "v"),
            (block.scalar, body_scalar, "s"),
            (block.tensor, body_tensor, None),
        ], sems)
    return nc




def _consts():
    maskT = np.triu(np.ones((P, P), dtype=np.float32))
    return {
        "maskT4": np.ascontiguousarray(np.tile(maskT, (1, 4))),
        "ident": np.eye(P, dtype=BF),
        "onescol": np.ones((P, 1), dtype=BF),
        "onesrow": np.ones((1, P), dtype=BF),
        "onesf": np.ones((1, P), dtype=np.float32),
    }


def kernel(L0_attn_x, L0_attn_Wqkv, L0_attn_bqkv, L0_attn_Wout, L0_attn_bout,
           L0_attn_gamma, L0_attn_beta, L0_ffn_w_gate, L0_ffn_w_up,
           L0_ffn_w_down):
    x = np.ascontiguousarray(np.asarray(L0_attn_x, dtype=np.float32)
                             ).reshape(2 * SEQ_LEN, D)
    wqkvT = np.ascontiguousarray(np.asarray(L0_attn_Wqkv, dtype=np.float32).T)
    woutT = np.ascontiguousarray(np.asarray(L0_attn_Wout, dtype=np.float32).T)
    bqkv = np.ascontiguousarray(
        np.asarray(L0_attn_bqkv, dtype=np.float32).reshape(1, 3 * D))
    bout = np.ascontiguousarray(
        np.asarray(L0_attn_bout, dtype=np.float32).reshape(1, D))
    gamma = np.ascontiguousarray(
        np.asarray(L0_attn_gamma, dtype=np.float32).reshape(1, D))
    beta = np.ascontiguousarray(
        np.asarray(L0_attn_beta, dtype=np.float32).reshape(1, D))
    wgT = np.ascontiguousarray(np.asarray(L0_ffn_w_gate, dtype=np.int32).T)
    wuT = np.ascontiguousarray(np.asarray(L0_ffn_w_up, dtype=np.int32).T)
    wdT = np.ascontiguousarray(np.asarray(L0_ffn_w_down, dtype=np.int32).T)
    consts = _consts()

    NT = T_MAIN * TOK
    xs = [np.ascontiguousarray(x[c * NT:(c + 1) * NT]) for c in range(N_CORES)]

    nc_b = build_main(T_MAIN, fused=True)
    in_b = []
    for c in range(N_CORES):
        seq = c // 4
        xseq = x[seq * SEQ_LEN:(seq + 1) * SEQ_LEN]
        npre = (c % 4) * T_MAIN
        pf = np.zeros((P, 3 * T_MAIN), dtype=np.float32)
        pf[:, :npre] = 1.0
        in_b.append({"x_main": xs[c], "x_pre": np.ascontiguousarray(xseq[0:3 * NT]),
                     "pflag": pf,
                     "wqkvT": wqkvT, "woutT": woutT, "bqkv": bqkv,
                     "bout": bout, "gamma": gamma, "beta": beta, "wgT": wgT,
                     "wuT": wuT, "wdT": wdT, **consts})
    res_b = run_bass_kernel_spmd(nc_b, in_b, core_ids=list(range(N_CORES)))
    outs = [np.asarray(res_b.results[c]["out"]) for c in range(N_CORES)]
    return np.concatenate(outs, axis=0).reshape(2, SEQ_LEN, D)


# version 12
# speedup vs baseline: 1.6615x; 1.0315x over previous
from contextlib import ExitStack, contextmanager

import numpy as np
import ml_dtypes
import concourse.bass as bass
import concourse.mybir as mybir
from concourse.bass_utils import run_bass_kernel_spmd

F32 = mybir.dt.float32
BF16 = mybir.dt.bfloat16
I32 = mybir.dt.int32
AF = mybir.ActivationFunctionType
ALU = mybir.AluOpType

D = 1024
HID = 3072
NPAIR = 8
EH = 64
P = 128
KC = D // P
HC = HID // P
LN_EPS = 1e-5
FEAT_EPS = 1e-4
DEN_EPS = 8.0 * FEAT_EPS
WSCALE = 1.0 / 127.0

N_CORES = 8
SEQ_LEN = 8192
T_MAIN = 16
TOK = P

BF = ml_dtypes.bfloat16




class _MockInstr:
    def then_inc(self, *a, **k):
        return self

    def __getattr__(self, name):
        return lambda *a, **k: self


class _MockEng:
    def __getattr__(self, name):
        return lambda *a, **k: _MockInstr()


_EDGE_OPS = frozenset((
    "activation", "add", "mul", "copy", "square", "sqrt", "sign", "memzero",
    "memset", "tensor_copy", "tensor_tensor", "tensor_add", "tensor_sub",
    "tensor_mul", "tensor_max", "tensor_scalar", "tensor_single_scalar",
    "tensor_scalar_add", "tensor_scalar_sub", "tensor_scalar_mul",
    "tensor_scalar_max", "tensor_scalar_min", "tensor_relu", "tensor_reduce",
    "reduce_max", "reduce_sum", "scalar_tensor_tensor", "tensor_tensor_scan",
    "bn_stats", "bn_aggr", "reciprocal", "select", "copy_predicated",
    "iota", "pool", "pool_avg", "pool_max", "stream_shuffle", "engine_nop",
))


class _Edged:

    def __init__(self, env, eng, sem_name):
        self._env = env
        self._eng = eng
        self._sem = sem_name

    def __getattr__(self, name):
        env = self._env
        if name in _EDGE_OPS:
            def f(*a, **k):
                c = env.counts.get(self._sem, 0)
                if env.emit:
                    if c > 0:
                        self._eng.wait_ge(env.sems[self._sem], c)
                    instr = getattr(self._eng, name)(*a, **k)
                    instr.then_inc(env.sems[self._sem], 1)
                else:
                    instr = _MockInstr()
                env.counts[self._sem] = c + 1
                return instr
            return f
        if not env.emit:
            return lambda *a, **k: _MockInstr()
        return getattr(self._eng, name)


class Env:

    AUTO = frozenset(("v", "s", "p"))

    def __init__(self, plan, emit, eng, sems, auto_sem=None):
        self.plan = plan
        self.counts = {}
        self.emit = emit
        self.sems = sems
        self.raw_eng = eng
        if auto_sem is not None:
            self.eng = _Edged(self, eng, auto_sem)
        else:
            self.eng = eng if emit else _MockEng()

    def _bump(self, sem, key, n):
        c = self.counts.get(sem, 0) + n
        self.counts[sem] = c
        if key is not None:
            self._mark(sem, key, c)
        return c

    def _mark(self, sem, key, c):
        if not self.emit:
            assert (sem, key) not in self.plan, ("dup mark", sem, key)
            self.plan[(sem, key)] = c
        else:
            assert self.plan.get((sem, key)) == c, (
                "plan/emit drift", sem, key, self.plan.get((sem, key)), c)

    def inc(self, instr, sem, key=None, n=1):
        if sem in self.AUTO:
            if key is not None:
                self._mark(sem, key, self.counts.get(sem, 0))
            return instr
        self._bump(sem, key, n)
        if self.emit:
            instr.then_inc(self.sems[sem], n)
        return instr

    def wait(self, sem, key):
        if self.emit:
            eng = self.raw_eng if self.raw_eng is not None else None
            eng.wait_ge(self.sems[sem], self.plan[(sem, key)])

    def selfwait(self, sem):
        if self.emit:
            self.raw_eng.wait_ge(self.sems[sem], self.counts[sem])

    def wait_val(self, sem, val):
        if self.emit:
            self.raw_eng.wait_ge(self.sems[sem], val)

    @contextmanager
    def gated(self, reg, j):
        if not self.emit:
            yield
            return
        before = dict(self.counts)
        with self.raw_eng.If_cmp(reg, j, "IS_GT"):
            yield
        with self.raw_eng.Else():
            for sem, c in self.counts.items():
                d = c - before.get(sem, 0)
                if d:
                    self.raw_eng.sem_inc(self.sems[sem], d)


def run_bodies(bodies, sems):
    plan = {}
    for item in bodies:
        attach, body, auto = item
        body(Env(plan, False, None, sems, auto))
    for item in bodies:
        attach, body, auto = item
        attach(lambda eng, body=body, auto=auto: body(
            Env(plan, True, eng, sems, auto)))
    return plan




def emit_ln_stats(E, v, stats, mv, src_lo, src_hi, key):
    v.bn_stats(out=stats[:, 0:6], in_=src_lo)
    E.inc(v.bn_stats(out=stats[:, 6:12], in_=src_hi), "v")
    E.selfwait("v")
    E.inc(v.bn_aggr(out=mv[:], in_=stats[:, 0:12]), "v", key)
    E.selfwait("v")


def emit_rstd(E, s, mv, rstd, eps_ap):
    s.activation(out=rstd[:], in_=mv[:, 1:2], func=AF.Ln, bias=eps_ap)
    return s.activation(out=rstd[:], in_=rstd[:], func=AF.Exp, scale=-0.5)


def sadd_views(dst, src, r, blk_src):
    raise NotImplementedError




def build_state(T):
    nc = bass.Bass("TRN2", target_bir_lowering=False)
    NT = T * TOK
    x_in = nc.dram_tensor("x_own", [NT, D], F32, kind="ExternalInput").ap()
    wqkvT = nc.dram_tensor("wqkvT", [D, 3 * D], F32, kind="ExternalInput").ap()
    bqkv = nc.dram_tensor("bqkv", [1, 3 * D], F32, kind="ExternalInput").ap()
    ident_in = nc.dram_tensor("ident", [P, P], BF16, kind="ExternalInput").ap()
    onescol_in = nc.dram_tensor("onescol", [P, 1], BF16, kind="ExternalInput").ap()
    onesrow_in = nc.dram_tensor("onesrow", [1, P], BF16, kind="ExternalInput").ap()
    stot_out = nc.dram_tensor("stot", [P, 528], F32, kind="ExternalOutput").ap()

    with ExitStack() as ctx:
        ident = ctx.enter_context(nc.sbuf_tensor("identS", [P, P], BF16))
        onescol = ctx.enter_context(nc.sbuf_tensor("onescolS", [P, 1], BF16))
        onesrow = ctx.enter_context(nc.sbuf_tensor("onesrowS", [1, P], BF16))
        brow = ctx.enter_context(nc.sbuf_tensor("brow", [1, 3 * D], BF16))
        wkv_bf = ctx.enter_context(nc.sbuf_tensor("wkv_bf", [P, KC * 2 * D], BF16))
        wld_t = ctx.enter_context(nc.sbuf_tensor("wld", [P, 2 * 3 * D], F32))
        xt_t = ctx.enter_context(nc.sbuf_tensor("xt", [P, 2 * D], F32))
        h1_tm = ctx.enter_context(nc.sbuf_tensor("h1_tm", [P, D], BF16))
        h1T = ctx.enter_context(nc.sbuf_tensor("h1T", [P, D], BF16))
        phik_tm = ctx.enter_context(nc.sbuf_tensor("phik_tm", [P, D], BF16))
        v_tm = ctx.enter_context(nc.sbuf_tensor("v_tm", [P, D], BF16))
        t1 = ctx.enter_context(nc.sbuf_tensor("t1", [P, D], F32))
        t2 = ctx.enter_context(nc.sbuf_tensor("t2", [P, D], F32))
        t3 = ctx.enter_context(nc.sbuf_tensor("t3", [P, D], F32))
        stats = ctx.enter_context(nc.sbuf_tensor("stats", [P, 12], F32))
        mv = ctx.enter_context(nc.sbuf_tensor("mv", [P, 2], F32))
        rstd = ctx.enter_context(nc.sbuf_tensor("rstd", [P, 1], F32))
        c_eps = ctx.enter_context(nc.sbuf_tensor("c_eps", [P, 1], F32))
        sacc = ctx.enter_context(nc.sbuf_tensor("sacc", [P, 528], F32))
        pT = ctx.enter_context(nc.psum_tensor("pT", [P, 1024], BF16))
        pK = ctx.enter_context(nc.psum_tensor("pK", [P, 1024], F32))
        pV = ctx.enter_context(nc.psum_tensor("pV", [P, 1024], F32))
        pS = ctx.enter_context(nc.psum_tensor("pS", [P, 512], F32))
        pC = ctx.enter_context(nc.psum_tensor("pC", [P, 512], F32))
        h_x0 = ctx.enter_context(nc.semaphore("m_x0"))
        h_x1 = ctx.enter_context(nc.semaphore("m_x1"))
        h_c = ctx.enter_context(nc.semaphore("m_c"))
        h_ld = ctx.enter_context(nc.semaphore("m_ld"))
        h_pe = ctx.enter_context(nc.semaphore("m_pe"))
        h_v = ctx.enter_context(nc.semaphore("m_v"))
        h_s = ctx.enter_context(nc.semaphore("m_s"))
        h_out = ctx.enter_context(nc.semaphore("m_out"))
        block = ctx.enter_context(nc.Block())

        wld = [wld_t.ap()[:, 0:3 * D], wld_t.ap()[:, 3 * D:6 * D]]
        xt = [xt_t.ap()[:, 0:D], xt_t.ap()[:, D:2 * D]]
        sems = {"x0": h_x0, "x1": h_x1, "c": h_c, "ld": h_ld, "pe": h_pe,
                "v": h_v, "s": h_s, "out": h_out}

        def body_sync(E):
            sy = E.eng
            for name, dst, srcap in (
                ("ident", ident[:], ident_in),
                ("onescol", onescol[:], onescol_in),
                ("onesrow", onesrow[:], onesrow_in),
            ):
                E.inc(sy.dma_start(out=dst, in_=srcap), "c", name, 16)
            E.inc(sy.dma_start(out=wld[0][0:1, :], in_=bqkv), "c", "call", 16)
            for c in range(KC):
                E.wait("v", "bq" if c == 0 else ("wcv", c - 1))
                E.inc(sy.dma_start(out=wld[0],
                                   in_=wqkvT[c * P:(c + 1) * P, :]),
                      "ld", ("w", c), 16)
            for j in range(T):
                if j >= 2:
                    E.wait("pe", ("h1T", j - 2))
                E.inc(sy.dma_start(out=xt[j % 2],
                                   in_=x_in[j * TOK:(j + 1) * TOK, :]),
                      "x%d" % (j % 2), ("x", j), 16)
            E.wait_val("out", 16)

        def body_vector(E):
            v = E.eng
            E.wait("c", "call")
            E.inc(v.tensor_copy(out=brow[:], in_=wld[0][0:1, :]), "v", "bq")
            for c in range(KC):
                E.wait("ld", ("w", c))
                E.inc(v.tensor_copy(out=wkv_bf[:, c * 2 * D:(c + 1) * 2 * D],
                                    in_=wld[0][:, D:3 * D]),
                      "v", ("wcv", c))
            E.inc(v.memset(sacc[:], 0.0), "v", "sz")
            E.inc(v.memset(c_eps[:], LN_EPS), "v", "ceps")
            for j in range(T):
                E.wait("x%d" % (j % 2), ("x", j))
                emit_ln_stats(E, v, stats, mv, xt[j % 2][:, 0:512],
                              xt[j % 2][:, 512:1024], ("xa", j))
                E.wait("s", ("rstd", j))
                E.inc(v.tensor_scalar(out=h1_tm[:], in0=xt[j % 2],
                                      scalar1=mv[:, 0:1], scalar2=rstd[:],
                                      op0=ALU.subtract, op1=ALU.mult),
                      "v", ("h1", j))
                E.wait("pe", ("h1T", j))
                E.inc(v.tensor_copy(out=h1T[:], in_=pT[:]), "v", ("h1T", j))
                E.wait("s", ("kel", j))
                E.inc(v.scalar_tensor_tensor(out=phik_tm[:], in0=t2[:],
                                             scalar=FEAT_EPS, in1=t3[:],
                                             op0=ALU.add, op1=ALU.add),
                      "v", ("phik", j))
                for r in range(2):
                    E.wait("pe", ("st", j, r))
                    ps = (pS if r == 0 else pC)[:]
                    sview = sacc[:, 0:512]
                    v.tensor_tensor(
                        out=sview[0:64].rearrange("p (g f) -> p g f", f=64
                                                  )[:, 4 * r:4 * r + 4, :],
                        in0=sview[0:64].rearrange("p (g f) -> p g f", f=64
                                                  )[:, 4 * r:4 * r + 4, :],
                        in1=ps[0:64].rearrange("p (g f) -> p g f", f=P
                                               )[:, :, 0:64],
                        op=ALU.add)
                    E.inc(v.tensor_tensor(
                        out=sview[64:128].rearrange("p (g f) -> p g f", f=64
                                                    )[:, 4 * r:4 * r + 4, :],
                        in0=sview[64:128].rearrange("p (g f) -> p g f", f=64
                                                    )[:, 4 * r:4 * r + 4, :],
                        in1=ps[64:128].rearrange("p (g f) -> p g f", f=P
                                                 )[:, :, 64:128],
                        op=ALU.add), "v", ("sadd", j, r))
                E.wait("pe", ("scol", j))
                sc = sacc[:, 512:528]
                v.tensor_tensor(
                    out=sc[0:64].rearrange("p (g o) -> p g o", o=1)[:, 0:16:2, :],
                    in0=sc[0:64].rearrange("p (g o) -> p g o", o=1)[:, 0:16:2, :],
                    in1=pK[0:64, 0:8].rearrange("p (g o) -> p g o", o=1),
                    op=ALU.add)
                E.inc(v.tensor_tensor(
                    out=sc[64:128].rearrange("p (g o) -> p g o", o=1)[:, 1:16:2, :],
                    in0=sc[64:128].rearrange("p (g o) -> p g o", o=1)[:, 1:16:2, :],
                    in1=pK[64:128, 0:8].rearrange("p (g o) -> p g o", o=1),
                    op=ALU.add), "v", ("scadd", j))

        def body_scalar(E):
            s = E.eng
            E.wait("v", "ceps")
            for j in range(T):
                E.wait("v", ("xa", j))
                E.inc(emit_rstd(E, s, mv, rstd, c_eps[:]), "s", ("rstd", j))
                E.wait("pe", ("k", j))
                s.activation(out=t1[:], in_=pK[:], func=AF.Relu, scale=-1.0)
                s.activation(out=t2[:], in_=t1[:], func=AF.Exp, scale=-1.0)
                E.inc(s.activation(out=t3[:], in_=pK[:], func=AF.Relu),
                      "s", ("kel", j))
                E.wait("pe", ("v", j))
                E.inc(s.activation(out=v_tm[:], in_=pV[:], func=AF.Copy),
                      "s", ("v", j))
            E.wait("v", ("scadd", T - 1))
            E.inc(s.dma_start(out=stot_out, in_=sacc[:]), "out", "res", 16)

        def body_tensor(E):
            t = E.eng
            for j in range(T):
                E.wait("v", ("h1", j))
                if j == 0:
                    E.wait("c", "call")
                if j > 0:
                    E.wait("v", ("h1T", j - 1))
                for c in range(KC):
                    ins = t.transpose(pT[:, c * P:(c + 1) * P],
                                      h1_tm[:, c * P:(c + 1) * P], ident[:])
                E.inc(ins, "pe", ("h1T", j))
                E.wait("v", ("h1T", j))
                if j == 0:
                    E.wait("v", ("wcv", KC - 1))
                for half, pdst, name in ((0, pK, "k"), (1, pV, "v")):
                    if j > 0:
                        if half == 0:
                            E.wait("s", ("kel", j - 1))
                            E.wait("v", ("scadd", j - 1))
                        else:
                            E.wait("s", ("v", j - 1))
                            E.wait("pe", ("st", j - 1, 1))
                    for n4 in range(2):
                        for c in range(KC):
                            t.matmul(pdst[:, n4 * 512:(n4 + 1) * 512],
                                     h1T[:, c * P:(c + 1) * P],
                                     wkv_bf[:, c * 2 * D + half * D + n4 * 512:
                                            c * 2 * D + half * D + (n4 + 1) * 512],
                                     start=(c == 0), stop=False)
                        ins = t.matmul(pdst[:, n4 * 512:(n4 + 1) * 512],
                                       onesrow[:],
                                       brow[0:1, D + half * D + n4 * 512:
                                            D + half * D + (n4 + 1) * 512],
                                       start=False, stop=True)
                    E.inc(ins, "pe", (name, j))
                E.wait("v", ("phik", j))
                E.wait("s", ("v", j))
                for r in range(2):
                    ps = pS if r == 0 else pC
                    if j > 0:
                        E.wait("v", ("sadd", j - 1, r))
                    for pp in range(4):
                        pr = r * 4 + pp
                        ins = t.matmul(ps[:, pp * P:(pp + 1) * P],
                                       phik_tm[:, pr * P:(pr + 1) * P],
                                       v_tm[:, pr * P:(pr + 1) * P],
                                       start=True, stop=True)
                    E.inc(ins, "pe", ("st", j, r))
                E.wait("s", ("kel", j))
                if j > 0:
                    E.wait("v", ("scadd", j - 1))
                for pr in range(NPAIR):
                    ins = t.matmul(pK[:, pr:pr + 1],
                                   phik_tm[:, pr * P:(pr + 1) * P],
                                   onescol[:], start=True, stop=True)
                E.inc(ins, "pe", ("scol", j))

        run_bodies([
            (block.sync, body_sync, None),
            (block.vector, body_vector, "v"),
            (block.scalar, body_scalar, "s"),
            (block.tensor, body_tensor, None),
        ], sems)
    return nc




def build_main(T, upto='full', fused=True):
    STAGES = ['setup', 'h1', 'qkv', 'attn', 'wout', 'main', 'full']
    LVL = STAGES.index(upto)
    NPRE = 3 * T if fused else 0
    nc = bass.Bass("TRN2", target_bir_lowering=False)
    NT = T * TOK
    HT = (T + 1) // 2
    HLEN = HT * TOK
    SL = 256
    NSL = HID // SL

    x_in = nc.dram_tensor("x_main", [NT, D], F32, kind="ExternalInput").ap()
    wqkvT_in = nc.dram_tensor("wqkvT", [D, 3 * D], F32, kind="ExternalInput").ap()
    woutT_in = nc.dram_tensor("woutT", [D, D], F32, kind="ExternalInput").ap()
    bqkv_in = nc.dram_tensor("bqkv", [1, 3 * D], F32, kind="ExternalInput").ap()
    bout_in = nc.dram_tensor("bout", [1, D], F32, kind="ExternalInput").ap()
    gamma_in = nc.dram_tensor("gamma", [1, D], F32, kind="ExternalInput").ap()
    beta_in = nc.dram_tensor("beta", [1, D], F32, kind="ExternalInput").ap()
    wgT_in = nc.dram_tensor("wgT", [D, HID], I32, kind="ExternalInput").ap()
    wuT_in = nc.dram_tensor("wuT", [D, HID], I32, kind="ExternalInput").ap()
    wdT_in = nc.dram_tensor("wdT", [HID, D], I32, kind="ExternalInput").ap()
    maskT4_in = nc.dram_tensor("maskT4", [P, 512], F32, kind="ExternalInput").ap()
    ident_in = nc.dram_tensor("ident", [P, P], BF16, kind="ExternalInput").ap()
    onescol_in = nc.dram_tensor("onescol", [P, 1], BF16, kind="ExternalInput").ap()
    onesrow_in = nc.dram_tensor("onesrow", [1, P], BF16, kind="ExternalInput").ap()
    onesf_in = nc.dram_tensor("onesf", [1, P], F32, kind="ExternalInput").ap()
    if fused:
        x_pre = nc.dram_tensor("x_pre", [3 * T * TOK, D], F32,
                               kind="ExternalInput").ap()
        pflag_in = nc.dram_tensor("pflag", [P, 3 * T], F32,
                                  kind="ExternalInput").ap()
    else:
        stot_in = nc.dram_tensor("stot_all", [P, 8 * 528], F32,
                                 kind="ExternalInput").ap()
        maskbc_in = nc.dram_tensor("mask_bc", [P, 8], F32,
                                   kind="ExternalInput").ap()

    out = nc.dram_tensor("out", [NT, D], F32, kind="ExternalOutput").ap()
    x2_hbm = nc.dram_tensor("x2_scratch", [NT, D], F32, kind="Internal").ap()

    with ExitStack() as ctx:
        maskT4 = ctx.enter_context(nc.sbuf_tensor("maskT4S", [P, 512], F32))
        ident = ctx.enter_context(nc.sbuf_tensor("identS", [P, P], BF16))
        onescol = ctx.enter_context(nc.sbuf_tensor("onescolS", [P, 1], BF16))
        onesrow = ctx.enter_context(nc.sbuf_tensor("onesrowS", [1, P], BF16))
        onesf = ctx.enter_context(nc.sbuf_tensor("onesfS", [1, P], F32))
        brow = ctx.enter_context(nc.sbuf_tensor("brow", [1, 3 * D], BF16))
        borow = ctx.enter_context(nc.sbuf_tensor("borow", [1, D], BF16))
        stats = ctx.enter_context(nc.sbuf_tensor("stats", [P, 12], F32))
        mv = ctx.enter_context(nc.sbuf_tensor("mv", [P, 2], F32))
        rstd = ctx.enter_context(nc.sbuf_tensor("rstd", [P, 1], F32))
        c_eps = ctx.enter_context(nc.sbuf_tensor("c_eps", [P, 1], F32))
        mv_a = ctx.enter_context(nc.sbuf_tensor("mv_a", [P, 2], F32))
        mv_b = ctx.enter_context(nc.sbuf_tensor("mv_b", [P, 2], F32))
        rstd_a = ctx.enter_context(nc.sbuf_tensor("rstd_a", [P, 1], F32))
        rstd_b = ctx.enter_context(nc.sbuf_tensor("rstd_b", [P, 1], F32))
        pflag = ctx.enter_context(nc.sbuf_tensor("pflagS", [P, 3 * T], F32))
        rden = ctx.enter_context(nc.sbuf_tensor("rden", [P, 16], F32))
        rdpre = ctx.enter_context(nc.sbuf_tensor("rdpre", [P, 16], F32))
        gbc = ctx.enter_context(nc.sbuf_tensor("gbc", [P, D], F32))
        bbc = ctx.enter_context(nc.sbuf_tensor("bbc", [P, D], F32))
        wq_bf = ctx.enter_context(nc.sbuf_tensor("wq_bf", [P, KC * 3 * D], BF16))
        wo_bf = ctx.enter_context(nc.sbuf_tensor("wo_bf", [P, KC * D], BF16))
        stall = ctx.enter_context(nc.sbuf_tensor("stall", [P, 8 * 528], F32))
        mbc = ctx.enter_context(nc.sbuf_tensor("mbc", [P, 8], F32))
        sini = ctx.enter_context(nc.sbuf_tensor("sini", [P, 528], F32))
        srun = ctx.enter_context(nc.sbuf_tensor("srun", [P, 8 * P], F32))
        ssum = ctx.enter_context(nc.sbuf_tensor("ssum", [P, 16], F32))
        srun_bf = ctx.enter_context(nc.sbuf_tensor("srun_bf", [P, 8 * P], BF16))
        ssum_bf = ctx.enter_context(nc.sbuf_tensor("ssum_bf", [P, 16], BF16))
        xt_t = ctx.enter_context(nc.sbuf_tensor("xt", [P, 2 * D], F32))
        h1_tm = ctx.enter_context(nc.sbuf_tensor("h1_tm", [P, D], BF16))
        h1T = ctx.enter_context(nc.sbuf_tensor("h1T", [P, D], BF16))
        phiq = ctx.enter_context(nc.sbuf_tensor("phiq", [P, D], BF16))
        phiq_e = ctx.enter_context(nc.sbuf_tensor("phiq_e", [P, D], BF16))
        phiq_o = ctx.enter_context(nc.sbuf_tensor("phiq_o", [P, D], BF16))
        phik_tm = ctx.enter_context(nc.sbuf_tensor("phik_tm", [P, D], BF16))
        phik_fm = ctx.enter_context(nc.sbuf_tensor("phik_fm", [P, D], BF16))
        v_tm = ctx.enter_context(nc.sbuf_tensor("v_tm", [P, D], BF16))
        am = ctx.enter_context(nc.sbuf_tensor("am", [P, 512], BF16))
        attnT = ctx.enter_context(nc.sbuf_tensor("attnT", [P, D], BF16))
        attn_bf = ctx.enter_context(nc.sbuf_tensor("attn_bf", [P, D], BF16))
        t1 = ctx.enter_context(nc.sbuf_tensor("t1", [P, D], F32))
        t2 = ctx.enter_context(nc.sbuf_tensor("t2", [P, D], F32))
        t3 = ctx.enter_context(nc.sbuf_tensor("t3", [P, D], F32))
        o_n = ctx.enter_context(nc.sbuf_tensor("o_n", [P, D], F32))
        a_g = ctx.enter_context(nc.sbuf_tensor("a_g", [P, D], F32))
        x2sb_t = ctx.enter_context(nc.sbuf_tensor("x2sb", [P, 2 * D], F32))
        h2T = ctx.enter_context(nc.sbuf_tensor("h2T", [P, KC * 1024], BF16))
        wg_sl_t = ctx.enter_context(nc.sbuf_tensor("wg_sl", [P, 2 * KC * SL], BF16))
        wu_sl_t = ctx.enter_context(nc.sbuf_tensor("wu_sl", [P, 2 * KC * SL], BF16))
        silu_t = ctx.enter_context(nc.sbuf_tensor("silu_t", [P, 1024], F32))
        f_n = ctx.enter_context(nc.sbuf_tensor("f_n", [P, D], F32))
        wdx = ctx.enter_context(nc.sbuf_tensor("wdx", [P, 1024], BF16))
        pT = ctx.enter_context(nc.psum_tensor("pT", [P, 1024], BF16))
        pQ = ctx.enter_context(nc.psum_tensor("pQ", [P, 1024], F32))
        pK = ctx.enter_context(nc.psum_tensor("pK", [P, 1024], F32))
        pV = ctx.enter_context(nc.psum_tensor("pV", [P, 1024], F32))
        pD = ctx.enter_context(nc.psum_tensor("pD", [P, 512], F32))
        h_x0 = ctx.enter_context(nc.semaphore("s_x0"))
        h_x1 = ctx.enter_context(nc.semaphore("s_x1"))
        h_c = ctx.enter_context(nc.semaphore("s_c"))
        h_ld = ctx.enter_context(nc.semaphore("s_ld"))
        h_pe = ctx.enter_context(nc.semaphore("s_pe"))
        h_v = ctx.enter_context(nc.semaphore("s_v"))
        h_s = ctx.enter_context(nc.semaphore("s_s"))
        h_p = ctx.enter_context(nc.semaphore("s_p"))
        h_w0 = ctx.enter_context(nc.semaphore("s_w0"))
        h_w1 = ctx.enter_context(nc.semaphore("s_w1"))
        h_o0 = ctx.enter_context(nc.semaphore("s_o0"))
        h_o1 = ctx.enter_context(nc.semaphore("s_o1"))
        h_p0 = ctx.enter_context(nc.semaphore("s_p0"))
        h_p1 = ctx.enter_context(nc.semaphore("s_p1"))
        h_g = ctx.enter_context(nc.semaphore("s_g"))
        block = ctx.enter_context(nc.Block())

        xt = [xt_t.ap()[:, 0:D], xt_t.ap()[:, D:2 * D]]
        mvp = [mv_a, mv_b]
        rstdp = [rstd_a, rstd_b]
        x2sb = [x2sb_t.ap()[:, 0:D], x2sb_t.ap()[:, D:2 * D]]
        wg_sl = [wg_sl_t.ap()[:, 0:KC * SL], wg_sl_t.ap()[:, KC * SL:]]
        wu_sl = [wu_sl_t.ap()[:, 0:KC * SL], wu_sl_t.ap()[:, KC * SL:]]
        sems = {"x0": h_x0, "x1": h_x1, "c": h_c, "ld": h_ld, "pe": h_pe,
                "v": h_v, "s": h_s, "p": h_p, "w0": h_w0, "w1": h_w1,
                "o0": h_o0, "o1": h_o1, "px0": h_p0, "px1": h_p1,
                "g": h_g}
        if upto == 'full':
            assert HLEN == 1024 and T == 16, "V1 layout sized for T=16"

        wld = h2T.ap().bitcast(F32)[:, 0:3 * D]
        hst = wq_bf.ap()
        wgu_int = stall.ap().bitcast(I32)[:, 0:KC * SL]
        wuu_int = stall.ap().bitcast(I32)[:, KC * SL:2 * KC * SL]
        wd_int = srun.ap().bitcast(I32)[:, 0:1024]
        wd_bf = []
        for tens in (t1, t2, t3, o_n, a_g, gbc, bbc):
            b = tens.ap().bitcast(BF16)
            wd_bf.append(b[:, 0:1024])
            wd_bf.append(b[:, 1024:2048])
        wd_bf.append(sini.ap().bitcast(BF16)[:, 0:1024])
        wd_bf.append(srun_bf.ap()[:, 0:1024])
        for tens in (phiq, phik_tm, phik_fm, v_tm, wdx, h1T, attnT,
                     attn_bf):
            wd_bf.append(tens.ap())
        assert len(wd_bf) == HC
        x2r = [xt[0], xt[1]]
        outsb = [x2sb[0], x2sb[1]]
        xp = [x2sb[0], x2sb[1]]
        pT32 = pT.ap().bitcast(F32)

        NU = 2
        UNIT_LIST = []
        for ha in range(2):
            for sl in range(NSL):
                for u in range(NU):
                    UNIT_LIST.append((ha * NSL + sl, u))
        KSEQ_OF = {ku: i for i, ku in enumerate(UNIT_LIST)}
        GUP_BANKS = {}
        for i, ku in enumerate(UNIT_LIST):
            if i % 2 == 0:
                GUP_BANKS[ku] = (pQ[:], [pK[:]])
            else:
                GUP_BANKS[ku] = (pV[:], [pT32, pD[:]])

        def halves():
            for ha in range(2):
                yield (ha, 0, HT) if ha == 0 else (ha, HT, T - HT)

        LASTK = [NSL - 1, 2 * NSL - 1]
        H0_UNITS = [ku for ku in UNIT_LIST if ku[0] < NSL]
        H0_LAST_ODD = [ku for ku in H0_UNITS if KSEQ_OF[ku] % 2 == 1][-1]

        def body_sync(E):
            sy = E.eng
            cl = [("maskT4", maskT4[:], maskT4_in),
                  ("ident", ident[:], ident_in),
                  ("onescol", onescol[:], onescol_in),
                  ("onesrow", onesrow[:], onesrow_in),
                  ("onesf", onesf[:], onesf_in)]
            if fused:
                cl += [("pflag", pflag[:], pflag_in)]
            else:
                cl += [("stall", stall[:], stot_in),
                       ("mbc", mbc[:], maskbc_in)]
            for name, dst, srcap in cl:
                E.inc(sy.dma_start(out=dst, in_=srcap), "c", name, 16)
            E.inc(sy.dma_start(out=wld[0:1, :], in_=bqkv_in), "c", "bq", 16)
            E.inc(sy.dma_start(out=t1[0:1, 0:D], in_=bout_in), "c", "bo", 16)
            E.inc(sy.dma_start(out=t2[0:1, 0:D], in_=gamma_in), "c", "ga", 16)
            E.inc(sy.dma_start(out=t3[0:1, 0:D], in_=beta_in), "c", "call", 16)
            for c in range(KC):
                if c == 0:
                    E.wait("v", "rows")
                    E.wait("g", "gbcmm")
                else:
                    E.wait("v", ("wq", c - 1))
                E.inc(sy.dma_start(out=wld[:, :],
                                   in_=wqkvT_in[c * P:(c + 1) * P, :]),
                      "ld", ("wq", c), 16)
            for c in range(KC):
                E.wait("v", ("wq", KC - 1) if c == 0 else ("wo", c - 1))
                E.inc(sy.dma_start(out=wld[:, 0:D],
                                   in_=woutT_in[c * P:(c + 1) * P, :]),
                      "ld", ("wo", c), 16)
            if upto == 'setup':
                return
            for j in range(min(2, T)):
                E.inc(sy.dma_start(out=xt[j % 2],
                                   in_=x_in[j * TOK:(j + 1) * TOK, :]),
                      "x%d" % (j % 2), ("x", j), 16)
            if fused:
                for pj in range(NPRE):
                    if pj >= 2:
                        E.wait("v", ("ph1", pj - 2))
                        E.wait("px%d" % (pj % 2), ("px", pj - 2))
                    E.inc(sy.dma_start(
                        out=xp[pj % 2],
                        in_=x_pre[pj * TOK:(pj + 1) * TOK, :]),
                        "px%d" % (pj % 2), ("px", pj), 16)
            for j in range(2, T):
                if LVL >= 5:
                    E.wait("v", ("x2", j - 2))
                else:
                    E.wait("v", ("h1", j - 2))
                E.inc(sy.dma_start(out=xt[j % 2],
                                   in_=x_in[j * TOK:(j + 1) * TOK, :]),
                      "x%d" % (j % 2), ("x", j), 16)
            if upto != 'full':
                if upto == 'main':
                    for jp in range(max(0, T - 2), T):
                        E.wait("w%d" % (jp % 2), ("x2w", jp))
                return
            buffree = [("v", ("x2", T - 1)), ("v", ("x2", T - 1))]
            PREV_CONV = [None]
            for ha, t0, tn in halves():
                for tt in range(tn):
                    tg = t0 + tt
                    E.wait("w%d" % (tg % 2), ("x2w", tg))
                    E.wait(*buffree[tt % 2])
                    E.inc(sy.dma_start(out=x2r[tt % 2],
                                       in_=x2_hbm[tg * TOK:(tg + 1) * TOK, :]),
                          "x%d" % (tt % 2), ("x2a", ha, tt), 16)
                    buffree[tt % 2] = ("v", ("h2", ha, tt))
                for sl in range(NSL):
                    kseq = ha * NSL + sl
                    if kseq == 0:
                        E.wait("v", "sexp")
                        E.wait("v", ("wo", KC - 1))
                    else:
                        E.wait("s", PREV_CONV[0])
                    if kseq >= 2:
                        E.wait("pe", ("gup", kseq - 2, NU - 1))
                    E.inc(sy.dma_start(
                        out=wgu_int.rearrange("p (c h) -> p c h", c=KC),
                        in_=wgT_in[:, sl * SL:(sl + 1) * SL].rearrange(
                            "(c p) h -> p c h", p=P)), "ld", ("wg", kseq), 16)
                    E.wait("s", ("wgc", kseq))
                    if kseq >= 2:
                        E.wait("pe", ("gup", kseq - 2, NU - 1))
                    E.inc(sy.dma_start(
                        out=wuu_int.rearrange("p (c h) -> p c h", c=KC),
                        in_=wuT_in[:, sl * SL:(sl + 1) * SL].rearrange(
                            "(c p) h -> p c h", p=P)), "ld", ("wu", kseq), 16)
                    PREV_CONV[0] = ("wuc", kseq)
                    if ha == 0:
                        for c in (2 * sl, 2 * sl + 1):
                            E.wait("s", PREV_CONV[0])
                            if c == 0:
                                E.wait("v", ("x2", T - 1))
                                E.wait("v", ("st", T - 1))
                                E.wait("w0", ("x2w", T - 2))
                                E.wait("w1", ("x2w", T - 1))
                            E.inc(sy.dma_start(
                                out=wd_int,
                                in_=wdT_in[c * P:(c + 1) * P, :]),
                                "ld", ("wd", c), 16)
                            PREV_CONV[0] = ("wdc", c)
                for tt in range(tn):
                    tg = t0 + tt
                    E.wait(*buffree[tt % 2])
                    E.inc(sy.dma_start(out=x2r[tt % 2],
                                       in_=x2_hbm[tg * TOK:(tg + 1) * TOK, :]),
                          "x%d" % (tt % 2), ("x2b", ha, tt), 16)
                    buffree[tt % 2] = ("v", ("res", ha, tt))
            E.wait("o0", ("out", 1, T - HT - 2))
            E.wait("o1", ("out", 1, T - HT - 1))

        def body_vector(E):
            v = E.eng
            E.wait("c", "call")
            v.tensor_copy(out=brow[:], in_=wld[0:1, :])
            E.inc(v.tensor_copy(out=borow[:], in_=t1[0:1, 0:D]), "v", "rows")
            E.wait("g", "gbcmm")
            v.tensor_copy(out=gbc[:], in_=pQ[:])
            E.inc(v.tensor_copy(out=bbc[:], in_=pK[:]), "v", "gbcc")
            for c in range(KC):
                E.wait("ld", ("wq", c))
                E.inc(v.tensor_copy(out=wq_bf[:, c * 3 * D:(c + 1) * 3 * D],
                                    in_=wld[:, :]), "v", ("wq", c))
            for c in range(KC):
                E.wait("ld", ("wo", c))
                E.inc(v.tensor_copy(out=wo_bf[:, c * D:(c + 1) * D],
                                    in_=wld[:, 0:D]), "v", ("wo", c))
            E.inc(v.memset(c_eps[:], LN_EPS), "v", "ceps")
            v.memset(phiq_e[:], 0.0)
            E.inc(v.memset(phiq_o[:], 0.0), "v", "phiqz0")
            if fused:
                v.memset(h1_tm[:], 0.0)
                v.memset(h1T[:], 0.0)
                v.memset(phik_tm[:], 0.0)
                E.inc(v.memset(v_tm[:], 0.0), "v", "pz")
            if fused:
                v.memset(srun[:], 0.0)
                E.inc(v.memset(ssum[:], 0.0), "v", "sexp")
            else:
                v.tensor_scalar(out=sini[:], in0=stall[:, 0:528],
                                scalar1=mbc[:, 0:1], scalar2=None,
                                op0=ALU.mult)
                for r in range(1, 8):
                    v.tensor_scalar(out=t1[:, 0:528],
                                    in0=stall[:, r * 528:(r + 1) * 528],
                                    scalar1=mbc[:, r:r + 1], scalar2=None,
                                    op0=ALU.mult)
                    v.tensor_tensor(out=sini[:], in0=sini[:],
                                    in1=t1[:, 0:528], op=ALU.add)
                v.memset(srun[:], 0.0)
                v.tensor_copy(
                    out=srun[0:64, :].rearrange("p (g f) -> p g f", f=P
                                                )[:, :, 0:64],
                    in_=sini[0:64, 0:512].rearrange("p (g f) -> p g f", f=64))
                v.tensor_copy(
                    out=srun[64:128, :].rearrange("p (g f) -> p g f", f=P
                                                  )[:, :, 64:128],
                    in_=sini[64:128, 0:512].rearrange("p (g f) -> p g f",
                                                      f=64))
                E.inc(v.tensor_copy(out=ssum[:], in_=sini[:, 512:528]),
                      "v", "sexp")
            if upto == 'setup':
                return
            if fused:
                E.wait("px0", ("px", 0))
                emit_ln_stats(E, v, stats, mvp[0], xp[0][:, 0:512],
                              xp[0][:, 512:1024], ("pxa", 0))
                for pj in range(NPRE):
                    if pj + 1 < NPRE:
                        E.wait("px%d" % ((pj + 1) % 2), ("px", pj + 1))
                        emit_ln_stats(E, v, stats, mvp[(pj + 1) % 2],
                                      xp[(pj + 1) % 2][:, 0:512],
                                      xp[(pj + 1) % 2][:, 512:1024],
                                      ("pxa", pj + 1))
                    E.wait("s", ("prstd", pj))
                    E.inc(v.tensor_scalar(out=h1_tm[:], in0=xp[pj % 2],
                                          scalar1=mvp[pj % 2][:, 0:1],
                                          scalar2=rstdp[pj % 2][:],
                                          op0=ALU.subtract, op1=ALU.mult),
                          "v", ("ph1", pj))
                    E.wait("pe", ("pxh1T", pj))
                    E.inc(v.tensor_copy(out=h1T[:], in_=pT[:]),
                          "v", ("ph1T", pj))
                    E.wait("s", ("pkel", pj))
                    E.inc(v.scalar_tensor_tensor(
                        out=phik_tm[:], in0=t2[:], scalar=FEAT_EPS,
                        in1=t3[:], op0=ALU.add, op1=ALU.add),
                        "v", ("pphik", pj))
                    for r in range(2):
                        E.wait("pe", ("pxst", pj, r))
                        ps = pV[:, 512:1024]
                        v.scalar_tensor_tensor(
                            out=srun[0:64, :].rearrange(
                                "p (g f) -> p g f", f=P
                            )[:, 4 * r:4 * r + 4, 0:64],
                            in0=ps[0:64].rearrange("p (g f) -> p g f",
                                                   f=P)[:, :, 0:64],
                            scalar=pflag[0:64, pj:pj + 1],
                            in1=srun[0:64, :].rearrange(
                                "p (g f) -> p g f", f=P
                            )[:, 4 * r:4 * r + 4, 0:64],
                            op0=ALU.mult, op1=ALU.add)
                        E.inc(v.scalar_tensor_tensor(
                            out=srun[64:128, :].rearrange(
                                "p (g f) -> p g f", f=P
                            )[:, 4 * r:4 * r + 4, 64:128],
                            in0=ps[64:128].rearrange("p (g f) -> p g f",
                                                     f=P)[:, :, 64:128],
                            scalar=pflag[64:128, pj:pj + 1],
                            in1=srun[64:128, :].rearrange(
                                "p (g f) -> p g f", f=P
                            )[:, 4 * r:4 * r + 4, 64:128],
                            op0=ALU.mult, op1=ALU.add), "v",
                            ("psadd", pj, r))
                    E.wait("pe", ("pxscol", pj))
                    v.scalar_tensor_tensor(
                        out=ssum[0:64, :].rearrange("p (g o) -> p g o",
                                                    o=1)[:, 0:16:2, :],
                        in0=pD[0:64, 32:40].rearrange("p (g o) -> p g o",
                                                      o=1),
                        scalar=pflag[0:64, pj:pj + 1],
                        in1=ssum[0:64, :].rearrange("p (g o) -> p g o",
                                                    o=1)[:, 0:16:2, :],
                        op0=ALU.mult, op1=ALU.add)
                    E.inc(v.scalar_tensor_tensor(
                        out=ssum[64:128, :].rearrange("p (g o) -> p g o",
                                                      o=1)[:, 1:16:2, :],
                        in0=pD[64:128, 32:40].rearrange(
                            "p (g o) -> p g o", o=1),
                        scalar=pflag[64:128, pj:pj + 1],
                        in1=ssum[64:128, :].rearrange("p (g o) -> p g o",
                                                      o=1)[:, 1:16:2, :],
                        op0=ALU.mult, op1=ALU.add), "v", ("pst", pj))
                E._mark("v", "preend", E.counts.get("v", 0))
            E.wait("x0", ("x", 0))
            emit_ln_stats(E, v, stats, mvp[0], xt[0][:, 0:512],
                          xt[0][:, 512:1024], ("xa", 0))
            for j in range(T):
                E.wait("s", ("xrstd", j))
                E.inc(v.tensor_scalar(out=h1_tm[:], in0=xt[j % 2],
                                      scalar1=mvp[j % 2][:, 0:1],
                                      scalar2=rstdp[j % 2][:],
                                      op0=ALU.subtract, op1=ALU.mult),
                      "v", ("h1", j))
                E.wait("pe", ("h1T", j))
                E.inc(v.tensor_copy(out=h1T[:], in_=pT[:]), "v", ("h1T", j))
                if LVL < 2:
                    continue
                if LVL >= 3:
                    if j > 0:
                        E.wait("pe", ("scol", j - 1))
                    v.tensor_copy(out=srun_bf[:], in_=srun[:])
                    E.inc(v.tensor_copy(out=ssum_bf[:], in_=ssum[:]),
                          "v", ("scast", j))
                E.wait("s", ("qel", j))
                E.inc(v.scalar_tensor_tensor(out=phiq[:], in0=t2[:],
                                             scalar=FEAT_EPS, in1=t3[:],
                                             op0=ALU.add, op1=ALU.add),
                      "v", ("phiq", j))
                if LVL >= 3:
                    if j > 0:
                        E.wait("pe", ("at", j - 1, 3))
                    v.tensor_copy(out=phiq_e[0:64, :], in_=phiq[0:64, :])
                    E.inc(v.tensor_copy(out=phiq_o[64:128, :],
                                        in_=phiq[64:128, :]),
                          "v", ("phiqz", j))
                E.wait("s", ("kel", j))
                E.inc(v.scalar_tensor_tensor(out=phik_tm[:], in0=t1[:],
                                             scalar=FEAT_EPS, in1=t3[:],
                                             op0=ALU.add, op1=ALU.add),
                      "v", ("phik", j))
                E.wait("pe", ("kfm", j))
                E.inc(v.tensor_copy(out=phik_fm[:], in_=pT[:]),
                      "v", ("kfm", j))
                if j + 1 < T:
                    E.wait("x%d" % ((j + 1) % 2), ("x", j + 1))
                    emit_ln_stats(E, v, stats, mvp[(j + 1) % 2],
                                  xt[(j + 1) % 2][:, 0:512],
                                  xt[(j + 1) % 2][:, 512:1024], ("xa", j + 1))
                if LVL < 3:
                    continue
                if j == 0:
                    E.wait("c", "call")
                for g in range(4):
                    E.wait("pe", ("at", j, g))
                    if g > 0:
                        E.wait("pe", ("avg", j, g - 1))
                    E.inc(v.tensor_tensor(out=am[:], in0=pV[:, 0:512],
                                          in1=maskT4[:], op=ALU.mult),
                          "v", ("mask", j, g))
                E.wait("pe", ("avg", j, 3))
                if j > 0:
                    E.wait("s", ("div", j - 1))
                E.inc(v.tensor_scalar(out=rdpre[:], in0=pD[:, 0:16],
                                      scalar1=DEN_EPS, scalar2=None,
                                      op0=ALU.add), "v")
                E.selfwait("v")
                E.inc(v.reciprocal(out=rden[:], in_=rdpre[:]),
                      "v", ("rden", j))
                E.selfwait("v")
                for h in range(8, 16):
                    ins = v.tensor_scalar(
                        out=attn_bf[:, h * EH:(h + 1) * EH],
                        in0=pK[:, h * EH:(h + 1) * EH],
                        scalar1=rden[:, h:h + 1], scalar2=None, op0=ALU.mult)
                E.inc(ins, "v", ("div", j))
                for r in range(2):
                    E.wait("pe", ("st", j, r))
                    ps = pV[:, 512:1024]
                    v.tensor_tensor(
                        out=srun[0:64, :].rearrange("p (g f) -> p g f", f=P
                                                    )[:, 4 * r:4 * r + 4, 0:64],
                        in0=srun[0:64, :].rearrange("p (g f) -> p g f", f=P
                                                    )[:, 4 * r:4 * r + 4, 0:64],
                        in1=ps[0:64].rearrange("p (g f) -> p g f", f=P
                                               )[:, :, 0:64],
                        op=ALU.add)
                    E.inc(v.tensor_tensor(
                        out=srun[64:128, :].rearrange("p (g f) -> p g f", f=P
                                                      )[:, 4 * r:4 * r + 4,
                                                        64:128],
                        in0=srun[64:128, :].rearrange("p (g f) -> p g f", f=P
                                                      )[:, 4 * r:4 * r + 4,
                                                        64:128],
                        in1=ps[64:128].rearrange("p (g f) -> p g f", f=P
                                                 )[:, :, 64:128],
                        op=ALU.add), "v", ("sadd", j, r))
                E.wait("pe", ("scol", j))
                v.tensor_tensor(
                    out=ssum[0:64, :].rearrange("p (g o) -> p g o", o=1
                                                )[:, 0:16:2, :],
                    in0=ssum[0:64, :].rearrange("p (g o) -> p g o", o=1
                                                )[:, 0:16:2, :],
                    in1=pD[0:64, 32:40].rearrange("p (g o) -> p g o", o=1),
                    op=ALU.add)
                E.inc(v.tensor_tensor(
                    out=ssum[64:128, :].rearrange("p (g o) -> p g o", o=1
                                                  )[:, 1:16:2, :],
                    in0=ssum[64:128, :].rearrange("p (g o) -> p g o", o=1
                                                  )[:, 1:16:2, :],
                    in1=pD[64:128, 32:40].rearrange("p (g o) -> p g o", o=1),
                    op=ALU.add), "v", ("st", j))
                if LVL < 4:
                    continue
                E.wait("pe", ("attnT", j))
                E.inc(v.tensor_copy(out=attnT[:], in_=pT[:]),
                      "v", ("attnT", j))
                E.wait("pe", ("wout", j))
                emit_ln_stats(E, v, stats, mv, pQ[:, 0:512], pQ[:, 512:1024],
                              ("oa", j))
                E.wait("s", ("orstd", j))
                E.inc(v.tensor_scalar(out=o_n[:], in0=pQ[:],
                                      scalar1=mv[:, 0:1], scalar2=rstd[:],
                                      op0=ALU.subtract, op1=ALU.mult),
                      "v", ("onorm", j))
                if LVL < 5:
                    continue
                v.tensor_tensor(out=a_g[:], in0=o_n[:], in1=gbc[:],
                                op=ALU.mult)
                E.inc(v.tensor_tensor(out=a_g[:], in0=a_g[:], in1=bbc[:],
                                      op=ALU.add), "v", ("gb", j))
                emit_ln_stats(E, v, stats, mv, a_g[:, 0:512], a_g[:, 512:1024],
                              ("aa", j))
                E.wait("s", ("arstd", j))
                E.inc(v.tensor_scalar(out=o_n[:], in0=a_g[:],
                                      scalar1=mv[:, 0:1], scalar2=rstd[:],
                                      op0=ALU.subtract, op1=ALU.mult),
                      "v", ("anorm", j))
                if j >= 2:
                    E.wait("w%d" % (j % 2), ("x2w", j - 2))
                elif fused:
                    E.wait("px%d" % (j % 2), ("px", NPRE - 2 + j))
                E.inc(v.tensor_tensor(out=x2sb[j % 2], in0=xt[j % 2],
                                      in1=o_n[:], op=ALU.add),
                      "v", ("x2", j))
            if upto != 'full':
                return
            for ha, t0, tn in halves():
                for tt in range(tn):
                    E.wait("x%d" % (tt % 2), ("x2a", ha, tt))
                    emit_ln_stats(E, v, stats, mv, x2r[tt % 2][:, 0:512],
                                  x2r[tt % 2][:, 512:1024], ("h2a", ha, tt))
                    E.wait("s", ("h2rstd", ha, tt))
                    E.inc(v.tensor_scalar(out=h1_tm[:], in0=x2r[tt % 2],
                                          scalar1=mv[:, 0:1], scalar2=rstd[:],
                                          op0=ALU.subtract, op1=ALU.mult),
                          "v", ("h2", ha, tt))
                    E.wait("pe", ("h2T", ha, tt))
                    E.inc(v.tensor_copy(
                        out=h2T.ap().rearrange("p (c i) -> p c i", c=KC
                                               )[:, :, tt * P:(tt + 1) * P],
                        in_=pT.ap().rearrange("p (c i) -> p c i", c=KC)),
                        "v", ("h2T", ha, tt))
                for sl in range(NSL):
                    kseq = ha * NSL + sl
                    for u in range(NU):
                        pgate, pups = GUP_BANKS[(kseq, u)]
                        E.wait("s", ("silu", kseq, u))
                        for hh in range(2):
                            if len(pups) == 1:
                                upsrc = pups[0][:, hh * 512:(hh + 1) * 512]
                            else:
                                upsrc = pups[hh]
                            ins = v.tensor_tensor(
                                out=hst.rearrange("p (c i) -> p c i", i=HLEN
                                                  )[:, 2 * sl + hh,
                                                    u * 512:(u + 1) * 512],
                                in0=silu_t[:, hh * 512:(hh + 1) * 512],
                                in1=upsrc, op=ALU.mult)
                        E.inc(ins, "v", ("hmul", kseq, u))
                for tt in range(tn):
                    pdn = pQ if tt % 2 == 0 else pK
                    E.wait("pe", ("down", ha, tt))
                    emit_ln_stats(E, v, stats, mv, pdn[:, 0:512],
                                  pdn[:, 512:1024], ("fa", ha, tt))
                    E.wait("s", ("frstd", ha, tt))
                    E.inc(v.tensor_scalar(out=f_n[:], in0=pdn[:],
                                          scalar1=mv[:, 0:1], scalar2=rstd[:],
                                          op0=ALU.subtract, op1=ALU.mult),
                          "v", ("fnorm", ha, tt))
                    E.wait("x%d" % (tt % 2), ("x2b", ha, tt))
                    if (ha, tt) == (0, 0):
                        E.wait("w1", ("x2w", T - 1))
                        E.wait("w0", ("x2w", T - 2))
                    if tt >= 2:
                        E.wait("o%d" % (tt % 2), ("out", ha, tt - 2))
                    elif ha == 1:
                        E.wait("o%d" % (tt % 2), ("out", 0, HT - 2 + tt))
                    E.inc(v.tensor_tensor(out=outsb[tt % 2],
                                          in0=x2r[tt % 2], in1=f_n[:],
                                          op=ALU.add), "v", ("res", ha, tt))

        def body_scalar(E):
            s = E.eng
            E.wait("v", "ceps")
            if upto == 'setup':
                return
            if fused:
                for pj in range(NPRE):
                    E.wait("v", ("pxa", pj))
                    E.inc(emit_rstd(E, s, mvp[pj % 2], rstdp[pj % 2],
                                    c_eps[:]), "s", ("prstd", pj))
                    E.wait("pe", ("pxk", pj))
                    if pj > 0:
                        E.wait("v", ("pphik", pj - 1))
                    s.activation(out=t1[:], in_=pK[:], func=AF.Relu,
                                 scale=-1.0)
                    s.activation(out=t2[:], in_=t1[:], func=AF.Exp,
                                 scale=-1.0)
                    E.inc(s.activation(out=t3[:], in_=pK[:],
                                       func=AF.Relu), "s", ("pkel", pj))
                    E.wait("pe", ("pxv", pj))
                    if pj > 0:
                        E.wait("pe", ("pxst", pj - 1, 1))
                    E.inc(s.activation(out=v_tm[:], in_=pV[:],
                                       func=AF.Copy), "s", ("pv", pj))
                E._mark("s", "preend", E.counts.get("s", 0))
            for j in range(T):
                E.wait("v", ("xa", j))
                E.inc(emit_rstd(E, s, mvp[j % 2], rstdp[j % 2], c_eps[:]),
                      "s", ("xrstd", j))
                if LVL < 2:
                    continue
                E.wait("pe", ("q", j))
                if j > 0:
                    E.wait("v", ("phik", j - 1))
                elif fused:
                    E.wait("v", "preend")
                s.activation(out=t1[:], in_=pQ[:], func=AF.Relu, scale=-1.0)
                s.activation(out=t2[:], in_=t1[:], func=AF.Exp, scale=-1.0)
                E.inc(s.activation(out=t3[:], in_=pQ[:], func=AF.Relu),
                      "s", ("qel", j))
                E.wait("pe", ("k", j))
                E.wait("v", ("phiq", j))
                s.activation(out=t2[:], in_=pK[:], func=AF.Relu, scale=-1.0)
                s.activation(out=t1[:], in_=t2[:], func=AF.Exp, scale=-1.0)
                E.inc(s.activation(out=t3[:], in_=pK[:], func=AF.Relu),
                      "s", ("kel", j))
                E.wait("pe", ("v", j))
                E.inc(s.activation(out=v_tm[:], in_=pV[:], func=AF.Copy),
                      "s", ("v", j))
                if LVL < 3:
                    continue
                E.wait("v", ("rden", j))
                for h in range(8):
                    ins = s.activation(out=attn_bf[:, h * EH:(h + 1) * EH],
                                       in_=pK[:, h * EH:(h + 1) * EH],
                                       func=AF.Copy, scale=rden[:, h:h + 1])
                E.inc(ins, "s", ("div", j))
                if LVL < 4:
                    continue
                E.wait("v", ("oa", j))
                E.inc(emit_rstd(E, s, mv, rstd, c_eps[:]), "s", ("orstd", j))
                if LVL < 5:
                    continue
                E.wait("v", ("aa", j))
                E.inc(emit_rstd(E, s, mv, rstd, c_eps[:]), "s", ("arstd", j))
                E.wait("v", ("x2", j))
                E.inc(s.dma_start(out=x2_hbm[j * TOK:(j + 1) * TOK, :],
                                  in_=x2sb[j % 2]), "w%d" % (j % 2),
                      ("x2w", j), 16)
            if upto != 'full':
                return
            def emit_silu(kq, u):
                pgate, _pups = GUP_BANKS[(kq, u)]
                E.wait("pe", ("gup", kq, u))
                kk = KSEQ_OF[(kq, u)]
                if kk >= 1:
                    E.wait("v", ("hmul",) + UNIT_LIST[kk - 1])
                E.inc(s.activation(out=silu_t[:], in_=pgate, func=AF.Silu),
                      "s", ("silu", kq, u))

            for ha, t0, tn in halves():
                for tt in range(tn):
                    E.wait("v", ("h2a", ha, tt))
                    E.inc(emit_rstd(E, s, mv, rstd, c_eps[:]),
                          "s", ("h2rstd", ha, tt))
                for sl in range(NSL):
                    kseq = ha * NSL + sl
                    E.wait("ld", ("wg", kseq))
                    E.inc(s.activation(out=wg_sl[kseq % 2], in_=wgu_int,
                                       func=AF.Copy, scale=WSCALE),
                          "s", ("wgc", kseq))
                    E.wait("ld", ("wu", kseq))
                    E.inc(s.activation(out=wu_sl[kseq % 2], in_=wuu_int,
                                       func=AF.Copy, scale=WSCALE),
                          "s", ("wuc", kseq))
                    if ha == 0:
                        for c in (2 * sl, 2 * sl + 1):
                            E.wait("ld", ("wd", c))
                            E.inc(s.activation(out=wd_bf[c], in_=wd_int,
                                               func=AF.Copy, scale=WSCALE),
                                  "s", ("wdc", c))
                    if sl > 0:
                        for u in range(NU):
                            emit_silu(ha * NSL + sl - 1, u)
                for u in range(NU):
                    emit_silu(ha * NSL + NSL - 1, u)
                for tt in range(tn):
                    E.wait("v", ("fa", ha, tt))
                    E.inc(emit_rstd(E, s, mv, rstd, c_eps[:]),
                          "s", ("frstd", ha, tt))
                    E.wait("v", ("res", ha, tt))
                    E.inc(s.dma_start(
                        out=out[(t0 + tt) * TOK:(t0 + tt + 1) * TOK, :],
                        in_=outsb[tt % 2]), "o%d" % (tt % 2),
                      ("out", ha, tt), 16)

        def body_tensor(E):
            t = E.eng
            E.wait("c", "call")
            for n4 in range(2):
                t.matmul(pQ[:, n4 * 512:(n4 + 1) * 512], onesf[:],
                         t2[0:1, n4 * 512:(n4 + 1) * 512],
                         start=True, stop=True)
            for n4 in range(2):
                ins = t.matmul(pK[:, n4 * 512:(n4 + 1) * 512], onesf[:],
                               t3[0:1, n4 * 512:(n4 + 1) * 512],
                               start=True, stop=True)
            E.inc(ins, "g", "gbcmm")
            if upto == 'setup':
                return
            if fused:
                for pj in range(NPRE):
                    E.wait("v", ("ph1", pj))
                    if pj == 0:
                        E.wait("c", "call")
                    else:
                        E.wait("v", ("ph1T", pj - 1))
                    for c in range(KC):
                        ins = t.transpose(pT[:, c * P:(c + 1) * P],
                                          h1_tm[:, c * P:(c + 1) * P],
                                          ident[:])
                    E.inc(ins, "pe", ("pxh1T", pj))
                    for half, pdst, name in ((0, pK, "pxk"), (1, pV, "pxv")):
                        E.wait("v", ("ph1T", pj))
                        if pj == 0:
                            E.wait("v", ("wq", KC - 1))
                            E.wait("v", "gbcc")
                        else:
                            if half == 0:
                                E.wait("s", ("pkel", pj - 1))
                            else:
                                E.wait("s", ("pv", pj - 1))
                                E.wait("v", ("psadd", pj - 1, 1))
                        for n4 in range(2):
                            for c in range(KC):
                                t.matmul(
                                    pdst[:, n4 * 512:(n4 + 1) * 512],
                                    h1T[:, c * P:(c + 1) * P],
                                    wq_bf[:, c * 3 * D + D + half * D +
                                          n4 * 512:
                                          c * 3 * D + D + half * D +
                                          (n4 + 1) * 512],
                                    start=(c == 0), stop=False)
                            ins = t.matmul(
                                pdst[:, n4 * 512:(n4 + 1) * 512],
                                onesrow[:],
                                brow[0:1, D + half * D + n4 * 512:
                                     D + half * D + (n4 + 1) * 512],
                                start=False, stop=True)
                        E.inc(ins, "pe", (name, pj))
                    for r in range(2):
                        E.wait("v", ("pphik", pj))
                        E.wait("s", ("pv", pj))
                        if r == 0:
                            if pj > 0:
                                E.wait("v", ("psadd", pj - 1, 1))
                        else:
                            E.wait("v", ("psadd", pj, 0))
                        for pp in range(4):
                            pr = r * 4 + pp
                            ins = t.matmul(
                                pV[:, 512 + pp * P:512 + (pp + 1) * P],
                                phik_tm[:, pr * P:(pr + 1) * P],
                                v_tm[:, pr * P:(pr + 1) * P],
                                start=True, stop=True)
                        E.inc(ins, "pe", ("pxst", pj, r))
                    if pj > 0:
                        E.wait("v", ("pst", pj - 1))
                    E.wait("v", ("pphik", pj))
                    for pr in range(NPAIR):
                        ins = t.matmul(pD[:, 32 + pr:33 + pr],
                                       phik_tm[:, pr * P:(pr + 1) * P],
                                       onescol[:], start=True, stop=True)
                    E.inc(ins, "pe", ("pxscol", pj))
                E._mark("pe", "preend", E.counts.get("pe", 0))
            for j in range(T):
                E.wait("v", ("h1", j))
                if j > 0:
                    E.wait("v", ("attnT", j - 1) if LVL >= 4 else
                           (("kfm", j - 1) if LVL >= 2 else ("h1T", j - 1)))
                elif fused:
                    E.wait("v", "preend")
                for c in range(KC):
                    ins = t.transpose(pT[:, c * P:(c + 1) * P],
                                      h1_tm[:, c * P:(c + 1) * P], ident[:])
                E.inc(ins, "pe", ("h1T", j))
                if LVL < 2:
                    continue
                E.wait("v", ("h1T", j))
                if j == 0:
                    E.wait("v", ("wq", KC - 1))
                    E.wait("v", "gbcc")
                elif LVL >= 4:
                    E.wait("v", ("onorm", j - 1))
                    E.wait("v", ("oa", j - 1))
                else:
                    E.wait("s", ("qel", j - 1))
                for ec in range(KC):
                    for c in range(KC):
                        t.matmul(pQ[:, ec * P:(ec + 1) * P],
                                 wq_bf[:, c * 3 * D + ec * P:
                                       c * 3 * D + (ec + 1) * P],
                                 h1T[:, c * P:(c + 1) * P],
                                 start=(c == 0), stop=False)
                    ins = t.matmul(pQ[:, ec * P:(ec + 1) * P],
                                   brow[0:1, ec * P:(ec + 1) * P],
                                   onesrow[:], start=False, stop=True)
                E.inc(ins, "pe", ("q", j))
                for half, pdst, name in ((0, pK, "k"), (1, pV, "v")):
                    if j == 0 and fused:
                        if half == 0:
                            E.wait("s", "preend")
                        else:
                            E.wait("s", "preend")
                            E.wait("v", "preend")
                    if j > 0:
                        if half == 0:
                            if LVL >= 3:
                                E.wait("s", ("div", j - 1))
                                E.wait("v", ("div", j - 1))
                            else:
                                E.wait("s", ("kel", j - 1))
                        else:
                            E.wait("s", ("v", j - 1))
                            if LVL >= 3:
                                E.wait("v", ("sadd", j - 1, 1))
                                E.wait("v", ("mask", j - 1, 3))
                    for n4 in range(2):
                        for c in range(KC):
                            t.matmul(pdst[:, n4 * 512:(n4 + 1) * 512],
                                     h1T[:, c * P:(c + 1) * P],
                                     wq_bf[:, c * 3 * D + D + half * D + n4 * 512:
                                           c * 3 * D + D + half * D +
                                           (n4 + 1) * 512],
                                     start=(c == 0), stop=False)
                        ins = t.matmul(pdst[:, n4 * 512:(n4 + 1) * 512],
                                       onesrow[:],
                                       brow[0:1, D + half * D + n4 * 512:
                                            D + half * D + (n4 + 1) * 512],
                                       start=False, stop=True)
                    E.inc(ins, "pe", (name, j))
                E.wait("v", ("phik", j))
                E.wait("v", ("h1T", j))
                for c in range(KC):
                    ins = t.transpose(pT[:, c * P:(c + 1) * P],
                                      phik_tm[:, c * P:(c + 1) * P], ident[:])
                E.inc(ins, "pe", ("kfm", j))
                if LVL < 3:
                    continue
                E.wait("v", ("kfm", j))
                E.wait("v", ("phiq", j))
                E.wait("s", ("v", j))
                E.wait("v", ("scast", j))
                if j > 0:
                    E.wait("v", ("st", j - 1))
                    E.wait("v", ("rden", j - 1))
                elif fused:
                    E.wait("v", "preend")
                E.wait("v", ("phiqz", j))
                for g in range(4):
                    if g > 0:
                        E.wait("v", ("mask", j, g - 1))
                    for hh in range(4):
                        h = g * 4 + hh
                        pr, hlf = h // 2, h % 2
                        ins = t.matmul(
                            pV[:, hh * P:(hh + 1) * P],
                            phik_fm[:, pr * P:(pr + 1) * P],
                            (phiq_e if hlf == 0 else phiq_o)[:, pr * P:(pr + 1) * P],
                            start=True, stop=True)
                    E.inc(ins, "pe", ("at", j, g))
                    E.wait("v", ("mask", j, g))
                    for pp in range(2):
                        pr = 2 * g + pp
                        t.matmul(pD[:, 2 * pr:2 * pr + 2],
                                 phiq[:, pr * P:(pr + 1) * P],
                                 ssum_bf[:, 2 * pr:2 * pr + 2],
                                 start=True, stop=False)
                        t.matmul(pD[:, 2 * pr:2 * pr + 1],
                                 am[:, 2 * pp * P:(2 * pp + 1) * P],
                                 onescol[:], start=False, stop=False)
                        ins = t.matmul(pD[:, 2 * pr + 1:2 * pr + 2],
                                       am[:, (2 * pp + 1) * P:(2 * pp + 2) * P],
                                       onescol[:], start=False, stop=True)
                        t.matmul(pK[:, pr * P:(pr + 1) * P],
                                 phiq[:, pr * P:(pr + 1) * P],
                                 srun_bf[:, pr * P:(pr + 1) * P],
                                 start=True, stop=False)
                        for sub in range(2):
                            h = 2 * pr + sub
                            ins = t.matmul(pK[:, h * EH:(h + 1) * EH],
                                           am[:, (2 * pp + sub) * P:
                                              (2 * pp + sub + 1) * P],
                                           v_tm[:, h * EH:(h + 1) * EH],
                                           start=False, stop=(sub == 1))
                    E.inc(ins, "pe", ("avg", j, g))
                for r in range(2):
                    if r == 0:
                        E.wait("v", ("mask", j, 3))
                        if j > 0:
                            E.wait("v", ("sadd", j - 1, 1))
                        elif fused:
                            E.wait("v", "preend")
                    else:
                        E.wait("v", ("sadd", j, 0))
                    for pp in range(4):
                        pr = r * 4 + pp
                        ins = t.matmul(pV[:, 512 + pp * P:512 + (pp + 1) * P],
                                       phik_tm[:, pr * P:(pr + 1) * P],
                                       v_tm[:, pr * P:(pr + 1) * P],
                                       start=True, stop=True)
                    E.inc(ins, "pe", ("st", j, r))
                E.wait("v", ("rden", j))
                if j > 0:
                    E.wait("v", ("st", j - 1))
                elif fused:
                    E.wait("v", "preend")
                for pr in range(NPAIR):
                    ins = t.matmul(pD[:, 32 + pr:33 + pr],
                                   phik_tm[:, pr * P:(pr + 1) * P],
                                   onescol[:], start=True, stop=True)
                E.inc(ins, "pe", ("scol", j))
                if LVL < 4:
                    continue
                E.wait("s", ("div", j))
                E.wait("v", ("div", j))
                E.wait("v", ("kfm", j))
                for c in range(KC):
                    ins = t.transpose(pT[:, c * P:(c + 1) * P],
                                      attn_bf[:, c * P:(c + 1) * P], ident[:])
                E.inc(ins, "pe", ("attnT", j))
                E.wait("v", ("attnT", j))
                if j == 0:
                    E.wait("v", ("wo", KC - 1))
                E.wait("s", ("qel", j))
                for n4 in range(2):
                    for c in range(KC):
                        t.matmul(pQ[:, n4 * 512:(n4 + 1) * 512],
                                 attnT[:, c * P:(c + 1) * P],
                                 wo_bf[:, c * D + n4 * 512:
                                       c * D + (n4 + 1) * 512],
                                 start=(c == 0), stop=False)
                    ins = t.matmul(pQ[:, n4 * 512:(n4 + 1) * 512],
                                   onesrow[:],
                                   borow[0:1, n4 * 512:(n4 + 1) * 512],
                                   start=False, stop=True)
                E.inc(ins, "pe", ("wout", j))
            if upto != 'full':
                return
            for ha, t0, tn in halves():
                for tt in range(tn):
                    E.wait("v", ("h2", ha, tt))
                    if (ha, tt) == (0, 0):
                        E.wait("v", ("attnT", T - 1))
                    elif ha == 1 and tt == 0:
                        E.wait("v", ("hmul",) + H0_LAST_ODD)
                        E.wait("s", ("silu",) + H0_LAST_ODD)
                    else:
                        E.wait("v", ("h2T", ha, tt - 1))
                    for c in range(KC):
                        ins = t.transpose(pT[:, c * P:(c + 1) * P],
                                          h1_tm[:, c * P:(c + 1) * P],
                                          ident[:])
                    E.inc(ins, "pe", ("h2T", ha, tt))
                for sl in range(NSL):
                    kseq = ha * NSL + sl
                    E.wait("s", ("wgc", kseq))
                    E.wait("s", ("wuc", kseq))
                    for u in range(NU):
                        pgate, pups = GUP_BANKS[(kseq, u)]
                        for tc in range(u * 4, min(u * 4 + 4, tn)):
                            E.wait("v", ("h2T", ha, tc))
                        kk = KSEQ_OF[(kseq, u)]
                        if kk >= 2:
                            pku = UNIT_LIST[kk - 2]
                            E.wait("v", ("hmul",) + pku)
                            E.wait("s", ("silu",) + pku)
                        if ha == 1 and kseq == NSL and u < 2:
                            E.wait("v", ("fnorm", 0, HT - 2 + u))
                        h2v = h2T.ap().rearrange("p (c i) -> p c i", c=KC)
                        for hh in range(2):
                            dst = pgate[:, hh * 512:(hh + 1) * 512]
                            for c in range(KC):
                                t.matmul(dst,
                                         wg_sl[kseq % 2][:, c * SL + hh * P:
                                                         c * SL + (hh + 1) * P],
                                         h2v[:, c, u * 512:(u + 1) * 512],
                                         start=(c == 0), stop=(c == KC - 1))
                        for hh in range(2):
                            if len(pups) == 1:
                                udst = pups[0][:, hh * 512:(hh + 1) * 512]
                            else:
                                udst = pups[hh]
                            for c in range(KC):
                                ins = t.matmul(
                                    udst,
                                    wu_sl[kseq % 2][:, c * SL + hh * P:
                                                    c * SL + (hh + 1) * P],
                                    h2v[:, c, u * 512:(u + 1) * 512],
                                    start=(c == 0), stop=(c == KC - 1))
                        E.inc(ins, "pe", ("gup", kseq, u))
                for tt in range(tn):
                    pdn = pQ if tt % 2 == 0 else pK
                    lastk = ha * NSL + NSL - 1
                    E.wait("v", ("hmul", lastk, tt // 4))
                    E.wait("s", ("silu", lastk, tt // 4))
                    if tt == 0:
                        E.wait("s", ("wdc", HC - 1))
                    if tt >= 2:
                        E.wait("v", ("fnorm", ha, tt - 2))
                    hstv = hst.rearrange("p (c i) -> p c i", i=HLEN)
                    for n4 in range(2):
                        for c in range(HC):
                            ins = t.matmul(
                                pdn[:, n4 * 512:(n4 + 1) * 512],
                                hstv[:, c, tt * P:(tt + 1) * P],
                                wd_bf[c][:, n4 * 512:(n4 + 1) * 512],
                                start=(c == 0), stop=(c == HC - 1))
                    E.inc(ins, "pe", ("down", ha, tt))

        run_bodies([
            (block.sync, body_sync, None),
            (block.vector, body_vector, "v"),
            (block.scalar, body_scalar, "s"),
            (block.tensor, body_tensor, None),
        ], sems)
    return nc




def _consts():
    maskT = np.triu(np.ones((P, P), dtype=np.float32))
    return {
        "maskT4": np.ascontiguousarray(np.tile(maskT, (1, 4))),
        "ident": np.eye(P, dtype=BF),
        "onescol": np.ones((P, 1), dtype=BF),
        "onesrow": np.ones((1, P), dtype=BF),
        "onesf": np.ones((1, P), dtype=np.float32),
    }


def kernel(L0_attn_x, L0_attn_Wqkv, L0_attn_bqkv, L0_attn_Wout, L0_attn_bout,
           L0_attn_gamma, L0_attn_beta, L0_ffn_w_gate, L0_ffn_w_up,
           L0_ffn_w_down):
    x = np.ascontiguousarray(np.asarray(L0_attn_x, dtype=np.float32)
                             ).reshape(2 * SEQ_LEN, D)
    wqkvT = np.ascontiguousarray(np.asarray(L0_attn_Wqkv, dtype=np.float32).T)
    woutT = np.ascontiguousarray(np.asarray(L0_attn_Wout, dtype=np.float32).T)
    bqkv = np.ascontiguousarray(
        np.asarray(L0_attn_bqkv, dtype=np.float32).reshape(1, 3 * D))
    bout = np.ascontiguousarray(
        np.asarray(L0_attn_bout, dtype=np.float32).reshape(1, D))
    gamma = np.ascontiguousarray(
        np.asarray(L0_attn_gamma, dtype=np.float32).reshape(1, D))
    beta = np.ascontiguousarray(
        np.asarray(L0_attn_beta, dtype=np.float32).reshape(1, D))
    wgT = np.ascontiguousarray(np.asarray(L0_ffn_w_gate, dtype=np.int32).T)
    wuT = np.ascontiguousarray(np.asarray(L0_ffn_w_up, dtype=np.int32).T)
    wdT = np.ascontiguousarray(np.asarray(L0_ffn_w_down, dtype=np.int32).T)
    consts = _consts()

    NT = T_MAIN * TOK
    xs = [np.ascontiguousarray(x[c * NT:(c + 1) * NT]) for c in range(N_CORES)]

    nc_b = build_main(T_MAIN, fused=True)
    in_b = []
    for c in range(N_CORES):
        seq = c // 4
        xseq = x[seq * SEQ_LEN:(seq + 1) * SEQ_LEN]
        npre = (c % 4) * T_MAIN
        pf = np.zeros((P, 3 * T_MAIN), dtype=np.float32)
        pf[:, :npre] = 1.0
        in_b.append({"x_main": xs[c], "x_pre": np.ascontiguousarray(xseq[0:3 * NT]),
                     "pflag": pf,
                     "wqkvT": wqkvT, "woutT": woutT, "bqkv": bqkv,
                     "bout": bout, "gamma": gamma, "beta": beta, "wgT": wgT,
                     "wuT": wuT, "wdT": wdT, **consts})
    res_b = run_bass_kernel_spmd(nc_b, in_b, core_ids=list(range(N_CORES)))
    outs = [np.asarray(res_b.results[c]["out"]) for c in range(N_CORES)]
    return np.concatenate(outs, axis=0).reshape(2, SEQ_LEN, D)


# version 13
# speedup vs baseline: 1.7400x; 1.0473x over previous
from contextlib import ExitStack, contextmanager

import numpy as np
import ml_dtypes
import concourse.bass as bass
import concourse.mybir as mybir
from concourse.bass_utils import run_bass_kernel_spmd

F32 = mybir.dt.float32
BF16 = mybir.dt.bfloat16
I32 = mybir.dt.int32
AF = mybir.ActivationFunctionType
ALU = mybir.AluOpType

D = 1024
HID = 3072
NPAIR = 8
EH = 64
P = 128
KC = D // P
HC = HID // P
LN_EPS = 1e-5
FEAT_EPS = 1e-4
DEN_EPS = 8.0 * FEAT_EPS
WSCALE = 1.0 / 127.0

N_CORES = 8
SEQ_LEN = 8192
T_MAIN = 16
TOK = P

BF = ml_dtypes.bfloat16




class _MockInstr:
    def then_inc(self, *a, **k):
        return self

    def __getattr__(self, name):
        return lambda *a, **k: self


class _MockEng:
    def __getattr__(self, name):
        return lambda *a, **k: _MockInstr()


_EDGE_OPS = frozenset((
    "activation", "add", "mul", "copy", "square", "sqrt", "sign", "memzero",
    "memset", "tensor_copy", "tensor_tensor", "tensor_add", "tensor_sub",
    "tensor_mul", "tensor_max", "tensor_scalar", "tensor_single_scalar",
    "tensor_scalar_add", "tensor_scalar_sub", "tensor_scalar_mul",
    "tensor_scalar_max", "tensor_scalar_min", "tensor_relu", "tensor_reduce",
    "reduce_max", "reduce_sum", "scalar_tensor_tensor", "tensor_tensor_scan",
    "bn_stats", "bn_aggr", "reciprocal", "select", "copy_predicated",
    "iota", "pool", "pool_avg", "pool_max", "stream_shuffle", "engine_nop",
))


class _Edged:

    def __init__(self, env, eng, sem_name):
        self._env = env
        self._eng = eng
        self._sem = sem_name

    def __getattr__(self, name):
        env = self._env
        if name in _EDGE_OPS:
            def f(*a, **k):
                c = env.counts.get(self._sem, 0)
                if env.emit:
                    if c > 0:
                        self._eng.wait_ge(env.sems[self._sem], c)
                    instr = getattr(self._eng, name)(*a, **k)
                    instr.then_inc(env.sems[self._sem], 1)
                else:
                    instr = _MockInstr()
                env.counts[self._sem] = c + 1
                return instr
            return f
        if not env.emit:
            return lambda *a, **k: _MockInstr()
        return getattr(self._eng, name)


class Env:

    AUTO = frozenset(("v", "s", "p"))

    def __init__(self, plan, emit, eng, sems, auto_sem=None):
        self.plan = plan
        self.counts = {}
        self.emit = emit
        self.sems = sems
        self.raw_eng = eng
        if auto_sem is not None:
            self.eng = _Edged(self, eng, auto_sem)
        else:
            self.eng = eng if emit else _MockEng()

    def _bump(self, sem, key, n):
        c = self.counts.get(sem, 0) + n
        self.counts[sem] = c
        if key is not None:
            self._mark(sem, key, c)
        return c

    def _mark(self, sem, key, c):
        if not self.emit:
            assert (sem, key) not in self.plan, ("dup mark", sem, key)
            self.plan[(sem, key)] = c
        else:
            assert self.plan.get((sem, key)) == c, (
                "plan/emit drift", sem, key, self.plan.get((sem, key)), c)

    def inc(self, instr, sem, key=None, n=1):
        if sem in self.AUTO:
            if key is not None:
                self._mark(sem, key, self.counts.get(sem, 0))
            return instr
        self._bump(sem, key, n)
        if self.emit:
            instr.then_inc(self.sems[sem], n)
        return instr

    def wait(self, sem, key):
        if self.emit:
            eng = self.raw_eng if self.raw_eng is not None else None
            eng.wait_ge(self.sems[sem], self.plan[(sem, key)])

    def selfwait(self, sem):
        if self.emit:
            self.raw_eng.wait_ge(self.sems[sem], self.counts[sem])

    def wait_val(self, sem, val):
        if self.emit:
            self.raw_eng.wait_ge(self.sems[sem], val)

    @contextmanager
    def gated(self, reg, j):
        if not self.emit:
            yield
            return
        before = dict(self.counts)
        with self.raw_eng.If_cmp(reg, j, "IS_GT"):
            yield
        with self.raw_eng.Else():
            for sem, c in self.counts.items():
                d = c - before.get(sem, 0)
                if d:
                    self.raw_eng.sem_inc(self.sems[sem], d)


def run_bodies(bodies, sems):
    plan = {}
    for item in bodies:
        attach, body, auto = item
        body(Env(plan, False, None, sems, auto))
    for item in bodies:
        attach, body, auto = item
        attach(lambda eng, body=body, auto=auto: body(
            Env(plan, True, eng, sems, auto)))
    return plan




def emit_ln_stats(E, v, stats, mv, src_lo, src_hi, key):
    v.bn_stats(out=stats[:, 0:6], in_=src_lo)
    E.inc(v.bn_stats(out=stats[:, 6:12], in_=src_hi), "v")
    E.selfwait("v")
    E.inc(v.bn_aggr(out=mv[:], in_=stats[:, 0:12]), "v", key)
    E.selfwait("v")


def emit_rstd(E, s, mv, rstd, eps_ap):
    s.activation(out=rstd[:], in_=mv[:, 1:2], func=AF.Ln, bias=eps_ap)
    return s.activation(out=rstd[:], in_=rstd[:], func=AF.Exp, scale=-0.5)


def sadd_views(dst, src, r, blk_src):
    raise NotImplementedError




def build_state(T):
    nc = bass.Bass("TRN2", target_bir_lowering=False)
    NT = T * TOK
    x_in = nc.dram_tensor("x_own", [NT, D], F32, kind="ExternalInput").ap()
    wqkvT = nc.dram_tensor("wqkvT", [D, 3 * D], F32, kind="ExternalInput").ap()
    bqkv = nc.dram_tensor("bqkv", [1, 3 * D], F32, kind="ExternalInput").ap()
    ident_in = nc.dram_tensor("ident", [P, P], BF16, kind="ExternalInput").ap()
    onescol_in = nc.dram_tensor("onescol", [P, 1], BF16, kind="ExternalInput").ap()
    onesrow_in = nc.dram_tensor("onesrow", [1, P], BF16, kind="ExternalInput").ap()
    stot_out = nc.dram_tensor("stot", [P, 528], F32, kind="ExternalOutput").ap()

    with ExitStack() as ctx:
        ident = ctx.enter_context(nc.sbuf_tensor("identS", [P, P], BF16))
        onescol = ctx.enter_context(nc.sbuf_tensor("onescolS", [P, 1], BF16))
        onesrow = ctx.enter_context(nc.sbuf_tensor("onesrowS", [1, P], BF16))
        brow = ctx.enter_context(nc.sbuf_tensor("brow", [1, 3 * D], BF16))
        wkv_bf = ctx.enter_context(nc.sbuf_tensor("wkv_bf", [P, KC * 2 * D], BF16))
        wld_t = ctx.enter_context(nc.sbuf_tensor("wld", [P, 2 * 3 * D], F32))
        xt_t = ctx.enter_context(nc.sbuf_tensor("xt", [P, 2 * D], F32))
        h1_tm = ctx.enter_context(nc.sbuf_tensor("h1_tm", [P, D], BF16))
        h1T = ctx.enter_context(nc.sbuf_tensor("h1T", [P, D], BF16))
        phik_tm = ctx.enter_context(nc.sbuf_tensor("phik_tm", [P, D], BF16))
        v_tm = ctx.enter_context(nc.sbuf_tensor("v_tm", [P, D], BF16))
        t1 = ctx.enter_context(nc.sbuf_tensor("t1", [P, D], F32))
        t2 = ctx.enter_context(nc.sbuf_tensor("t2", [P, D], F32))
        t3 = ctx.enter_context(nc.sbuf_tensor("t3", [P, D], F32))
        stats = ctx.enter_context(nc.sbuf_tensor("stats", [P, 12], F32))
        mv = ctx.enter_context(nc.sbuf_tensor("mv", [P, 2], F32))
        rstd = ctx.enter_context(nc.sbuf_tensor("rstd", [P, 1], F32))
        c_eps = ctx.enter_context(nc.sbuf_tensor("c_eps", [P, 1], F32))
        sacc = ctx.enter_context(nc.sbuf_tensor("sacc", [P, 528], F32))
        pT = ctx.enter_context(nc.psum_tensor("pT", [P, 1024], BF16))
        pK = ctx.enter_context(nc.psum_tensor("pK", [P, 1024], F32))
        pV = ctx.enter_context(nc.psum_tensor("pV", [P, 1024], F32))
        pS = ctx.enter_context(nc.psum_tensor("pS", [P, 512], F32))
        pC = ctx.enter_context(nc.psum_tensor("pC", [P, 512], F32))
        h_x0 = ctx.enter_context(nc.semaphore("m_x0"))
        h_x1 = ctx.enter_context(nc.semaphore("m_x1"))
        h_c = ctx.enter_context(nc.semaphore("m_c"))
        h_ld = ctx.enter_context(nc.semaphore("m_ld"))
        h_pe = ctx.enter_context(nc.semaphore("m_pe"))
        h_v = ctx.enter_context(nc.semaphore("m_v"))
        h_s = ctx.enter_context(nc.semaphore("m_s"))
        h_out = ctx.enter_context(nc.semaphore("m_out"))
        block = ctx.enter_context(nc.Block())

        wld = [wld_t.ap()[:, 0:3 * D], wld_t.ap()[:, 3 * D:6 * D]]
        xt = [xt_t.ap()[:, 0:D], xt_t.ap()[:, D:2 * D]]
        sems = {"x0": h_x0, "x1": h_x1, "c": h_c, "ld": h_ld, "pe": h_pe,
                "v": h_v, "s": h_s, "out": h_out}

        def body_sync(E):
            sy = E.eng
            for name, dst, srcap in (
                ("ident", ident[:], ident_in),
                ("onescol", onescol[:], onescol_in),
                ("onesrow", onesrow[:], onesrow_in),
            ):
                E.inc(sy.dma_start(out=dst, in_=srcap), "c", name, 16)
            E.inc(sy.dma_start(out=wld[0][0:1, :], in_=bqkv), "c", "call", 16)
            for c in range(KC):
                E.wait("v", "bq" if c == 0 else ("wcv", c - 1))
                E.inc(sy.dma_start(out=wld[0],
                                   in_=wqkvT[c * P:(c + 1) * P, :]),
                      "ld", ("w", c), 16)
            for j in range(T):
                if j >= 2:
                    E.wait("pe", ("h1T", j - 2))
                E.inc(sy.dma_start(out=xt[j % 2],
                                   in_=x_in[j * TOK:(j + 1) * TOK, :]),
                      "x%d" % (j % 2), ("x", j), 16)
            E.wait_val("out", 16)

        def body_vector(E):
            v = E.eng
            E.wait("c", "call")
            E.inc(v.tensor_copy(out=brow[:], in_=wld[0][0:1, :]), "v", "bq")
            for c in range(KC):
                E.wait("ld", ("w", c))
                E.inc(v.tensor_copy(out=wkv_bf[:, c * 2 * D:(c + 1) * 2 * D],
                                    in_=wld[0][:, D:3 * D]),
                      "v", ("wcv", c))
            E.inc(v.memset(sacc[:], 0.0), "v", "sz")
            E.inc(v.memset(c_eps[:], LN_EPS), "v", "ceps")
            for j in range(T):
                E.wait("x%d" % (j % 2), ("x", j))
                emit_ln_stats(E, v, stats, mv, xt[j % 2][:, 0:512],
                              xt[j % 2][:, 512:1024], ("xa", j))
                E.wait("s", ("rstd", j))
                E.inc(v.tensor_scalar(out=h1_tm[:], in0=xt[j % 2],
                                      scalar1=mv[:, 0:1], scalar2=rstd[:],
                                      op0=ALU.subtract, op1=ALU.mult),
                      "v", ("h1", j))
                E.wait("pe", ("h1T", j))
                E.inc(v.tensor_copy(out=h1T[:], in_=pT[:]), "v", ("h1T", j))
                E.wait("s", ("kel", j))
                E.inc(v.scalar_tensor_tensor(out=phik_tm[:], in0=t2[:],
                                             scalar=FEAT_EPS, in1=t3[:],
                                             op0=ALU.add, op1=ALU.add),
                      "v", ("phik", j))
                for r in range(2):
                    E.wait("pe", ("st", j, r))
                    ps = (pS if r == 0 else pC)[:]
                    sview = sacc[:, 0:512]
                    v.tensor_tensor(
                        out=sview[0:64].rearrange("p (g f) -> p g f", f=64
                                                  )[:, 4 * r:4 * r + 4, :],
                        in0=sview[0:64].rearrange("p (g f) -> p g f", f=64
                                                  )[:, 4 * r:4 * r + 4, :],
                        in1=ps[0:64].rearrange("p (g f) -> p g f", f=P
                                               )[:, :, 0:64],
                        op=ALU.add)
                    E.inc(v.tensor_tensor(
                        out=sview[64:128].rearrange("p (g f) -> p g f", f=64
                                                    )[:, 4 * r:4 * r + 4, :],
                        in0=sview[64:128].rearrange("p (g f) -> p g f", f=64
                                                    )[:, 4 * r:4 * r + 4, :],
                        in1=ps[64:128].rearrange("p (g f) -> p g f", f=P
                                                 )[:, :, 64:128],
                        op=ALU.add), "v", ("sadd", j, r))
                E.wait("pe", ("scol", j))
                sc = sacc[:, 512:528]
                v.tensor_tensor(
                    out=sc[0:64].rearrange("p (g o) -> p g o", o=1)[:, 0:16:2, :],
                    in0=sc[0:64].rearrange("p (g o) -> p g o", o=1)[:, 0:16:2, :],
                    in1=pK[0:64, 0:8].rearrange("p (g o) -> p g o", o=1),
                    op=ALU.add)
                E.inc(v.tensor_tensor(
                    out=sc[64:128].rearrange("p (g o) -> p g o", o=1)[:, 1:16:2, :],
                    in0=sc[64:128].rearrange("p (g o) -> p g o", o=1)[:, 1:16:2, :],
                    in1=pK[64:128, 0:8].rearrange("p (g o) -> p g o", o=1),
                    op=ALU.add), "v", ("scadd", j))

        def body_scalar(E):
            s = E.eng
            E.wait("v", "ceps")
            for j in range(T):
                E.wait("v", ("xa", j))
                E.inc(emit_rstd(E, s, mv, rstd, c_eps[:]), "s", ("rstd", j))
                E.wait("pe", ("k", j))
                s.activation(out=t1[:], in_=pK[:], func=AF.Relu, scale=-1.0)
                s.activation(out=t2[:], in_=t1[:], func=AF.Exp, scale=-1.0)
                E.inc(s.activation(out=t3[:], in_=pK[:], func=AF.Relu),
                      "s", ("kel", j))
                E.wait("pe", ("v", j))
                E.inc(s.activation(out=v_tm[:], in_=pV[:], func=AF.Copy),
                      "s", ("v", j))
            E.wait("v", ("scadd", T - 1))
            E.inc(s.dma_start(out=stot_out, in_=sacc[:]), "out", "res", 16)

        def body_tensor(E):
            t = E.eng
            for j in range(T):
                E.wait("v", ("h1", j))
                if j == 0:
                    E.wait("c", "call")
                if j > 0:
                    E.wait("v", ("h1T", j - 1))
                for c in range(KC):
                    ins = t.transpose(pT[:, c * P:(c + 1) * P],
                                      h1_tm[:, c * P:(c + 1) * P], ident[:])
                E.inc(ins, "pe", ("h1T", j))
                E.wait("v", ("h1T", j))
                if j == 0:
                    E.wait("v", ("wcv", KC - 1))
                for half, pdst, name in ((0, pK, "k"), (1, pV, "v")):
                    if j > 0:
                        if half == 0:
                            E.wait("s", ("kel", j - 1))
                            E.wait("v", ("scadd", j - 1))
                        else:
                            E.wait("s", ("v", j - 1))
                            E.wait("pe", ("st", j - 1, 1))
                    for n4 in range(2):
                        for c in range(KC):
                            t.matmul(pdst[:, n4 * 512:(n4 + 1) * 512],
                                     h1T[:, c * P:(c + 1) * P],
                                     wkv_bf[:, c * 2 * D + half * D + n4 * 512:
                                            c * 2 * D + half * D + (n4 + 1) * 512],
                                     start=(c == 0), stop=False)
                        ins = t.matmul(pdst[:, n4 * 512:(n4 + 1) * 512],
                                       onesrow[:],
                                       brow[0:1, D + half * D + n4 * 512:
                                            D + half * D + (n4 + 1) * 512],
                                       start=False, stop=True)
                    E.inc(ins, "pe", (name, j))
                E.wait("v", ("phik", j))
                E.wait("s", ("v", j))
                for r in range(2):
                    ps = pS if r == 0 else pC
                    if j > 0:
                        E.wait("v", ("sadd", j - 1, r))
                    for pp in range(4):
                        pr = r * 4 + pp
                        ins = t.matmul(ps[:, pp * P:(pp + 1) * P],
                                       phik_tm[:, pr * P:(pr + 1) * P],
                                       v_tm[:, pr * P:(pr + 1) * P],
                                       start=True, stop=True)
                    E.inc(ins, "pe", ("st", j, r))
                E.wait("s", ("kel", j))
                if j > 0:
                    E.wait("v", ("scadd", j - 1))
                for pr in range(NPAIR):
                    ins = t.matmul(pK[:, pr:pr + 1],
                                   phik_tm[:, pr * P:(pr + 1) * P],
                                   onescol[:], start=True, stop=True)
                E.inc(ins, "pe", ("scol", j))

        run_bodies([
            (block.sync, body_sync, None),
            (block.vector, body_vector, "v"),
            (block.scalar, body_scalar, "s"),
            (block.tensor, body_tensor, None),
        ], sems)
    return nc




def build_main(T, upto='full', fused=True):
    STAGES = ['setup', 'h1', 'qkv', 'attn', 'wout', 'main', 'full']
    LVL = STAGES.index(upto)
    NPRE = 3 * T if fused else 0
    nc = bass.Bass("TRN2", target_bir_lowering=False)
    NT = T * TOK
    HT = (T + 1) // 2
    HLEN = HT * TOK
    SL = 256
    NSL = HID // SL

    x_in = nc.dram_tensor("x_main", [NT, D], F32, kind="ExternalInput").ap()
    wqkvT_in = nc.dram_tensor("wqkvT", [D, 3 * D], F32, kind="ExternalInput").ap()
    woutT_in = nc.dram_tensor("woutT", [D, D], F32, kind="ExternalInput").ap()
    bqkv_in = nc.dram_tensor("bqkv", [1, 3 * D], F32, kind="ExternalInput").ap()
    bout_in = nc.dram_tensor("bout", [1, D], F32, kind="ExternalInput").ap()
    gamma_in = nc.dram_tensor("gamma", [1, D], F32, kind="ExternalInput").ap()
    beta_in = nc.dram_tensor("beta", [1, D], F32, kind="ExternalInput").ap()
    wgT_in = nc.dram_tensor("wgT", [D, HID], I32, kind="ExternalInput").ap()
    wuT_in = nc.dram_tensor("wuT", [D, HID], I32, kind="ExternalInput").ap()
    wdT_in = nc.dram_tensor("wdT", [HID, D], I32, kind="ExternalInput").ap()
    maskT4_in = nc.dram_tensor("maskT4", [P, 512], F32, kind="ExternalInput").ap()
    ident_in = nc.dram_tensor("ident", [P, P], BF16, kind="ExternalInput").ap()
    onescol_in = nc.dram_tensor("onescol", [P, 1], BF16, kind="ExternalInput").ap()
    onesrow_in = nc.dram_tensor("onesrow", [1, P], BF16, kind="ExternalInput").ap()
    onesf_in = nc.dram_tensor("onesf", [1, P], F32, kind="ExternalInput").ap()
    if fused:
        x_pre = nc.dram_tensor("x_pre", [3 * T * TOK, D], F32,
                               kind="ExternalInput").ap()
        pflag_in = nc.dram_tensor("pflag", [P, 3 * T], F32,
                                  kind="ExternalInput").ap()
    else:
        stot_in = nc.dram_tensor("stot_all", [P, 8 * 528], F32,
                                 kind="ExternalInput").ap()
        maskbc_in = nc.dram_tensor("mask_bc", [P, 8], F32,
                                   kind="ExternalInput").ap()

    out = nc.dram_tensor("out", [NT, D], F32, kind="ExternalOutput").ap()
    x2_hbm = nc.dram_tensor("x2_scratch", [NT, D], F32, kind="Internal").ap()

    with ExitStack() as ctx:
        maskT4 = ctx.enter_context(nc.sbuf_tensor("maskT4S", [P, 512], F32))
        ident = ctx.enter_context(nc.sbuf_tensor("identS", [P, P], BF16))
        onescol = ctx.enter_context(nc.sbuf_tensor("onescolS", [P, 1], BF16))
        onesrow = ctx.enter_context(nc.sbuf_tensor("onesrowS", [1, P], BF16))
        onesf = ctx.enter_context(nc.sbuf_tensor("onesfS", [1, P], F32))
        brow = ctx.enter_context(nc.sbuf_tensor("brow", [1, 3 * D], BF16))
        borow = ctx.enter_context(nc.sbuf_tensor("borow", [1, D], BF16))
        stats = ctx.enter_context(nc.sbuf_tensor("stats", [P, 12], F32))
        mv = ctx.enter_context(nc.sbuf_tensor("mv", [P, 2], F32))
        rstd = ctx.enter_context(nc.sbuf_tensor("rstd", [P, 1], F32))
        c_eps = ctx.enter_context(nc.sbuf_tensor("c_eps", [P, 1], F32))
        mv_a = ctx.enter_context(nc.sbuf_tensor("mv_a", [P, 2], F32))
        mv_b = ctx.enter_context(nc.sbuf_tensor("mv_b", [P, 2], F32))
        rstd_a = ctx.enter_context(nc.sbuf_tensor("rstd_a", [P, 1], F32))
        rstd_b = ctx.enter_context(nc.sbuf_tensor("rstd_b", [P, 1], F32))
        pflag = ctx.enter_context(nc.sbuf_tensor("pflagS", [P, 3 * T], F32))
        rden = ctx.enter_context(nc.sbuf_tensor("rden", [P, 16], F32))
        rdpre = ctx.enter_context(nc.sbuf_tensor("rdpre", [P, 16], F32))
        gbc = ctx.enter_context(nc.sbuf_tensor("gbc", [P, D], F32))
        bbc = ctx.enter_context(nc.sbuf_tensor("bbc", [P, D], F32))
        wq_bf = ctx.enter_context(nc.sbuf_tensor("wq_bf", [P, KC * 3 * D], BF16))
        wo_bf = ctx.enter_context(nc.sbuf_tensor("wo_bf", [P, KC * D], BF16))
        stall = ctx.enter_context(nc.sbuf_tensor("stall", [P, 8 * 528], F32))
        mbc = ctx.enter_context(nc.sbuf_tensor("mbc", [P, 8], F32))
        sini = ctx.enter_context(nc.sbuf_tensor("sini", [P, 528], F32))
        srun = ctx.enter_context(nc.sbuf_tensor("srun", [P, 8 * P], F32))
        ssum = ctx.enter_context(nc.sbuf_tensor("ssum", [P, 16], F32))
        srun_bf = ctx.enter_context(nc.sbuf_tensor("srun_bf", [P, 8 * P], BF16))
        ssum_bf = ctx.enter_context(nc.sbuf_tensor("ssum_bf", [P, 16], BF16))
        xt_t = ctx.enter_context(nc.sbuf_tensor("xt", [P, 2 * D], F32))
        h1_tm = ctx.enter_context(nc.sbuf_tensor("h1_tm", [P, D], BF16))
        h1T = ctx.enter_context(nc.sbuf_tensor("h1T", [P, D], BF16))
        phiq = ctx.enter_context(nc.sbuf_tensor("phiq", [P, D], BF16))
        phiq_e = ctx.enter_context(nc.sbuf_tensor("phiq_e", [P, D], BF16))
        phiq_o = ctx.enter_context(nc.sbuf_tensor("phiq_o", [P, D], BF16))
        phik_tm = ctx.enter_context(nc.sbuf_tensor("phik_tm", [P, D], BF16))
        phik_fm = ctx.enter_context(nc.sbuf_tensor("phik_fm", [P, D], BF16))
        v_tm = ctx.enter_context(nc.sbuf_tensor("v_tm", [P, D], BF16))
        am = ctx.enter_context(nc.sbuf_tensor("am", [P, 512], BF16))
        attnT = ctx.enter_context(nc.sbuf_tensor("attnT", [P, D], BF16))
        attn_bf = ctx.enter_context(nc.sbuf_tensor("attn_bf", [P, D], BF16))
        t1 = ctx.enter_context(nc.sbuf_tensor("t1", [P, D], F32))
        t2 = ctx.enter_context(nc.sbuf_tensor("t2", [P, D], F32))
        t3 = ctx.enter_context(nc.sbuf_tensor("t3", [P, D], F32))
        o_n = ctx.enter_context(nc.sbuf_tensor("o_n", [P, D], F32))
        a_g = ctx.enter_context(nc.sbuf_tensor("a_g", [P, D], F32))
        x2sb_t = ctx.enter_context(nc.sbuf_tensor("x2sb", [P, 2 * D], F32))
        h2T = ctx.enter_context(nc.sbuf_tensor("h2T", [P, KC * 1024], BF16))
        wg_sl_t = ctx.enter_context(nc.sbuf_tensor("wg_sl", [P, 2 * KC * SL], BF16))
        wu_sl_t = ctx.enter_context(nc.sbuf_tensor("wu_sl", [P, 2 * KC * SL], BF16))
        silu_t = ctx.enter_context(nc.sbuf_tensor("silu_t", [P, 1024], F32))
        f_n = ctx.enter_context(nc.sbuf_tensor("f_n", [P, D], F32))
        wdx = ctx.enter_context(nc.sbuf_tensor("wdx", [P, 1024], BF16))
        pT = ctx.enter_context(nc.psum_tensor("pT", [P, 1024], BF16))
        pQ = ctx.enter_context(nc.psum_tensor("pQ", [P, 1024], F32))
        pK = ctx.enter_context(nc.psum_tensor("pK", [P, 1024], F32))
        pV = ctx.enter_context(nc.psum_tensor("pV", [P, 1024], F32))
        pD = ctx.enter_context(nc.psum_tensor("pD", [P, 512], F32))
        h_x0 = ctx.enter_context(nc.semaphore("s_x0"))
        h_x1 = ctx.enter_context(nc.semaphore("s_x1"))
        h_c = ctx.enter_context(nc.semaphore("s_c"))
        h_ld = ctx.enter_context(nc.semaphore("s_ld"))
        h_pe = ctx.enter_context(nc.semaphore("s_pe"))
        h_v = ctx.enter_context(nc.semaphore("s_v"))
        h_s = ctx.enter_context(nc.semaphore("s_s"))
        h_p = ctx.enter_context(nc.semaphore("s_p"))
        h_w0 = ctx.enter_context(nc.semaphore("s_w0"))
        h_w1 = ctx.enter_context(nc.semaphore("s_w1"))
        h_o0 = ctx.enter_context(nc.semaphore("s_o0"))
        h_o1 = ctx.enter_context(nc.semaphore("s_o1"))
        h_p0 = ctx.enter_context(nc.semaphore("s_p0"))
        h_p1 = ctx.enter_context(nc.semaphore("s_p1"))
        h_g = ctx.enter_context(nc.semaphore("s_g"))
        block = ctx.enter_context(nc.Block())

        xt = [xt_t.ap()[:, 0:D], xt_t.ap()[:, D:2 * D]]
        mvp = [mv_a, mv_b]
        rstdp = [rstd_a, rstd_b]
        x2sb = [x2sb_t.ap()[:, 0:D], x2sb_t.ap()[:, D:2 * D]]
        wg_sl = [wg_sl_t.ap()[:, 0:KC * SL], wg_sl_t.ap()[:, KC * SL:]]
        wu_sl = [wu_sl_t.ap()[:, 0:KC * SL], wu_sl_t.ap()[:, KC * SL:]]
        sems = {"x0": h_x0, "x1": h_x1, "c": h_c, "ld": h_ld, "pe": h_pe,
                "v": h_v, "s": h_s, "p": h_p, "w0": h_w0, "w1": h_w1,
                "o0": h_o0, "o1": h_o1, "px0": h_p0, "px1": h_p1,
                "g": h_g}
        if upto == 'full':
            assert HLEN == 1024 and T == 16, "V1 layout sized for T=16"

        wld = h2T.ap().bitcast(F32)[:, 0:3 * D]
        hst = wq_bf.ap()
        wgu_int = stall.ap().bitcast(I32)[:, 0:KC * SL]
        wuu_int = stall.ap().bitcast(I32)[:, KC * SL:2 * KC * SL]
        wd_int = srun.ap().bitcast(I32)[:, 0:1024]
        wd_bf = []
        for tens in (t1, t2, t3, o_n, a_g, gbc, bbc):
            b = tens.ap().bitcast(BF16)
            wd_bf.append(b[:, 0:1024])
            wd_bf.append(b[:, 1024:2048])
        wd_bf.append(sini.ap().bitcast(BF16)[:, 0:1024])
        wd_bf.append(srun_bf.ap()[:, 0:1024])
        for tens in (phiq, phik_tm, phik_fm, v_tm, wdx, h1T, attnT,
                     attn_bf):
            wd_bf.append(tens.ap())
        assert len(wd_bf) == HC
        x2r = [xt[0], xt[1]]
        outsb = [x2sb[0], x2sb[1]]
        xp = [x2sb[0], x2sb[1]]
        pT32 = pT.ap().bitcast(F32)

        NU = 2
        UNIT_LIST = []
        for ha in range(2):
            for sl in range(NSL):
                for u in range(NU):
                    UNIT_LIST.append((ha * NSL + sl, u))
        KSEQ_OF = {ku: i for i, ku in enumerate(UNIT_LIST)}
        GUP_BANKS = {}
        for i, ku in enumerate(UNIT_LIST):
            if i % 2 == 0:
                GUP_BANKS[ku] = (pQ[:], [pK[:]])
            else:
                GUP_BANKS[ku] = (pV[:], [pT32, pD[:]])

        def halves():
            for ha in range(2):
                yield (ha, 0, HT) if ha == 0 else (ha, HT, T - HT)

        LASTK = [NSL - 1, 2 * NSL - 1]
        H0_UNITS = [ku for ku in UNIT_LIST if ku[0] < NSL]
        H0_LAST_ODD = [ku for ku in H0_UNITS if KSEQ_OF[ku] % 2 == 1][-1]

        def body_sync(E):
            sy = E.eng
            cl = [("maskT4", maskT4[:], maskT4_in),
                  ("ident", ident[:], ident_in),
                  ("onescol", onescol[:], onescol_in),
                  ("onesrow", onesrow[:], onesrow_in),
                  ("onesf", onesf[:], onesf_in)]
            if fused:
                cl += [("pflag", pflag[:], pflag_in)]
            else:
                cl += [("stall", stall[:], stot_in),
                       ("mbc", mbc[:], maskbc_in)]
            for name, dst, srcap in cl:
                E.inc(sy.dma_start(out=dst, in_=srcap), "c", name, 16)
            E.inc(sy.dma_start(out=wld[0:1, :], in_=bqkv_in), "c", "bq", 16)
            E.inc(sy.dma_start(out=t1[0:1, 0:D], in_=bout_in), "c", "bo", 16)
            E.inc(sy.dma_start(out=t2[0:1, 0:D], in_=gamma_in), "c", "ga", 16)
            E.inc(sy.dma_start(out=t3[0:1, 0:D], in_=beta_in), "c", "call", 16)
            for c in range(KC):
                if c == 0:
                    E.wait("v", "rows")
                    E.wait("g", "gbcmm")
                else:
                    E.wait("v", ("wq", c - 1))
                E.inc(sy.dma_start(out=wld[:, :],
                                   in_=wqkvT_in[c * P:(c + 1) * P, :]),
                      "ld", ("wq", c), 16)
            for c in range(KC):
                E.wait("v", ("wq", KC - 1) if c == 0 else ("wo", c - 1))
                E.inc(sy.dma_start(out=wld[:, 0:D],
                                   in_=woutT_in[c * P:(c + 1) * P, :]),
                      "ld", ("wo", c), 16)
            if upto == 'setup':
                return
            for j in range(min(2, T)):
                E.inc(sy.dma_start(out=xt[j % 2],
                                   in_=x_in[j * TOK:(j + 1) * TOK, :]),
                      "x%d" % (j % 2), ("x", j), 16)
            if fused:
                for pj in range(NPRE):
                    if pj >= 2:
                        E.wait("v", ("ph1", pj - 2))
                        E.wait("px%d" % (pj % 2), ("px", pj - 2))
                    E.inc(sy.dma_start(
                        out=xp[pj % 2],
                        in_=x_pre[pj * TOK:(pj + 1) * TOK, :]),
                        "px%d" % (pj % 2), ("px", pj), 16)
            for j in range(2, T):
                if LVL >= 5:
                    E.wait("v", ("x2", j - 2))
                else:
                    E.wait("v", ("h1", j - 2))
                E.inc(sy.dma_start(out=xt[j % 2],
                                   in_=x_in[j * TOK:(j + 1) * TOK, :]),
                      "x%d" % (j % 2), ("x", j), 16)
            if upto != 'full':
                if upto == 'main':
                    for jp in range(max(0, T - 2), T):
                        E.wait("w%d" % (jp % 2), ("x2w", jp))
                return
            buffree = [("v", ("x2", T - 1)), ("v", ("x2", T - 1))]
            PREV_CONV = [None]
            for ha, t0, tn in halves():
                for tt in range(tn):
                    tg = t0 + tt
                    E.wait("w%d" % (tg % 2), ("x2w", tg))
                    E.wait(*buffree[tt % 2])
                    E.inc(sy.dma_start(out=x2r[tt % 2],
                                       in_=x2_hbm[tg * TOK:(tg + 1) * TOK, :]),
                          "x%d" % (tt % 2), ("x2a", ha, tt), 16)
                    buffree[tt % 2] = ("v", ("h2", ha, tt))
                for sl in range(NSL):
                    kseq = ha * NSL + sl
                    if kseq == 0:
                        E.wait("v", "sexp")
                        E.wait("v", ("wo", KC - 1))
                    else:
                        E.wait("s", PREV_CONV[0])
                    if kseq >= 2:
                        E.wait("pe", ("gup", kseq - 2, NU - 1))
                    E.inc(sy.dma_start(
                        out=wgu_int.rearrange("p (c h) -> p c h", c=KC),
                        in_=wgT_in[:, sl * SL:(sl + 1) * SL].rearrange(
                            "(c p) h -> p c h", p=P)), "ld", ("wg", kseq), 16)
                    E.wait("s", ("wgc", kseq))
                    if kseq >= 2:
                        E.wait("pe", ("gup", kseq - 2, NU - 1))
                    E.inc(sy.dma_start(
                        out=wuu_int.rearrange("p (c h) -> p c h", c=KC),
                        in_=wuT_in[:, sl * SL:(sl + 1) * SL].rearrange(
                            "(c p) h -> p c h", p=P)), "ld", ("wu", kseq), 16)
                    PREV_CONV[0] = ("wuc", kseq)
                    if ha == 0:
                        for c in (2 * sl, 2 * sl + 1):
                            E.wait("s", PREV_CONV[0])
                            if c == 0:
                                E.wait("v", ("x2", T - 1))
                                E.wait("v", ("st", T - 1))
                                E.wait("w0", ("x2w", T - 2))
                                E.wait("w1", ("x2w", T - 1))
                            E.inc(sy.dma_start(
                                out=wd_int,
                                in_=wdT_in[c * P:(c + 1) * P, :]),
                                "ld", ("wd", c), 16)
                            PREV_CONV[0] = ("wdc", c)
                for tt in range(tn):
                    tg = t0 + tt
                    E.wait(*buffree[tt % 2])
                    E.inc(sy.dma_start(out=x2r[tt % 2],
                                       in_=x2_hbm[tg * TOK:(tg + 1) * TOK, :]),
                          "x%d" % (tt % 2), ("x2b", ha, tt), 16)
                    buffree[tt % 2] = ("v", ("res", ha, tt))
            E.wait("o0", ("out", 1, T - HT - 2))
            E.wait("o1", ("out", 1, T - HT - 1))

        def body_vector(E):
            v = E.eng
            E.wait("c", "call")
            v.tensor_copy(out=brow[:], in_=wld[0:1, :])
            E.inc(v.tensor_copy(out=borow[:], in_=t1[0:1, 0:D]), "v", "rows")
            E.wait("g", "gbcmm")
            v.tensor_copy(out=gbc[:], in_=pQ[:])
            E.inc(v.tensor_copy(out=bbc[:], in_=pK[:]), "v", "gbcc")
            for c in range(KC):
                E.wait("ld", ("wq", c))
                E.inc(v.tensor_copy(out=wq_bf[:, c * 3 * D:(c + 1) * 3 * D],
                                    in_=wld[:, :]), "v", ("wq", c))
            for c in range(KC):
                E.wait("ld", ("wo", c))
                E.inc(v.tensor_copy(out=wo_bf[:, c * D:(c + 1) * D],
                                    in_=wld[:, 0:D]), "v", ("wo", c))
            E.inc(v.memset(c_eps[:], LN_EPS), "v", "ceps")
            v.memset(phiq_e[:], 0.0)
            E.inc(v.memset(phiq_o[:], 0.0), "v", "phiqz0")
            if fused:
                v.memset(h1_tm[:], 0.0)
                v.memset(h1T[:], 0.0)
                v.memset(phik_tm[:], 0.0)
                E.inc(v.memset(v_tm[:], 0.0), "v", "pz")
            if fused:
                v.memset(srun[:], 0.0)
                E.inc(v.memset(ssum[:], 0.0), "v", "sexp")
            else:
                v.tensor_scalar(out=sini[:], in0=stall[:, 0:528],
                                scalar1=mbc[:, 0:1], scalar2=None,
                                op0=ALU.mult)
                for r in range(1, 8):
                    v.tensor_scalar(out=t1[:, 0:528],
                                    in0=stall[:, r * 528:(r + 1) * 528],
                                    scalar1=mbc[:, r:r + 1], scalar2=None,
                                    op0=ALU.mult)
                    v.tensor_tensor(out=sini[:], in0=sini[:],
                                    in1=t1[:, 0:528], op=ALU.add)
                v.memset(srun[:], 0.0)
                v.tensor_copy(
                    out=srun[0:64, :].rearrange("p (g f) -> p g f", f=P
                                                )[:, :, 0:64],
                    in_=sini[0:64, 0:512].rearrange("p (g f) -> p g f", f=64))
                v.tensor_copy(
                    out=srun[64:128, :].rearrange("p (g f) -> p g f", f=P
                                                  )[:, :, 64:128],
                    in_=sini[64:128, 0:512].rearrange("p (g f) -> p g f",
                                                      f=64))
                E.inc(v.tensor_copy(out=ssum[:], in_=sini[:, 512:528]),
                      "v", "sexp")
            if upto == 'setup':
                return
            if fused:
                E.wait("px0", ("px", 0))
                emit_ln_stats(E, v, stats, mvp[0], xp[0][:, 0:512],
                              xp[0][:, 512:1024], ("pxa", 0))
                E.wait("s", ("prstd", 0))
                E.inc(v.tensor_scalar(out=h1_tm[:], in0=xp[0],
                                      scalar1=mvp[0][:, 0:1],
                                      scalar2=rstdp[0][:],
                                      op0=ALU.subtract, op1=ALU.mult),
                      "v", ("ph1", 0))
                for pj in range(NPRE):
                    E.wait("pe", ("pxh1T", pj))
                    if pj > 0:
                        E.wait("pe", ("pxv", pj - 1))
                    E.inc(v.tensor_copy(out=h1T[:], in_=pT[:]),
                          "v", ("ph1T", pj))
                    if pj + 1 < NPRE:
                        E.wait("px%d" % ((pj + 1) % 2), ("px", pj + 1))
                        emit_ln_stats(E, v, stats, mvp[(pj + 1) % 2],
                                      xp[(pj + 1) % 2][:, 0:512],
                                      xp[(pj + 1) % 2][:, 512:1024],
                                      ("pxa", pj + 1))
                        E.wait("s", ("prstd", pj + 1))
                        E.inc(v.tensor_scalar(
                            out=h1_tm[:], in0=xp[(pj + 1) % 2],
                            scalar1=mvp[(pj + 1) % 2][:, 0:1],
                            scalar2=rstdp[(pj + 1) % 2][:],
                            op0=ALU.subtract, op1=ALU.mult),
                            "v", ("ph1", pj + 1))
                    E.wait("s", ("pkel", pj))
                    E.inc(v.scalar_tensor_tensor(
                        out=phik_tm[:], in0=t2[:], scalar=FEAT_EPS,
                        in1=t3[:], op0=ALU.add, op1=ALU.add),
                        "v", ("pphik", pj))
                    for r in range(2):
                        E.wait("pe", ("pxst", pj, r))
                        ps = pQ[:, 512 * r:512 * (r + 1)]
                        v.scalar_tensor_tensor(
                            out=srun[0:64, :].rearrange(
                                "p (g f) -> p g f", f=P
                            )[:, 4 * r:4 * r + 4, 0:64],
                            in0=ps[0:64].rearrange("p (g f) -> p g f",
                                                   f=P)[:, :, 0:64],
                            scalar=pflag[0:64, pj:pj + 1],
                            in1=srun[0:64, :].rearrange(
                                "p (g f) -> p g f", f=P
                            )[:, 4 * r:4 * r + 4, 0:64],
                            op0=ALU.mult, op1=ALU.add)
                        E.inc(v.scalar_tensor_tensor(
                            out=srun[64:128, :].rearrange(
                                "p (g f) -> p g f", f=P
                            )[:, 4 * r:4 * r + 4, 64:128],
                            in0=ps[64:128].rearrange("p (g f) -> p g f",
                                                     f=P)[:, :, 64:128],
                            scalar=pflag[64:128, pj:pj + 1],
                            in1=srun[64:128, :].rearrange(
                                "p (g f) -> p g f", f=P
                            )[:, 4 * r:4 * r + 4, 64:128],
                            op0=ALU.mult, op1=ALU.add), "v",
                            ("psadd", pj, r))
                    E.wait("pe", ("pxscol", pj))
                    v.scalar_tensor_tensor(
                        out=ssum[0:64, :].rearrange("p (g o) -> p g o",
                                                    o=1)[:, 0:16:2, :],
                        in0=pD[0:64, 32:40].rearrange("p (g o) -> p g o",
                                                      o=1),
                        scalar=pflag[0:64, pj:pj + 1],
                        in1=ssum[0:64, :].rearrange("p (g o) -> p g o",
                                                    o=1)[:, 0:16:2, :],
                        op0=ALU.mult, op1=ALU.add)
                    E.inc(v.scalar_tensor_tensor(
                        out=ssum[64:128, :].rearrange("p (g o) -> p g o",
                                                      o=1)[:, 1:16:2, :],
                        in0=pD[64:128, 32:40].rearrange(
                            "p (g o) -> p g o", o=1),
                        scalar=pflag[64:128, pj:pj + 1],
                        in1=ssum[64:128, :].rearrange("p (g o) -> p g o",
                                                      o=1)[:, 1:16:2, :],
                        op0=ALU.mult, op1=ALU.add), "v", ("pst", pj))
                E._mark("v", "preend", E.counts.get("v", 0))
            E.wait("x0", ("x", 0))
            emit_ln_stats(E, v, stats, mvp[0], xt[0][:, 0:512],
                          xt[0][:, 512:1024], ("xa", 0))
            for j in range(T):
                E.wait("s", ("xrstd", j))
                E.inc(v.tensor_scalar(out=h1_tm[:], in0=xt[j % 2],
                                      scalar1=mvp[j % 2][:, 0:1],
                                      scalar2=rstdp[j % 2][:],
                                      op0=ALU.subtract, op1=ALU.mult),
                      "v", ("h1", j))
                E.wait("pe", ("h1T", j))
                E.inc(v.tensor_copy(out=h1T[:], in_=pT[:]), "v", ("h1T", j))
                if LVL < 2:
                    continue
                if LVL >= 3:
                    if j > 0:
                        E.wait("pe", ("scol", j - 1))
                    v.tensor_copy(out=srun_bf[:], in_=srun[:])
                    E.inc(v.tensor_copy(out=ssum_bf[:], in_=ssum[:]),
                          "v", ("scast", j))
                E.wait("s", ("qel", j))
                E.inc(v.scalar_tensor_tensor(out=phiq[:], in0=t2[:],
                                             scalar=FEAT_EPS, in1=t3[:],
                                             op0=ALU.add, op1=ALU.add),
                      "v", ("phiq", j))
                if LVL >= 3:
                    if j > 0:
                        E.wait("pe", ("at", j - 1, 3))
                    v.tensor_copy(out=phiq_e[0:64, :], in_=phiq[0:64, :])
                    E.inc(v.tensor_copy(out=phiq_o[64:128, :],
                                        in_=phiq[64:128, :]),
                          "v", ("phiqz", j))
                E.wait("s", ("kel", j))
                E.inc(v.scalar_tensor_tensor(out=phik_tm[:], in0=t1[:],
                                             scalar=FEAT_EPS, in1=t3[:],
                                             op0=ALU.add, op1=ALU.add),
                      "v", ("phik", j))
                E.wait("pe", ("kfm", j))
                E.inc(v.tensor_copy(out=phik_fm[:], in_=pT[:]),
                      "v", ("kfm", j))
                if j + 1 < T:
                    E.wait("x%d" % ((j + 1) % 2), ("x", j + 1))
                    emit_ln_stats(E, v, stats, mvp[(j + 1) % 2],
                                  xt[(j + 1) % 2][:, 0:512],
                                  xt[(j + 1) % 2][:, 512:1024], ("xa", j + 1))
                if LVL < 3:
                    continue
                if j == 0:
                    E.wait("c", "call")
                for g in range(4):
                    E.wait("pe", ("at", j, g))
                    if g > 0:
                        E.wait("pe", ("avg", j, g - 1))
                    E.inc(v.tensor_tensor(out=am[:], in0=pV[:, 0:512],
                                          in1=maskT4[:], op=ALU.mult),
                          "v", ("mask", j, g))
                E.wait("pe", ("avg", j, 3))
                if j > 0:
                    E.wait("s", ("div", j - 1))
                E.inc(v.tensor_scalar(out=rdpre[:], in0=pD[:, 0:16],
                                      scalar1=DEN_EPS, scalar2=None,
                                      op0=ALU.add), "v")
                E.selfwait("v")
                E.inc(v.reciprocal(out=rden[:], in_=rdpre[:]),
                      "v", ("rden", j))
                E.selfwait("v")
                for h in range(8, 16):
                    ins = v.tensor_scalar(
                        out=attn_bf[:, h * EH:(h + 1) * EH],
                        in0=pK[:, h * EH:(h + 1) * EH],
                        scalar1=rden[:, h:h + 1], scalar2=None, op0=ALU.mult)
                E.inc(ins, "v", ("div", j))
                for r in range(2):
                    E.wait("pe", ("st", j, r))
                    ps = pV[:, 512:1024]
                    v.tensor_tensor(
                        out=srun[0:64, :].rearrange("p (g f) -> p g f", f=P
                                                    )[:, 4 * r:4 * r + 4, 0:64],
                        in0=srun[0:64, :].rearrange("p (g f) -> p g f", f=P
                                                    )[:, 4 * r:4 * r + 4, 0:64],
                        in1=ps[0:64].rearrange("p (g f) -> p g f", f=P
                                               )[:, :, 0:64],
                        op=ALU.add)
                    E.inc(v.tensor_tensor(
                        out=srun[64:128, :].rearrange("p (g f) -> p g f", f=P
                                                      )[:, 4 * r:4 * r + 4,
                                                        64:128],
                        in0=srun[64:128, :].rearrange("p (g f) -> p g f", f=P
                                                      )[:, 4 * r:4 * r + 4,
                                                        64:128],
                        in1=ps[64:128].rearrange("p (g f) -> p g f", f=P
                                                 )[:, :, 64:128],
                        op=ALU.add), "v", ("sadd", j, r))
                E.wait("pe", ("scol", j))
                v.tensor_tensor(
                    out=ssum[0:64, :].rearrange("p (g o) -> p g o", o=1
                                                )[:, 0:16:2, :],
                    in0=ssum[0:64, :].rearrange("p (g o) -> p g o", o=1
                                                )[:, 0:16:2, :],
                    in1=pD[0:64, 32:40].rearrange("p (g o) -> p g o", o=1),
                    op=ALU.add)
                E.inc(v.tensor_tensor(
                    out=ssum[64:128, :].rearrange("p (g o) -> p g o", o=1
                                                  )[:, 1:16:2, :],
                    in0=ssum[64:128, :].rearrange("p (g o) -> p g o", o=1
                                                  )[:, 1:16:2, :],
                    in1=pD[64:128, 32:40].rearrange("p (g o) -> p g o", o=1),
                    op=ALU.add), "v", ("st", j))
                if LVL < 4:
                    continue
                E.wait("pe", ("attnT", j))
                E.inc(v.tensor_copy(out=attnT[:], in_=pT[:]),
                      "v", ("attnT", j))
                E.wait("pe", ("wout", j))
                emit_ln_stats(E, v, stats, mv, pQ[:, 0:512], pQ[:, 512:1024],
                              ("oa", j))
                E.wait("s", ("orstd", j))
                E.inc(v.tensor_scalar(out=o_n[:], in0=pQ[:],
                                      scalar1=mv[:, 0:1], scalar2=rstd[:],
                                      op0=ALU.subtract, op1=ALU.mult),
                      "v", ("onorm", j))
                if LVL < 5:
                    continue
                v.tensor_tensor(out=a_g[:], in0=o_n[:], in1=gbc[:],
                                op=ALU.mult)
                E.inc(v.tensor_tensor(out=a_g[:], in0=a_g[:], in1=bbc[:],
                                      op=ALU.add), "v", ("gb", j))
                emit_ln_stats(E, v, stats, mv, a_g[:, 0:512], a_g[:, 512:1024],
                              ("aa", j))
                E.wait("s", ("arstd", j))
                E.inc(v.tensor_scalar(out=o_n[:], in0=a_g[:],
                                      scalar1=mv[:, 0:1], scalar2=rstd[:],
                                      op0=ALU.subtract, op1=ALU.mult),
                      "v", ("anorm", j))
                if j >= 2:
                    E.wait("w%d" % (j % 2), ("x2w", j - 2))
                elif fused:
                    E.wait("px%d" % (j % 2), ("px", NPRE - 2 + j))
                E.inc(v.tensor_tensor(out=x2sb[j % 2], in0=xt[j % 2],
                                      in1=o_n[:], op=ALU.add),
                      "v", ("x2", j))
            if upto != 'full':
                return
            for ha, t0, tn in halves():
                for tt in range(tn):
                    E.wait("x%d" % (tt % 2), ("x2a", ha, tt))
                    emit_ln_stats(E, v, stats, mv, x2r[tt % 2][:, 0:512],
                                  x2r[tt % 2][:, 512:1024], ("h2a", ha, tt))
                    E.wait("s", ("h2rstd", ha, tt))
                    E.inc(v.tensor_scalar(out=h1_tm[:], in0=x2r[tt % 2],
                                          scalar1=mv[:, 0:1], scalar2=rstd[:],
                                          op0=ALU.subtract, op1=ALU.mult),
                          "v", ("h2", ha, tt))
                    E.wait("pe", ("h2T", ha, tt))
                    E.inc(v.tensor_copy(
                        out=h2T.ap().rearrange("p (c i) -> p c i", c=KC
                                               )[:, :, tt * P:(tt + 1) * P],
                        in_=pT.ap().rearrange("p (c i) -> p c i", c=KC)),
                        "v", ("h2T", ha, tt))
                for sl in range(NSL):
                    kseq = ha * NSL + sl
                    for u in range(NU):
                        pgate, pups = GUP_BANKS[(kseq, u)]
                        E.wait("s", ("silu", kseq, u))
                        for hh in range(2):
                            if len(pups) == 1:
                                upsrc = pups[0][:, hh * 512:(hh + 1) * 512]
                            else:
                                upsrc = pups[hh]
                            ins = v.tensor_tensor(
                                out=hst.rearrange("p (c i) -> p c i", i=HLEN
                                                  )[:, 2 * sl + hh,
                                                    u * 512:(u + 1) * 512],
                                in0=silu_t[:, hh * 512:(hh + 1) * 512],
                                in1=upsrc, op=ALU.mult)
                        E.inc(ins, "v", ("hmul", kseq, u))
                for tt in range(tn):
                    pdn = pQ if tt % 2 == 0 else pK
                    E.wait("pe", ("down", ha, tt))
                    emit_ln_stats(E, v, stats, mv, pdn[:, 0:512],
                                  pdn[:, 512:1024], ("fa", ha, tt))
                    E.wait("s", ("frstd", ha, tt))
                    E.inc(v.tensor_scalar(out=f_n[:], in0=pdn[:],
                                          scalar1=mv[:, 0:1], scalar2=rstd[:],
                                          op0=ALU.subtract, op1=ALU.mult),
                          "v", ("fnorm", ha, tt))
                    E.wait("x%d" % (tt % 2), ("x2b", ha, tt))
                    if (ha, tt) == (0, 0):
                        E.wait("w1", ("x2w", T - 1))
                        E.wait("w0", ("x2w", T - 2))
                    if tt >= 2:
                        E.wait("o%d" % (tt % 2), ("out", ha, tt - 2))
                    elif ha == 1:
                        E.wait("o%d" % (tt % 2), ("out", 0, HT - 2 + tt))
                    E.inc(v.tensor_tensor(out=outsb[tt % 2],
                                          in0=x2r[tt % 2], in1=f_n[:],
                                          op=ALU.add), "v", ("res", ha, tt))

        def body_scalar(E):
            s = E.eng
            E.wait("v", "ceps")
            if upto == 'setup':
                return
            if fused:
                E.wait("v", ("pxa", 0))
                E.inc(emit_rstd(E, s, mvp[0], rstdp[0], c_eps[:]),
                      "s", ("prstd", 0))
                for pj in range(NPRE):
                    E.wait("pe", ("pxk", pj))
                    if pj > 0:
                        E.wait("v", ("pphik", pj - 1))
                    s.activation(out=t1[:], in_=pK[:], func=AF.Relu,
                                 scale=-1.0)
                    s.activation(out=t2[:], in_=t1[:], func=AF.Exp,
                                 scale=-1.0)
                    E.inc(s.activation(out=t3[:], in_=pK[:],
                                       func=AF.Relu), "s", ("pkel", pj))
                    if pj + 1 < NPRE:
                        E.wait("v", ("pxa", pj + 1))
                        E.inc(emit_rstd(E, s, mvp[(pj + 1) % 2],
                                        rstdp[(pj + 1) % 2], c_eps[:]),
                              "s", ("prstd", pj + 1))
                    E.wait("pe", ("pxv", pj))
                    if pj > 0:
                        E.wait("pe", ("pxst", pj - 1, 1))
                    E.inc(s.activation(out=v_tm[:], in_=pV[:],
                                       func=AF.Copy), "s", ("pv", pj))
                E._mark("s", "preend", E.counts.get("s", 0))
            for j in range(T):
                E.wait("v", ("xa", j))
                E.inc(emit_rstd(E, s, mvp[j % 2], rstdp[j % 2], c_eps[:]),
                      "s", ("xrstd", j))
                if LVL < 2:
                    continue
                E.wait("pe", ("q", j))
                if j > 0:
                    E.wait("v", ("phik", j - 1))
                elif fused:
                    E.wait("v", "preend")
                s.activation(out=t1[:], in_=pQ[:], func=AF.Relu, scale=-1.0)
                s.activation(out=t2[:], in_=t1[:], func=AF.Exp, scale=-1.0)
                E.inc(s.activation(out=t3[:], in_=pQ[:], func=AF.Relu),
                      "s", ("qel", j))
                E.wait("pe", ("k", j))
                E.wait("v", ("phiq", j))
                s.activation(out=t2[:], in_=pK[:], func=AF.Relu, scale=-1.0)
                s.activation(out=t1[:], in_=t2[:], func=AF.Exp, scale=-1.0)
                E.inc(s.activation(out=t3[:], in_=pK[:], func=AF.Relu),
                      "s", ("kel", j))
                E.wait("pe", ("v", j))
                E.inc(s.activation(out=v_tm[:], in_=pV[:], func=AF.Copy),
                      "s", ("v", j))
                if LVL < 3:
                    continue
                E.wait("v", ("rden", j))
                for h in range(8):
                    ins = s.activation(out=attn_bf[:, h * EH:(h + 1) * EH],
                                       in_=pK[:, h * EH:(h + 1) * EH],
                                       func=AF.Copy, scale=rden[:, h:h + 1])
                E.inc(ins, "s", ("div", j))
                if LVL < 4:
                    continue
                E.wait("v", ("oa", j))
                E.inc(emit_rstd(E, s, mv, rstd, c_eps[:]), "s", ("orstd", j))
                if LVL < 5:
                    continue
                E.wait("v", ("aa", j))
                E.inc(emit_rstd(E, s, mv, rstd, c_eps[:]), "s", ("arstd", j))
                E.wait("v", ("x2", j))
                E.inc(s.dma_start(out=x2_hbm[j * TOK:(j + 1) * TOK, :],
                                  in_=x2sb[j % 2]), "w%d" % (j % 2),
                      ("x2w", j), 16)
            if upto != 'full':
                return
            def emit_silu(kq, u):
                pgate, _pups = GUP_BANKS[(kq, u)]
                E.wait("pe", ("gup", kq, u))
                kk = KSEQ_OF[(kq, u)]
                if kk >= 1:
                    E.wait("v", ("hmul",) + UNIT_LIST[kk - 1])
                E.inc(s.activation(out=silu_t[:], in_=pgate, func=AF.Silu),
                      "s", ("silu", kq, u))

            for ha, t0, tn in halves():
                for tt in range(tn):
                    E.wait("v", ("h2a", ha, tt))
                    E.inc(emit_rstd(E, s, mv, rstd, c_eps[:]),
                          "s", ("h2rstd", ha, tt))
                for sl in range(NSL):
                    kseq = ha * NSL + sl
                    E.wait("ld", ("wg", kseq))
                    E.inc(s.activation(out=wg_sl[kseq % 2], in_=wgu_int,
                                       func=AF.Copy, scale=WSCALE),
                          "s", ("wgc", kseq))
                    E.wait("ld", ("wu", kseq))
                    E.inc(s.activation(out=wu_sl[kseq % 2], in_=wuu_int,
                                       func=AF.Copy, scale=WSCALE),
                          "s", ("wuc", kseq))
                    if ha == 0:
                        for c in (2 * sl, 2 * sl + 1):
                            E.wait("ld", ("wd", c))
                            E.inc(s.activation(out=wd_bf[c], in_=wd_int,
                                               func=AF.Copy, scale=WSCALE),
                                  "s", ("wdc", c))
                    if sl > 0:
                        for u in range(NU):
                            emit_silu(ha * NSL + sl - 1, u)
                for u in range(NU):
                    emit_silu(ha * NSL + NSL - 1, u)
                for tt in range(tn):
                    E.wait("v", ("fa", ha, tt))
                    E.inc(emit_rstd(E, s, mv, rstd, c_eps[:]),
                          "s", ("frstd", ha, tt))
                    E.wait("v", ("res", ha, tt))
                    E.inc(s.dma_start(
                        out=out[(t0 + tt) * TOK:(t0 + tt + 1) * TOK, :],
                        in_=outsb[tt % 2]), "o%d" % (tt % 2),
                      ("out", ha, tt), 16)

        def body_tensor(E):
            t = E.eng
            E.wait("c", "call")
            for n4 in range(2):
                t.matmul(pQ[:, n4 * 512:(n4 + 1) * 512], onesf[:],
                         t2[0:1, n4 * 512:(n4 + 1) * 512],
                         start=True, stop=True)
            for n4 in range(2):
                ins = t.matmul(pK[:, n4 * 512:(n4 + 1) * 512], onesf[:],
                               t3[0:1, n4 * 512:(n4 + 1) * 512],
                               start=True, stop=True)
            E.inc(ins, "g", "gbcmm")
            if upto == 'setup':
                return
            if fused:
                for pj in range(NPRE):
                    E.wait("v", ("ph1", pj))
                    if pj == 0:
                        E.wait("c", "call")
                    else:
                        E.wait("v", ("ph1T", pj - 1))
                    for c in range(KC):
                        ins = t.transpose(pT[:, c * P:(c + 1) * P],
                                          h1_tm[:, c * P:(c + 1) * P],
                                          ident[:])
                    E.inc(ins, "pe", ("pxh1T", pj))
                    for half, pdst, name in ((0, pK, "pxk"), (1, pV, "pxv")):
                        E.wait("v", ("ph1T", pj))
                        if pj == 0:
                            E.wait("v", ("wq", KC - 1))
                            E.wait("v", "gbcc")
                        else:
                            if half == 0:
                                E.wait("s", ("pkel", pj - 1))
                            else:
                                E.wait("s", ("pv", pj - 1))
                        for n4 in range(2):
                            for c in range(KC):
                                t.matmul(
                                    pdst[:, n4 * 512:(n4 + 1) * 512],
                                    h1T[:, c * P:(c + 1) * P],
                                    wq_bf[:, c * 3 * D + D + half * D +
                                          n4 * 512:
                                          c * 3 * D + D + half * D +
                                          (n4 + 1) * 512],
                                    start=(c == 0), stop=False)
                            ins = t.matmul(
                                pdst[:, n4 * 512:(n4 + 1) * 512],
                                onesrow[:],
                                brow[0:1, D + half * D + n4 * 512:
                                     D + half * D + (n4 + 1) * 512],
                                start=False, stop=True)
                        E.inc(ins, "pe", (name, pj))
                    for r in range(2):
                        E.wait("v", ("pphik", pj))
                        E.wait("s", ("pv", pj))
                        if pj > 0:
                            E.wait("v", ("psadd", pj - 1, r))
                        for pp in range(4):
                            ins = t.matmul(
                                pQ[:, 512 * r + pp * P:
                                   512 * r + (pp + 1) * P],
                                phik_tm[:, (r * 4 + pp) * P:
                                        (r * 4 + pp + 1) * P],
                                v_tm[:, (r * 4 + pp) * P:
                                     (r * 4 + pp + 1) * P],
                                start=True, stop=True)
                        E.inc(ins, "pe", ("pxst", pj, r))
                    if pj > 0:
                        E.wait("v", ("pst", pj - 1))
                    for pr in range(NPAIR):
                        ins = t.matmul(pD[:, 32 + pr:33 + pr],
                                       phik_tm[:, pr * P:(pr + 1) * P],
                                       onescol[:], start=True, stop=True)
                    E.inc(ins, "pe", ("pxscol", pj))
                E._mark("pe", "preend", E.counts.get("pe", 0))
            for j in range(T):
                E.wait("v", ("h1", j))
                if j > 0:
                    E.wait("v", ("attnT", j - 1) if LVL >= 4 else
                           (("kfm", j - 1) if LVL >= 2 else ("h1T", j - 1)))
                elif fused:
                    E.wait("v", "preend")
                for c in range(KC):
                    ins = t.transpose(pT[:, c * P:(c + 1) * P],
                                      h1_tm[:, c * P:(c + 1) * P], ident[:])
                E.inc(ins, "pe", ("h1T", j))
                if LVL < 2:
                    continue
                E.wait("v", ("h1T", j))
                if j == 0:
                    E.wait("v", ("wq", KC - 1))
                    E.wait("v", "gbcc")
                elif LVL >= 4:
                    E.wait("v", ("onorm", j - 1))
                    E.wait("v", ("oa", j - 1))
                else:
                    E.wait("s", ("qel", j - 1))
                for ec in range(KC):
                    for c in range(KC):
                        t.matmul(pQ[:, ec * P:(ec + 1) * P],
                                 wq_bf[:, c * 3 * D + ec * P:
                                       c * 3 * D + (ec + 1) * P],
                                 h1T[:, c * P:(c + 1) * P],
                                 start=(c == 0), stop=False)
                    ins = t.matmul(pQ[:, ec * P:(ec + 1) * P],
                                   brow[0:1, ec * P:(ec + 1) * P],
                                   onesrow[:], start=False, stop=True)
                E.inc(ins, "pe", ("q", j))
                for half, pdst, name in ((0, pK, "k"), (1, pV, "v")):
                    if j == 0 and fused:
                        if half == 0:
                            E.wait("s", "preend")
                        else:
                            E.wait("s", "preend")
                            E.wait("v", "preend")
                    if j > 0:
                        if half == 0:
                            if LVL >= 3:
                                E.wait("s", ("div", j - 1))
                                E.wait("v", ("div", j - 1))
                            else:
                                E.wait("s", ("kel", j - 1))
                        else:
                            E.wait("s", ("v", j - 1))
                            if LVL >= 3:
                                E.wait("v", ("sadd", j - 1, 1))
                                E.wait("v", ("mask", j - 1, 3))
                    for n4 in range(2):
                        for c in range(KC):
                            t.matmul(pdst[:, n4 * 512:(n4 + 1) * 512],
                                     h1T[:, c * P:(c + 1) * P],
                                     wq_bf[:, c * 3 * D + D + half * D + n4 * 512:
                                           c * 3 * D + D + half * D +
                                           (n4 + 1) * 512],
                                     start=(c == 0), stop=False)
                        ins = t.matmul(pdst[:, n4 * 512:(n4 + 1) * 512],
                                       onesrow[:],
                                       brow[0:1, D + half * D + n4 * 512:
                                            D + half * D + (n4 + 1) * 512],
                                       start=False, stop=True)
                    E.inc(ins, "pe", (name, j))
                E.wait("v", ("phik", j))
                E.wait("v", ("h1T", j))
                for c in range(KC):
                    ins = t.transpose(pT[:, c * P:(c + 1) * P],
                                      phik_tm[:, c * P:(c + 1) * P], ident[:])
                E.inc(ins, "pe", ("kfm", j))
                if LVL < 3:
                    continue
                E.wait("v", ("kfm", j))
                E.wait("v", ("phiq", j))
                E.wait("s", ("v", j))
                E.wait("v", ("scast", j))
                if j > 0:
                    E.wait("v", ("st", j - 1))
                    E.wait("v", ("rden", j - 1))
                elif fused:
                    E.wait("v", "preend")
                E.wait("v", ("phiqz", j))
                for g in range(4):
                    if g > 0:
                        E.wait("v", ("mask", j, g - 1))
                    for hh in range(4):
                        h = g * 4 + hh
                        pr, hlf = h // 2, h % 2
                        ins = t.matmul(
                            pV[:, hh * P:(hh + 1) * P],
                            phik_fm[:, pr * P:(pr + 1) * P],
                            (phiq_e if hlf == 0 else phiq_o)[:, pr * P:(pr + 1) * P],
                            start=True, stop=True)
                    E.inc(ins, "pe", ("at", j, g))
                    E.wait("v", ("mask", j, g))
                    for pp in range(2):
                        pr = 2 * g + pp
                        t.matmul(pD[:, 2 * pr:2 * pr + 2],
                                 phiq[:, pr * P:(pr + 1) * P],
                                 ssum_bf[:, 2 * pr:2 * pr + 2],
                                 start=True, stop=False)
                        t.matmul(pD[:, 2 * pr:2 * pr + 1],
                                 am[:, 2 * pp * P:(2 * pp + 1) * P],
                                 onescol[:], start=False, stop=False)
                        ins = t.matmul(pD[:, 2 * pr + 1:2 * pr + 2],
                                       am[:, (2 * pp + 1) * P:(2 * pp + 2) * P],
                                       onescol[:], start=False, stop=True)
                        t.matmul(pK[:, pr * P:(pr + 1) * P],
                                 phiq[:, pr * P:(pr + 1) * P],
                                 srun_bf[:, pr * P:(pr + 1) * P],
                                 start=True, stop=False)
                        for sub in range(2):
                            h = 2 * pr + sub
                            ins = t.matmul(pK[:, h * EH:(h + 1) * EH],
                                           am[:, (2 * pp + sub) * P:
                                              (2 * pp + sub + 1) * P],
                                           v_tm[:, h * EH:(h + 1) * EH],
                                           start=False, stop=(sub == 1))
                    E.inc(ins, "pe", ("avg", j, g))
                for r in range(2):
                    if r == 0:
                        E.wait("v", ("mask", j, 3))
                        if j > 0:
                            E.wait("v", ("sadd", j - 1, 1))
                        elif fused:
                            E.wait("v", "preend")
                    else:
                        E.wait("v", ("sadd", j, 0))
                    for pp in range(4):
                        pr = r * 4 + pp
                        ins = t.matmul(pV[:, 512 + pp * P:512 + (pp + 1) * P],
                                       phik_tm[:, pr * P:(pr + 1) * P],
                                       v_tm[:, pr * P:(pr + 1) * P],
                                       start=True, stop=True)
                    E.inc(ins, "pe", ("st", j, r))
                E.wait("v", ("rden", j))
                if j > 0:
                    E.wait("v", ("st", j - 1))
                elif fused:
                    E.wait("v", "preend")
                for pr in range(NPAIR):
                    ins = t.matmul(pD[:, 32 + pr:33 + pr],
                                   phik_tm[:, pr * P:(pr + 1) * P],
                                   onescol[:], start=True, stop=True)
                E.inc(ins, "pe", ("scol", j))
                if LVL < 4:
                    continue
                E.wait("s", ("div", j))
                E.wait("v", ("div", j))
                E.wait("v", ("kfm", j))
                for c in range(KC):
                    ins = t.transpose(pT[:, c * P:(c + 1) * P],
                                      attn_bf[:, c * P:(c + 1) * P], ident[:])
                E.inc(ins, "pe", ("attnT", j))
                E.wait("v", ("attnT", j))
                if j == 0:
                    E.wait("v", ("wo", KC - 1))
                E.wait("s", ("qel", j))
                for n4 in range(2):
                    for c in range(KC):
                        t.matmul(pQ[:, n4 * 512:(n4 + 1) * 512],
                                 attnT[:, c * P:(c + 1) * P],
                                 wo_bf[:, c * D + n4 * 512:
                                       c * D + (n4 + 1) * 512],
                                 start=(c == 0), stop=False)
                    ins = t.matmul(pQ[:, n4 * 512:(n4 + 1) * 512],
                                   onesrow[:],
                                   borow[0:1, n4 * 512:(n4 + 1) * 512],
                                   start=False, stop=True)
                E.inc(ins, "pe", ("wout", j))
            if upto != 'full':
                return
            for ha, t0, tn in halves():
                for tt in range(tn):
                    E.wait("v", ("h2", ha, tt))
                    if (ha, tt) == (0, 0):
                        E.wait("v", ("attnT", T - 1))
                    elif ha == 1 and tt == 0:
                        E.wait("v", ("hmul",) + H0_LAST_ODD)
                        E.wait("s", ("silu",) + H0_LAST_ODD)
                    else:
                        E.wait("v", ("h2T", ha, tt - 1))
                    for c in range(KC):
                        ins = t.transpose(pT[:, c * P:(c + 1) * P],
                                          h1_tm[:, c * P:(c + 1) * P],
                                          ident[:])
                    E.inc(ins, "pe", ("h2T", ha, tt))
                for sl in range(NSL):
                    kseq = ha * NSL + sl
                    E.wait("s", ("wgc", kseq))
                    E.wait("s", ("wuc", kseq))
                    for u in range(NU):
                        pgate, pups = GUP_BANKS[(kseq, u)]
                        for tc in range(u * 4, min(u * 4 + 4, tn)):
                            E.wait("v", ("h2T", ha, tc))
                        kk = KSEQ_OF[(kseq, u)]
                        if kk >= 2:
                            pku = UNIT_LIST[kk - 2]
                            E.wait("v", ("hmul",) + pku)
                            E.wait("s", ("silu",) + pku)
                        if ha == 1 and kseq == NSL and u < 2:
                            E.wait("v", ("fnorm", 0, HT - 2 + u))
                        h2v = h2T.ap().rearrange("p (c i) -> p c i", c=KC)
                        for hh in range(2):
                            dst = pgate[:, hh * 512:(hh + 1) * 512]
                            for c in range(KC):
                                t.matmul(dst,
                                         wg_sl[kseq % 2][:, c * SL + hh * P:
                                                         c * SL + (hh + 1) * P],
                                         h2v[:, c, u * 512:(u + 1) * 512],
                                         start=(c == 0), stop=(c == KC - 1))
                        for hh in range(2):
                            if len(pups) == 1:
                                udst = pups[0][:, hh * 512:(hh + 1) * 512]
                            else:
                                udst = pups[hh]
                            for c in range(KC):
                                ins = t.matmul(
                                    udst,
                                    wu_sl[kseq % 2][:, c * SL + hh * P:
                                                    c * SL + (hh + 1) * P],
                                    h2v[:, c, u * 512:(u + 1) * 512],
                                    start=(c == 0), stop=(c == KC - 1))
                        E.inc(ins, "pe", ("gup", kseq, u))
                for tt in range(tn):
                    pdn = pQ if tt % 2 == 0 else pK
                    lastk = ha * NSL + NSL - 1
                    E.wait("v", ("hmul", lastk, tt // 4))
                    E.wait("s", ("silu", lastk, tt // 4))
                    if tt == 0:
                        E.wait("s", ("wdc", HC - 1))
                    if tt >= 2:
                        E.wait("v", ("fnorm", ha, tt - 2))
                    hstv = hst.rearrange("p (c i) -> p c i", i=HLEN)
                    for n4 in range(2):
                        for c in range(HC):
                            ins = t.matmul(
                                pdn[:, n4 * 512:(n4 + 1) * 512],
                                hstv[:, c, tt * P:(tt + 1) * P],
                                wd_bf[c][:, n4 * 512:(n4 + 1) * 512],
                                start=(c == 0), stop=(c == HC - 1))
                    E.inc(ins, "pe", ("down", ha, tt))

        run_bodies([
            (block.sync, body_sync, None),
            (block.vector, body_vector, "v"),
            (block.scalar, body_scalar, "s"),
            (block.tensor, body_tensor, None),
        ], sems)
    return nc




def _consts():
    maskT = np.triu(np.ones((P, P), dtype=np.float32))
    return {
        "maskT4": np.ascontiguousarray(np.tile(maskT, (1, 4))),
        "ident": np.eye(P, dtype=BF),
        "onescol": np.ones((P, 1), dtype=BF),
        "onesrow": np.ones((1, P), dtype=BF),
        "onesf": np.ones((1, P), dtype=np.float32),
    }


def kernel(L0_attn_x, L0_attn_Wqkv, L0_attn_bqkv, L0_attn_Wout, L0_attn_bout,
           L0_attn_gamma, L0_attn_beta, L0_ffn_w_gate, L0_ffn_w_up,
           L0_ffn_w_down):
    x = np.ascontiguousarray(np.asarray(L0_attn_x, dtype=np.float32)
                             ).reshape(2 * SEQ_LEN, D)
    wqkvT = np.ascontiguousarray(np.asarray(L0_attn_Wqkv, dtype=np.float32).T)
    woutT = np.ascontiguousarray(np.asarray(L0_attn_Wout, dtype=np.float32).T)
    bqkv = np.ascontiguousarray(
        np.asarray(L0_attn_bqkv, dtype=np.float32).reshape(1, 3 * D))
    bout = np.ascontiguousarray(
        np.asarray(L0_attn_bout, dtype=np.float32).reshape(1, D))
    gamma = np.ascontiguousarray(
        np.asarray(L0_attn_gamma, dtype=np.float32).reshape(1, D))
    beta = np.ascontiguousarray(
        np.asarray(L0_attn_beta, dtype=np.float32).reshape(1, D))
    wgT = np.ascontiguousarray(np.asarray(L0_ffn_w_gate, dtype=np.int32).T)
    wuT = np.ascontiguousarray(np.asarray(L0_ffn_w_up, dtype=np.int32).T)
    wdT = np.ascontiguousarray(np.asarray(L0_ffn_w_down, dtype=np.int32).T)
    consts = _consts()

    NT = T_MAIN * TOK
    xs = [np.ascontiguousarray(x[c * NT:(c + 1) * NT]) for c in range(N_CORES)]

    nc_b = build_main(T_MAIN, fused=True)
    in_b = []
    for c in range(N_CORES):
        seq = c // 4
        xseq = x[seq * SEQ_LEN:(seq + 1) * SEQ_LEN]
        npre = (c % 4) * T_MAIN
        pf = np.zeros((P, 3 * T_MAIN), dtype=np.float32)
        pf[:, :npre] = 1.0
        in_b.append({"x_main": xs[c], "x_pre": np.ascontiguousarray(xseq[0:3 * NT]),
                     "pflag": pf,
                     "wqkvT": wqkvT, "woutT": woutT, "bqkv": bqkv,
                     "bout": bout, "gamma": gamma, "beta": beta, "wgT": wgT,
                     "wuT": wuT, "wdT": wdT, **consts})
    res_b = run_bass_kernel_spmd(nc_b, in_b, core_ids=list(range(N_CORES)))
    outs = [np.asarray(res_b.results[c]["out"]) for c in range(N_CORES)]
    return np.concatenate(outs, axis=0).reshape(2, SEQ_LEN, D)
